# version 09
# baseline (speedup 1.0000x reference)
import contextlib
import numpy as np

import concourse.bass as bass
import concourse.mybir as mybir
from concourse.bass_utils import run_bass_kernel_spmd
from concourse.alu_op_type import AluOpType as ALU

F32 = mybir.dt.float32
BF16 = mybir.dt.bfloat16
AF = mybir.ActivationFunctionType

D = 1024
L = 2048
DT = 8
NCH = 4
NEG = -1.0e30

_QUEUES = ("pe", "act", "dve", "pool", "sp")
_BLOCK_ATTR = {"pe": "tensor", "act": "scalar", "dve": "vector",
               "pool": "gpsimd", "sp": "sync"}


class Op:
    __slots__ = ("qname", "fn", "deps", "seq", "needed", "sem", "val", "incn",
                 "tag", "_dmawait")

    def __init__(self, qname, fn, deps, seq, sem, incn, tag):
        self.qname, self.fn, self.deps, self.seq = qname, fn, deps, seq
        self.needed, self.sem, self.val, self.incn = False, sem, None, incn
        self.tag = tag


class Prog:
    def __init__(self, nc, ctx):
        self.nc, self.ctx = nc, ctx
        self.ops = []
        self.stream = {q: [] for q in _QUEUES}
        self.sems = {}

    def add(self, qname, fn, deps=(), sem=None, incn=1, tag=""):
        deps = [d for d in deps if d is not None]
        op = Op(qname, fn, deps, len(self.ops), sem or qname, incn, tag)
        self.ops.append(op)
        self.stream[qname].append(op)
        return op

    def dma(self, qname, dst, src, deps=(), sem=None, tag="dma", slow=False):
        assert qname in ("pool", "sp")
        return self.add(
            qname,
            lambda q: q.dma_start(dst, src, allow_slow_non_contiguous=slow),
            deps, sem=sem or ("D" + qname), incn=16, tag=tag)

    def tails(self, *qnames):
        if not qnames:
            qnames = _QUEUES
        return [self.stream[q][-1] for q in qnames if self.stream[q]]

    def finalize(self, block):
        semq = {}
        for op in self.ops:
            qq = semq.setdefault(op.sem, op.qname)
            assert qq == op.qname, ("sem on two queues", op.sem, qq, op.qname)
        for op in self.ops:
            for d in op.deps:
                if d.qname != op.qname or d.incn == 16:
                    d.needed = True
            if op.incn == 16:
                op.needed = True
        semval = {}
        for q in _QUEUES:
            for op in self.stream[q]:
                if op.needed:
                    semval[op.sem] = semval.get(op.sem, 0) + op.incn
                    op.val = semval[op.sem]
        issued = {}
        for op in self.ops:
            snap = None
            for dep in op.deps:
                if dep.incn == 16:
                    if snap is None:
                        snap = dict(issued)
                    op_snap = getattr(op, "_dmawait", None)
                    if op_snap is None:
                        op_snap = {}
                        op._dmawait = op_snap
                    op_snap[dep.sem] = max(op_snap.get(dep.sem, 0),
                                           snap.get(dep.sem, dep.val),
                                           dep.val)
            if op.incn == 16:
                issued[op.sem] = issued.get(op.sem, 0) + 16
        for s in semval:
            if s not in self.sems:
                self.sems[s] = self.ctx.enter_context(
                    self.nc.semaphore("S_" + s))
        for q in _QUEUES:
            ops = self.stream[q]
            if not ops:
                continue

            def emit(eng, ops=ops):
                hw = {}
                for op in ops:
                    for dep in op.deps:
                        if dep.qname == op.qname and dep.incn != 16:
                            continue
                        assert dep.val is not None, (dep.tag, "->", op.tag)
                        v = dep.val
                        if dep.incn == 16:
                            v = op._dmawait[dep.sem]
                        if hw.get(dep.sem, 0) < v:
                            eng.wait_ge(self.sems[dep.sem], v)
                            hw[dep.sem] = v
                    instr = op.fn(eng)
                    if op.needed:
                        assert instr is not None, (op.tag, op.qname)
                        instr.then_inc(self.sems[op.sem], op.incn)
            getattr(block, _BLOCK_ATTR[q])(emit)



def _col(v):
    v = np.asarray(v, np.float32).reshape(-1)
    return np.ascontiguousarray(v.reshape(-1, 128).T)


class BiasPack:
    def __init__(self):
        self.cols, self.off, self.w = [], {}, 0

    def add(self, name, arr):
        arr = np.asarray(arr, np.float32)
        assert arr.shape[0] == 128, (name, arr.shape)
        self.off[name] = self.w
        self.w += arr.shape[1]
        self.cols.append(arr)

    def pack(self):
        return np.ascontiguousarray(np.concatenate(self.cols, axis=1))


ROWPACK = ["l2_bv", "l2_bfq", "l2_bfk"]


def prep_weights(inp):
    w = {}
    bp = BiasPack()
    t = lambda a: np.ascontiguousarray(np.asarray(a, np.float32).T)

    w["l1a_wa_t"] = np.ascontiguousarray(
        np.asarray(inp["L1_attn_Wa_w"], np.float32).transpose(0, 2, 1))
    w["l1a_wb_t"] = np.ascontiguousarray(
        np.asarray(inp["L1_attn_Wb_w"], np.float32).transpose(0, 2, 1))
    w["l1a_c_t"] = np.ascontiguousarray(
        np.asarray(inp["L1_attn_C_w"], np.float32).transpose(0, 2, 1))
    w["l1a_d_t"] = t(inp["L1_attn_D_w"])
    w["l1a_wout_t"] = t(inp["L1_attn_Wout_w"])
    w["l1a_ffn1_t"] = t(inp["L1_attn_ffn1_w"])
    w["l1a_ffn2_t"] = t(inp["L1_attn_ffn2_w"])
    w["l1a_proj_t"] = t(inp["L1_attn_proj_w"])
    bp.add("l1a_wab", _col(np.asarray(inp["L1_attn_Wa_b"]).reshape(-1)))
    bp.add("l1a_wbb", _col(np.asarray(inp["L1_attn_Wb_b"]).reshape(-1)))
    bp.add("l1a_cb", _col(np.asarray(inp["L1_attn_C_b"]).reshape(-1)))
    bp.add("l1a_db", _col(inp["L1_attn_D_b"]))
    bp.add("l1a_woutb", _col(inp["L1_attn_Wout_b"]))
    bp.add("l1a_hng", _col(np.asarray(inp["L1_attn_hn_g"]).reshape(-1)))
    bp.add("l1a_hnb", _col(np.asarray(inp["L1_attn_hn_b"]).reshape(-1)))
    bp.add("l1a_headw", np.ascontiguousarray(np.broadcast_to(
        np.asarray(inp["L1_attn_head_w"], np.float32)[None, :], (128, 8))))
    bp.add("l1a_f1b", _col(inp["L1_attn_ffn1_b"]))
    bp.add("l1a_f2b", _col(inp["L1_attn_ffn2_b"]))
    bp.add("l1a_pb", _col(inp["L1_attn_proj_b"]))
    bp.add("l1a_ng", _col(inp["L1_attn_norm_g"]))
    bp.add("l1a_nb", _col(inp["L1_attn_norm_b"]))
    bp.add("l1a_scr", np.zeros((128, 32), np.float32))

    for nm in ["wq", "wk", "wv", "wo", "e0_up", "e0_down", "e1_w1", "e1_w2",
               "e1_down", "e2_l1", "e2_l2", "e2_l3", "e2_l4", "e3_down",
               "e3_up", "e3_out"]:
        w["l1f_" + nm] = t(inp["L1_ffn_" + nm])
    w["l1f_gate_kt"] = np.ascontiguousarray(
        np.asarray(inp["L1_ffn_gate_w"], np.float32).T.reshape(
            8, 128, 4).transpose(1, 0, 2).reshape(128, 32))
    bp.add("l1f_n1", _col(inp["L1_ffn_n1_w"]))
    bp.add("l1f_n2", _col(inp["L1_ffn_n2_w"]))
    qn = np.asarray(inp["L1_ffn_qn_w"], np.float32)
    kn = np.asarray(inp["L1_ffn_kn_w"], np.float32)
    bp.add("l1f_qn", np.concatenate([qn, qn]).reshape(-1, 1)[0:128].reshape(128, 1) if False else np.ascontiguousarray(np.concatenate([qn, qn]).reshape(128, 1)))
    bp.add("l1f_kn", np.ascontiguousarray(np.concatenate([kn, kn]).reshape(128, 1)))
    bp.add("l1f_scr", np.zeros((128, 96), np.float32))

    for nm in ["Wq", "Wk", "Wv", "Wfq", "Wfk"]:
        w["l2_" + nm] = t(inp["L2_attn_" + nm])
    w["l2_w"] = t(inp["L2_ffn_weight"])
    bp.add("l2_bq", _col(inp["L2_attn_bq"]))
    bp.add("l2_bk", _col(inp["L2_attn_bk"]))
    bp.add("l2_alpha", np.broadcast_to(
        np.asarray(inp["L2_ffn_alpha"], np.float32).reshape(1, 1),
        (128, 1)).copy())
    bp.add("l2_bias", _col(inp["L2_ffn_bias"]))
    bp.add("scratch", np.zeros((128, 32), np.float32))
    rows = {"l2_bv": inp["L2_attn_bv"], "l2_bfq": inp["L2_attn_bfq"],
            "l2_bfk": inp["L2_attn_bfk"]}
    w["rowpack"] = np.ascontiguousarray(np.stack(
        [np.asarray(rows[k], np.float32).reshape(D) for k in ROWPACK], 0))
    w["l3_qkv_t"] = t(inp["L3_attn_qkv_weight"])
    w["l3_qkvm_t"] = t(inp["L3_attn_qkv_weight_mask"])
    w["l3_out_t"] = t(inp["L3_attn_out_weight"])
    w["l3_outm_t"] = t(inp["L3_attn_out_weight_mask"])
    bp.add("l3_ob", _col(inp["L3_attn_out_bias"]))
    bp.add("l3_obm", _col(inp["L3_attn_out_bias_mask"]))

    w["l3f_wexp_t"] = t(inp["L3_ffn_W_exp"])
    w["l3f_wcon"] = np.ascontiguousarray(
        np.asarray(inp["L3_ffn_W_con"], np.float32))
    bp.add("l3f_bexp", _col(inp["L3_ffn_b_exp"]))
    bp.add("l3f_upb", _col(inp["L3_ffn_up_bias"]))
    bp.add("l3f_bcon", _col(inp["L3_ffn_b_con"]))
    bp.add("l3f_dnb", _col(inp["L3_ffn_down_bias"]))
    bp.add("l3f_scr", np.zeros((128, 80), np.float32))
    masks = []
    for s in range(7):
        m = np.zeros((128, 128), np.float32)
        idx = np.arange(128)
        m[idx ^ (1 << s), idx] = 1.0
        masks.append(m)
    masks.append(np.eye(128, dtype=np.float32))
    w["bmasks"] = np.ascontiguousarray(np.stack(masks, 0))

    def bfly_cols(wts, down):
        wts = np.asarray(wts, np.float32)
        i = np.arange(4096)
        out = np.zeros((128, 12 * 64), np.float32)
        for s in range(12):
            s_ = 1 << s
            k = (i // (2 * s_)) * s_ + (i % s_)
            bit = (i >> s) & 1
            alpha = wts[s, k, bit, bit]
            if not down:
                beta = wts[s, k, 1 - bit, bit] if s < 7 else                     wts[s, k, bit, 1 - bit]
            else:
                beta = wts[s, k, bit, 1 - bit] if s < 7 else                     wts[s, k, 1 - bit, bit]
            for ft in range(32):
                out[:, s * 64 + ft] = alpha[ft * 128:(ft + 1) * 128]
                out[:, s * 64 + 32 + ft] = beta[ft * 128:(ft + 1) * 128]
        return out
    w["bfcols"] = np.ascontiguousarray(np.concatenate(
        [bfly_cols(inp["L3_ffn_up_weights"], False),
         bfly_cols(inp["L3_ffn_down_weights"], True)], axis=1))

    w["l0_wq_t"] = t(inp["L0_attn_w_q"])
    w["l0_wkvd_t"] = t(inp["L0_attn_w_kv_down"])
    w["l0_wkvu_t"] = t(inp["L0_attn_w_kv_up"])
    w["l0_wout_t"] = t(inp["L0_attn_w_out"])
    w["l0_scorer_kt"] = np.ascontiguousarray(
        np.asarray(inp["L0_attn_w_scorer"], np.float32).reshape(8, 128).T)
    w["l0_wup_t"] = t(inp["L0_ffn_w_up"])
    w["l0_wdn_t"] = t(inp["L0_ffn_w_down"])
    bp.add("l0_scr", np.zeros((128, 32), np.float32))
    pos = np.arange(L, dtype=np.float64)
    invf = 1.0 / (100000.0 ** (np.arange(0, 64, 2, dtype=np.float64) / 64.0))
    fr = pos[:, None] * invf[None, :]
    cosт = np.cos(fr).T.astype(np.float32)
    sinт = np.sin(fr).T.astype(np.float32)
    cos64 = np.concatenate([cosт, cosт], 0)
    sin64p = np.concatenate([-sinт, sinт], 0)
    w["rope_cos"] = np.ascontiguousarray(np.concatenate([cos64, cos64], 0))
    w["rope_sin"] = np.ascontiguousarray(np.concatenate([sin64p, sin64p], 0))
    tril = np.zeros((128, 4 * 512), np.float32)
    for o in range(4):
        pp = np.arange(128)[:, None]
        jj = np.arange(512)[None, :]
        tril[:, o * 512:(o + 1) * 512] = np.where(o * 128 + pp <= jj, 0.0,
                                                  NEG)
    w["tril4"] = tril
    A32 = np.zeros((32, 128), np.float32)
    for j in range(32):
        A32[j, :] = ((np.arange(128) // 64) == (j % 2)).astype(np.float32)
    D32 = np.zeros((32, 16), np.float32)
    for j in range(32):
        D32[j, j // 2] = 1.0
    LT32 = np.tril(np.ones((32, 32), np.float32), -1)
    w["l0sel"] = np.ascontiguousarray(
        np.concatenate([A32, D32, LT32], axis=1))

    o64 = np.zeros((128, 128), np.float32)
    o64[:64, :64] = 1.0
    o64[64:, 64:] = 1.0
    w["ones64"] = o64
    w["biasbuf"] = bp.pack()
    w["ident"] = np.eye(128, dtype=np.float32)
    w["ones"] = np.ones((128, 128), np.float32)
    return w, bp.off



XR_C = 0
R0_C = 32768
R1_C = 49152
R2_C = 65536
SM_C = 81920
WS_C = 88064
ARENA = 96256
WSLOT = 4096


class MK:
    def __init__(self, nc, ctx, p, bias_off, bias_w=None):
        self.nc, self.p, self.bias_off = nc, p, bias_off
        sb = lambda n_, s, dt: ctx.enter_context(nc.sbuf_tensor(n_, s, dt))
        self.ident = sb("identb", [128, 128], BF16)
        self.ones_f = sb("ones_f", [128, 128], F32)
        self.ones_bf = sb("ones_bf", [128, 128], BF16)
        self.ones_row_f = sb("ones_row_f", [1, 128], F32)
        self.rows = sb("rows_f", [1, 512], F32)
        bw = bias_w or ((max(bias_off.values()) + 40) if bias_off else 8)
        self.bias = sb("biasbuf_s", [128, bw], F32)
        self.brow = sb("brow_s", [128, len(ROWPACK) * D], BF16)
        self.gapbuf = sb("gapbuf", [128, 64], F32)
        self.bmasks = sb("bmasks_s", [128, 8 * 128], BF16)
        self.ident_f = sb("ident_f", [128, 128], F32)
        self.bfc = sb("bfc_s", [128, 64], F32)
        self.sel = sb("sel_s", [32, 176], F32)
        self.selbf = sb("selbf_s", [32, 16], BF16)
        self.selA = sb("selA_s", [32, 128], BF16)
        self.rows2 = sb("rows2_f", [1, 512], F32)
        self.ones64_f = sb("ones64_f", [128, 128], F32)
        self.wtsbf = sb("wtsbf", [128, 64], BF16)
        self.wtt = sb("wtt", [64, 128], BF16)
        self.scr = {}
        self.scr_ready = {}
        self.l3_built = False
        self.tern_built = False
        self.bank_last = {}
        self.big = sb("big", [128, ARENA], BF16)
        self.bigf = self.big[:].bitcast(F32)
        self.ps = [ctx.enter_context(nc.psum_tensor(f"ps{i}", [128, 512], F32))
                   for i in range(8)]
        self.ps_bf = [b[:].bitcast(BF16) for b in self.ps]
        self.ws = None

    def bf(self, c0, ncols, rows=128, r0=0):
        assert c0 + ncols <= ARENA, (c0, ncols)
        return self.big[r0:r0 + rows, c0:c0 + ncols]

    def f32(self, c0, nf32, rows=128, r0=0):
        assert c0 % 2 == 0 and c0 + 2 * nf32 <= ARENA, (c0, nf32)
        return self.bigf[r0:r0 + rows, c0 // 2: c0 // 2 + nf32]

    def dve_gap(self, q):
        q.tensor_scalar(self.gapbuf[:], self.ones_f[:, 0:64], 1.0, None,
                        op0=ALU.mult)

    def bcol(self, name, i=0, n=1):
        o = self.bias_off[name]
        return self.bias[:, o + i: o + i + n]

    def browr(self, name, c0=0, n=D):
        i = ROWPACK.index(name)
        return self.brow[:, i * D + c0: i * D + c0 + n]


def xr_tiles(mk):
    return [mk.f32(XR_C + dt * 4096, 2048) for dt in range(DT)]


def reg_tiles(mk, c0):
    return [mk.bf(c0 + dt * 2048, 2048) for dt in range(DT)]



class WStream:

    def __init__(self, mk):
        self.mk = mk
        self.free_after = [None, None, None, None]
        self.turn1 = 0
        self.turn2 = 0

    def _alloc(self, ncols_bf16):
        if ncols_bf16 <= 2048:
            s = self.turn1
            self.turn1 = (self.turn1 + 1) % 4
            return [s]
        assert ncols_bf16 <= 4096
        s = 2 * self.turn2
        self.turn2 ^= 1
        return [s, s + 1]

    def load(self, wt_ap, m0, mcols, K, deps=(), cast=True, src_bf16=False):
        mk = self.mk
        kt = K // 128
        ncols = kt * mcols * (1 if (cast or src_bf16) else 2)
        subs = self._alloc(ncols)
        c0 = WS_C + subs[0] * 2048
        sl = wt_ap[:, m0:m0 + mcols].rearrange("(kt p) m -> p kt m", p=128)
        dd = list(deps) + [self.free_after[i] for i in subs]
        if src_bf16:
            dst = mk.bf(c0, kt * mcols).rearrange("p (kt m) -> p kt m", kt=kt)
            op = mk.p.dma("sp", dst, sl, deps=dd, sem="WS%d" % subs[0],
                          tag="wloadb")
            slab = mk.bf(c0, kt * mcols)
        elif cast:
            dst = mk.bf(c0, kt * mcols).rearrange("p (kt m) -> p kt m", kt=kt)
            op = mk.p.dma("pool", dst, sl, deps=dd, sem="WP%d" % subs[0],
                          tag="wload")
            slab = mk.bf(c0, kt * mcols)
        else:
            dst = mk.f32(c0, kt * mcols).rearrange("p (kt m) -> p kt m", kt=kt)
            op = mk.p.dma("sp", dst, sl, deps=dd, sem="WS%d" % subs[0],
                          tag="wloadf")
            slab = mk.f32(c0, kt * mcols)
        return op, slab, tuple(subs)

    def load_raw(self, src_ap, nf32, deps=()):
        mk = self.mk
        subs = self._alloc(nf32 * 2)
        c0 = WS_C + subs[0] * 2048
        dst = mk.f32(c0, nf32)
        op = mk.p.dma("sp", dst, src_ap,
                      deps=list(deps) + [self.free_after[i] for i in subs],
                      sem="WS%d" % subs[0], tag="wstrip")
        return op, dst, tuple(subs)

    def release(self, slot, op):
        for s in (slot if isinstance(slot, (tuple, list)) else (slot,)):
            self.free_after[s] = op


def make_mk(nc, ctx, p, bias_off, bias_w=None):
    mk = MK(nc, ctx, p, bias_off, bias_w)
    mk.ws = WStream(mk)
    return mk


def mm_jobs(mk, jobs, banks, tag="", state=None):
    p = mk.p
    if state is None:
        state = {}
    glast = mk.bank_last
    last = {id(b): glast.get(id(b)) for b in banks}
    j0 = state.setdefault("j", 0)
    out = []
    for j, job in enumerate(jobs, start=j0):
        b = banks[j % len(banks)]
        o_mm = p.add("pe", (lambda q, jb=job, bb=b: jb["mms"](q, bb)),
                     list(job.get("deps", ())) + [last[id(b)]],
                     tag=tag + "_mm")
        job["_mm"] = o_mm
        o_ev = p.add(job.get("evac_q", "dve"),
                     (lambda q, jb=job, bb=b: jb["evac"](q, bb)),
                     [o_mm] + list(job.get("evac_deps", ())), tag=tag + "_ev")
        last[id(b)] = o_ev
        glast[id(b)] = o_ev
        out.append(o_ev)
    state["j"] = j0 + len(jobs)
    return out


def e_linear_fm(mk, ws, wt_ap, K, M, rhs, evac, deps, banks,
                mcols=512, evac_q="act", tag="lin", nch_range=None,
                src_bf16=False, w_deps=()):
    kt = K // 128
    nmt = M // 128
    mps = mcols // 128
    if nch_range is None:
        nch_range = range(NCH)
    grid = [[None] * NCH for _ in range(nmt)]
    evs = []
    state = {}
    for s0 in range(0, nmt, mps):
        o_ld, slab, slot = ws.load(wt_ap, s0 * 128, mcols, K, deps=w_deps,
                                   src_bf16=src_bf16)
        jobs, meta = [], []
        for mi in range(mps):
            mt = s0 + mi
            if mt >= nmt:
                continue
            for nch in nch_range:
                def mms(q, bank, slab=slab, mi=mi, n=nch):
                    ii = None
                    for kk in range(kt):
                        ii = q.matmul(bank[:, 0:512],
                                      slab[:, kk * mcols + mi * 128:
                                           kk * mcols + mi * 128 + 128],
                                      rhs[kk][:, n * 512:(n + 1) * 512],
                                      start=(kk == 0), stop=(kk == kt - 1))
                    return ii
                jobs.append(dict(
                    mms=mms, deps=[o_ld] + list(deps),
                    evac=(lambda q, bank, mt=mt, n=nch: evac(q, bank, mt, n)),
                    evac_q=evac_q))
                meta.append((mt, nch))
        sevs = mm_jobs(mk, jobs, banks, tag=tag, state=state)
        for (mt, nch), ev in zip(meta, sevs):
            grid[mt][nch] = ev
        ws.release(slot, jobs[-1]["_mm"])
        evs += sevs
    return grid, evs



def e_layernorm(mk, deps, xn_c0=R0_C, keep_c0=None):
    p = mk.p
    XR = xr_tiles(mk)
    xn = reg_tiles(mk, xn_c0)
    sqb = [mk.f32(SM_C + i * 1024, 512) for i in range(2)]
    ntmp = [mk.f32(SM_C + 4096 + i * 1024, 512) for i in range(2)]
    out_ops = []
    ps1, ps2 = mk.ps[6], mk.ps[7]
    prev_chunk = None
    for nch in range(NCH):
        if keep_c0 is not None:
            mrep = mk.f32(keep_c0 + nch * 1024, 512)
            rrep = mk.f32(keep_c0 + 4096 + nch * 1024, 512)
        else:
            mrep = mk.f32(SM_C + 2048, 512)
            rrep = mk.f32(SM_C + 3072, 512)
        mm_ops = []
        for dt in range(DT):
            buf = sqb[dt % 2]
            dd = list(deps) + [mm_ops[-2] if len(mm_ops) >= 2 else prev_chunk]
            o_sq = p.add("act", lambda q, b=buf, d=dt, n=nch: q.activation(
                b[:], XR[d][:, n * 512:(n + 1) * 512], AF.Square), dd,
                tag="ln_sq")
            def mmfn(q, b=buf, d=dt, n=nch):
                q.matmul(ps2[:, 0:512], mk.ones_f[:], b[:],
                         start=(d == 0), stop=(d == DT - 1))
                return q.matmul(ps1[:, 0:512], mk.ones_f[:],
                                XR[d][:, n * 512:(n + 1) * 512],
                                start=(d == 0), stop=(d == DT - 1))
            o_mm = p.add("pe", mmfn,
                         list(deps) + [o_sq, prev_chunk], tag="ln_mm")
            mm_ops.append(o_mm)

        def statfn(q, mrep=mrep, rrep=rrep):
            q.tensor_scalar(mrep[:], ps1[:, 0:512], 1.0 / D, None,
                            op0=ALU.mult)
            q.tensor_tensor(rrep[:], mrep[:], mrep[:], op=ALU.mult)
            q.scalar_tensor_tensor(rrep[:], ps2[:, 0:512], 1.0 / D, rrep[:],
                                   op0=ALU.mult, op1=ALU.subtract)
            q.tensor_scalar(rrep[:], rrep[:], 1e-5, None, op0=ALU.add)
            return q.reciprocal(rrep[:], rrep[:])
        o_st = p.add("dve", statfn, [mm_ops[-1]], tag="ln_stat")
        o_rs = p.add("act", lambda q, rrep=rrep: q.activation(
            rrep[:], rrep[:], AF.Sqrt), [o_st], tag="ln_sqrt")
        for dt in range(DT):
            def nfn(q, d=dt, n=nch, mrep=mrep, rrep=rrep):
                q.tensor_tensor(ntmp[d % 2][:],
                                XR[d][:, n * 512:(n + 1) * 512],
                                mrep[:], op=ALU.subtract)
                return q.tensor_tensor(xn[d][:, n * 512:(n + 1) * 512],
                                       ntmp[d % 2][:], rrep[:], op=ALU.mult)
            out_ops.append(p.add("dve", nfn, [o_rs], tag="ln_norm"))
        prev_chunk = out_ops[-1]
    return out_ops



def e_l2_ffn(mk, W, deps):
    p = mk.p
    XR = xr_tiles(mk)
    xn = reg_tiles(mk, R0_C)
    ws = mk.ws
    psA = mk.ps[6]
    absb = [mk.f32(SM_C + i * 512, 128) for i in range(2)]
    mm_ops = []
    k = 0
    for s0 in range(4):
        o_ld, slab, slot = ws.load(W["l2_w"], s0 * 256, 256, D, deps=(),
                                   cast=False)
        for kt in range(DT):
            for mh in range(2):
                buf = absb[k % 2]
                dd = [o_ld, mm_ops[-2] if len(mm_ops) >= 2 else None] + \
                    (list(deps) if k < 2 else [])
                o_abs = p.add("act", lambda q, b=buf, sl=slab, kk=kt, m=mh:
                              q.activation(b[:], sl[:, kk * 256 + m * 128:
                                                    kk * 256 + m * 128 + 128],
                                           AF.Abs), dd, tag="l2_abs")
                o_mm = p.add("pe", lambda q, b=buf, first=(k == 0),
                             last=(k == 63):
                             q.matmul(psA[0:1, 0:128], mk.ones_f[:, 0:1],
                                      b[:], start=first, stop=last),
                             [o_abs], tag="l2_absmm")
                mm_ops.append(o_mm)
                k += 1
        ws.release(slot, mm_ops[-1])
    dsum = mk.rows[0:1, 0:1]
    o_red = p.add("dve", lambda q: q.tensor_reduce(
        dsum, psA[0:1, 0:128], axis=mybir.AxisListType.X, op=ALU.add),
        [mm_ops[-1]], tag="l2_dred")
    dcol = mk.bcol("scratch", 0, 2)
    psB = mk.ps[7]
    o_drep = p.add("pe", lambda q: q.matmul(
        psB[:, 0:1], mk.ones_row_f[:], dsum, start=True, stop=True),
        [o_red], tag="l2_drep")

    def dcols(q):
        q.tensor_scalar(dcol[:, 0:1], psB[:, 0:1], 0.7 / (D * D), None,
                        op0=ALU.mult)
        return q.tensor_scalar(dcol[:, 1:2], psB[:, 0:1], -0.7 / (D * D),
                               None, op0=ALU.mult)
    o_dcol = p.add("dve", dcols, [o_drep], tag="l2_dcol")

    tw = [mk.bf(R1_C + kt * 1024, 1024) for kt in range(DT)]
    gpos = mk.f32(SM_C + 1024, 128)
    gneg = mk.f32(SM_C + 1280, 128)
    t_last = None
    tern_ops = []
    for s0 in range(4):
        o_ld, slab, slot = ws.load(W["l2_w"], s0 * 256, 256, D,
                                   deps=(), cast=False)
        for kt in range(DT):
            for mh in range(2):
                mcol = s0 * 256 + mh * 128
                def tfn(q, sl=slab, kk=kt, mm=mh, mcol=mcol):
                    src = sl[:, kk * 256 + mm * 128: kk * 256 + mm * 128 + 128]
                    q.tensor_scalar(gpos[:], src, dcol[:, 0:1], None,
                                    op0=ALU.is_gt)
                    q.tensor_scalar(gneg[:], src, dcol[:, 1:2], None,
                                    op0=ALU.is_lt)
                    return q.tensor_tensor(tw[kk][:, mcol:mcol + 128],
                                           gpos[:], gneg[:], op=ALU.subtract)
                t_last = p.add("dve", tfn,
                               [o_ld, o_dcol] +
                               (list(deps) if t_last is None else []),
                               tag="l2_tern")
                tern_ops.append(t_last)
        ws.release(slot, t_last)

    alpha = mk.bcol("l2_alpha")
    tbe = mk.f32(SM_C + 1536, 512)
    jobs = []
    for mt in range(DT):
        for nch in range(NCH):
            def mms(q, bank, m=mt, n=nch):
                ii = None
                for kk in range(DT):
                    ii = q.matmul(bank[:, 0:512],
                                  tw[kk][:, m * 128:(m + 1) * 128],
                                  xn[kk][:, n * 512:(n + 1) * 512],
                                  start=(kk == 0), stop=(kk == DT - 1))
                return ii
            def evac(q, bank, m=mt, n=nch):
                q.tensor_scalar(tbe[:], bank[:, 0:512], alpha,
                                mk.bcol("l2_bias", m), op0=ALU.mult,
                                op1=ALU.add)
                return q.tensor_tensor(XR[m][:, n * 512:(n + 1) * 512],
                                       XR[m][:, n * 512:(n + 1) * 512],
                                       tbe[:], op=ALU.add)
            jobs.append(dict(mms=mms, deps=[t_last] + list(deps),
                             evac=evac, evac_q="dve"))
    return mm_jobs(mk, jobs, [mk.ps[0], mk.ps[1], mk.ps[2], mk.ps[3]],
                   tag="l2f")


def e_l2_attn(mk, W, deps):
    p = mk.p
    XR = xr_tiles(mk)
    xn = reg_tiles(mk, R0_C)
    ws = mk.ws
    H = 16

    sg = [mk.f32(SM_C + i * 2048, 1024) for i in range(2)]
    tmpe = [mk.bf(SM_C + 4096 + i * 1024, 1024) for i in range(2)]
    ssum = mk.bcol("scratch", 2, 2)
    srcp = mk.bcol("scratch", 4, 2)

    path_out = {}
    for path in ("q", "k"):
        reg_c0 = R1_C if path == "q" else R2_C
        s1 = reg_tiles(mk, reg_c0)
        bname = "l2_bq" if path == "q" else "l2_bk"
        wname = "l2_Wq" if path == "q" else "l2_Wk"
        fname = "l2_Wfq" if path == "q" else "l2_Wfk"
        frow = "l2_bfq" if path == "q" else "l2_bfk"

        def ev1(q, bank, mt, nch, s1=s1, bname=bname):
            return q.activation(s1[mt][:, nch * 512:(nch + 1) * 512],
                                bank[:, 0:512], AF.Identity,
                                bias=mk.bcol(bname, mt))
        grid1, evs1 = e_linear_fm(mk, ws, W[wname], D, D, xn, ev1, deps,
                                  [mk.ps[0], mk.ps[1]], tag="l2s1" + path)

        oldA, slabA, slotA = ws.load(W[fname], 0, 512, D, deps=())
        oldB, slabB, slotB = ws.load(W[fname], 512, 512, D, deps=())
        slabs = [(oldA, slabA), (oldB, slabB)]
        out_ops = []
        last_sm = [None, None]
        banks = [mk.ps[0], mk.ps[1], mk.ps[2], mk.ps[3]]
        lastb = {id(b): mk.bank_last.get(id(b)) for b in banks}
        last_mm = None
        for tt in range(16):
            evs2 = []
            for half in range(2):
                o_ld, slab = slabs[half]
                b = banks[(tt * 2 + half) % 4]
                def mms(q, bank, tt=tt, slab=slab, s1=s1):
                    ii = None
                    for kk in range(DT):
                        ii = q.matmul(bank[:, 0:512],
                                      s1[kk][:, tt * 128:(tt + 1) * 128],
                                      slab[:, kk * 512:(kk + 1) * 512],
                                      start=(kk == 0), stop=(kk == DT - 1))
                    return ii
                o_mm = p.add("pe", (lambda q, f=mms, bb=b: f(q, bb)),
                             [o_ld, lastb[id(b)], last_sm[tt % 2]] +
                             [grid1[kk][tt // 4] for kk in range(DT)],
                             tag="l2s2mm")
                last_mm = o_mm
                def evc(q, bank, tt=tt, half=half, frow=frow):
                    return q.tensor_tensor(
                        sg[tt % 2][:, half * 512:(half + 1) * 512],
                        bank[:, 0:512],
                        mk.browr(frow, half * 512, 512), op=ALU.add)
                o_ev = p.add("dve", (lambda q, f=evc, bb=b: f(q, bb)),
                             [o_mm], tag="l2s2ev")
                lastb[id(b)] = o_ev
                evs2.append(o_ev)
            def sexp(q, tt=tt):
                return q.activation(tmpe[tt % 2][:], sg[tt % 2][:], AF.Exp,
                                    accum_out=ssum[:, tt % 2:tt % 2 + 1])
            o_exp = p.add("act", sexp, evs2, tag="l2exp")
            if path == "q":
                prev = None
                for kk in range(DT):
                    psb = mk.ps_bf[4 + (kk % 2)]
                    def tp1(q, tt=tt, kk=kk, psb=psb):
                        return q.transpose(psb[:, 0:128],
                                           tmpe[tt % 2][:, kk * 128:(kk + 1) * 128],
                                           mk.ident[:])
                    o_tp = p.add("pe", tp1,
                                 [o_exp, out_ops[-2] if len(out_ops) >= 2 else None],
                                 tag="l2qtp")
                    def cp1(q, tt=tt, kk=kk, psb=psb, s1=s1):
                        return q.tensor_copy(
                            s1[kk][:, tt * 128:(tt + 1) * 128], psb[:, 0:128])
                    o_cp = p.add("dve", cp1, [o_tp], tag="l2qcp")
                    out_ops.append(o_cp)
                last_sm[tt % 2] = out_ops[-1]
            else:
                dst = mk.bf(reg_c0, 16384).rearrange(
                    "p (kt l) -> p kt l", kt=DT)[:, :, tt * 128:tt * 128 + 128]
                src3 = tmpe[tt % 2][:].rearrange("p (kt m) -> p kt m", kt=DT)
                def kdiv(q, tt=tt, dst=dst, src3=src3):
                    q.reciprocal(srcp[:, tt % 2:tt % 2 + 1],
                                 ssum[:, tt % 2:tt % 2 + 1])
                    mk.dve_gap(q)
                    return q.tensor_scalar(dst, src3,
                                           srcp[:, tt % 2:tt % 2 + 1], None,
                                           op0=ALU.mult)
                o_div = p.add("dve", kdiv, [o_exp], tag="l2kdiv")
                out_ops.append(o_div)
                last_sm[tt % 2] = o_div
        ws.release(slotA, last_mm)
        ws.release(slotB, last_mm)
        for b in banks:
            mk.bank_last[id(b)] = lastb[id(b)]
        path_out[path] = out_ops

    oldA, slabA, slotA = ws.load(W["l2_Wv"], 0, 512, D, deps=path_out["k"][-1:])
    oldB, slabB, slotB = ws.load(W["l2_Wv"], 512, 512, D, deps=path_out["k"][-1:])
    v_ops = []
    lastv = [None, None]
    last_mm = None
    for tt in range(16):
        mmops = []
        for half, (o_ld, slab) in enumerate([(oldA, slabA), (oldB, slabB)]):
            b = mk.ps[half]
            def mms(q, bank, tt=tt, slab=slab):
                ii = None
                for kk in range(DT):
                    ii = q.matmul(bank[:, 0:512],
                                  xn[kk][:, tt * 128:(tt + 1) * 128],
                                  slab[:, kk * 512:(kk + 1) * 512],
                                  start=(kk == 0), stop=(kk == DT - 1))
                return ii
            o_mm = p.add("pe", (lambda q, f=mms, bb=b: f(q, bb)),
                         [o_ld, lastv[half]], tag="l2v_mm")
            mmops.append(o_mm)
            last_mm = o_mm
        for half in range(2):
            b = mk.ps[half]
            dst = mk.bf(R0_C, 16384).rearrange(
                "p (kt l) -> p kt l", kt=DT)[:, half * 4:half * 4 + 4,
                                             tt * 128:tt * 128 + 128]
            def evc(q, bank, tt=tt, half=half, dst=dst):
                return q.tensor_tensor(
                    dst, bank[:, 0:512].rearrange("p (a m) -> p a m", a=4),
                    mk.browr("l2_bv", half * 512, 512).rearrange(
                        "p (a m) -> p a m", a=4), op=ALU.add)
            o_ev = p.add("dve", (lambda q, f=evc, bb=b: f(q, bb)),
                         mmops, tag="l2v_ev")
            lastv[half] = o_ev
            v_ops.append(o_ev)
    ws.release(slotA, last_mm)
    ws.release(slotB, last_mm)

    S_sb = mk.bf(WS_C, 1024)
    zcol = mk.f32(WS_C + 2048, 8)
    zcol_bf = mk.bf(WS_C + 2080, 8)
    ktm = mk.bf(R2_C, 16384).rearrange("p (kt l) -> p kt l", kt=DT)
    vtm = mk.bf(R0_C, 16384).rearrange("p (kt l) -> p kt l", kt=DT)
    qfm = reg_tiles(mk, R1_C)

    psS = mk.ps[6]
    S_ops = []
    for h in range(H):
        def smm(q, h=h):
            ii = None
            for tt in range(16):
                ii = q.matmul(psS[0:64, 0:64],
                              ktm[:, h // 2, tt * 128 + (h % 2) * 64:
                                  tt * 128 + (h % 2) * 64 + 64],
                              vtm[:, h // 2, tt * 128 + (h % 2) * 64:
                                  tt * 128 + (h % 2) * 64 + 64],
                              start=(tt == 0), stop=(tt == 15))
            return ii
        o_mm = p.add("pe", smm,
                     v_ops + path_out["k"] + [S_ops[-1] if S_ops else None],
                     tag="l2S_mm")
        r0 = (h % 2) * 64
        def sev(q, h=h, r0=r0):
            return q.activation(S_sb[r0:r0 + 64, h * 64:(h + 1) * 64],
                                psS[0:64, 0:64], AF.Copy)
        S_ops.append(p.add("act", sev, [o_mm], tag="l2S_ev"))

    psZ = mk.ps[7]
    def zmm(q):
        ii = None
        for kt in range(DT):
            for tt in range(16):
                ii = q.matmul(psZ[:, kt:kt + 1],
                              ktm[:, kt, tt * 128:(tt + 1) * 128],
                              mk.ones_bf[:, 0:1],
                              start=(tt == 0), stop=(tt == 15))
        return ii
    o_zmm = p.add("pe", zmm, path_out["k"] + [S_ops[-1]], tag="l2z_mm")
    def zcp(q):
        q.tensor_copy(zcol[:, 0:8], psZ[:, 0:8])
        return q.tensor_copy(zcol_bf[:, 0:8], psZ[:, 0:8])
    o_zcp = p.add("dve", zcp, [o_zmm], tag="l2zcp")

    tails = []
    psC1, psC2 = mk.ps[0], mk.ps[1]
    lastc = {id(psC1): None, id(psC2): None}
    dinvs = [mk.rows[0:1, 0:512], mk.rows2[0:1, 0:512]]
    reps2 = [mk.f32(SM_C, 512, rows=64), mk.f32(SM_C + 2048, 512, rows=64)]
    psDs = [mk.ps[6], mk.ps[7]]
    psRs = [mk.ps[5], mk.ps[4]]
    tbv = mk.f32(SM_C + 1024, 512)
    lastrep2 = [None, None]
    prev_cevp = [None, None]
    prev_rcp2 = [None, None]
    prev_repmm2 = [None, None]
    pend = [None]

    def flush2(pe_):
        if pe_ is None:
            return
        h, n, par, bC, o_rcp, o_cmm = pe_
        def repmm(q, par=par):
            return q.matmul(psRs[par][0:64, 0:512],
                            mk.ones_row_f[0:1, 0:64],
                            dinvs[par], start=True, stop=True)
        o_rep = p.add("pe", repmm, [o_rcp, lastrep2[par]], tag="l2rep")
        prev_repmm2[par] = o_rep
        o_repc = p.add("act", lambda q, par=par: q.activation(
            reps2[par][:], psRs[par][0:64, 0:512], AF.Copy),
            [o_rep, prev_cevp[par]], tag="l2repc")
        lastrep2[par] = o_repc
        def cev(q, bank, h=h, n=n, par=par):
            dt_, r0 = h // 2, (h % 2) * 64
            q.tensor_tensor(tbv[r0:r0 + 64, :], bank[0:64, 0:512],
                            reps2[par][:], op=ALU.mult)
            return q.tensor_tensor(
                XR[dt_][r0:r0 + 64, n * 512:(n + 1) * 512],
                XR[dt_][r0:r0 + 64, n * 512:(n + 1) * 512],
                tbv[r0:r0 + 64, :], op=ALU.add)
        o_cev = p.add("dve", (lambda q, f=cev, bb=bC: f(q, bb)),
                      [o_cmm, o_repc], tag="l2cev")
        lastc[id(bC)] = o_cev
        prev_cevp[par] = o_cev
        tails.append(o_cev)

    for h in range(H):
        for nch in range(NCH):
            par = (h * NCH + nch) % 2
            def dmm(q, h=h, n=nch, par=par):
                return q.matmul(
                    psDs[par][0:1, 0:512],
                    zcol_bf[(h % 2) * 64:(h % 2) * 64 + 64, h // 2:h // 2 + 1],
                    qfm[h // 2][(h % 2) * 64:(h % 2) * 64 + 64,
                                n * 512:(n + 1) * 512],
                    start=True, stop=True, tile_position=((h % 2) * 64, 0))
            o_dmm = p.add("pe", dmm,
                          [o_zcp, path_out["q"][-1], prev_rcp2[par]],
                          tag="l2den")
            o_rcp = p.add("dve", lambda q, par=par: q.reciprocal(
                dinvs[par], psDs[par][0:1, 0:512]),
                [o_dmm, prev_repmm2[par]], tag="l2drcp")
            prev_rcp2[par] = o_rcp
            flush2(pend[0])
            pend[0] = None
            bC = psC1 if par == 0 else psC2
            def cmm(q, bank, h=h, n=nch):
                return q.matmul(bank[0:64, 0:512],
                                S_sb[(h % 2) * 64:(h % 2) * 64 + 64,
                                     h * 64:(h + 1) * 64],
                                qfm[h // 2][(h % 2) * 64:(h % 2) * 64 + 64,
                                            n * 512:(n + 1) * 512],
                                start=True, stop=True,
                                tile_position=((h % 2) * 64, 0))
            o_cmm = p.add("pe", (lambda q, f=cmm, bb=bC: f(q, bb)),
                          [S_ops[h], lastc[id(bC)]], tag="l2ctx")
            pend[0] = (h, nch, par, bC, o_rcp, o_cmm)
    flush2(pend[0])
    pend[0] = None
    ws.free_after[0] = tails[-1]
    ws.free_after[1] = tails[-1]
    return tails


def e_layer2(mk, W, deps, stage="all"):
    ln_ops = e_layernorm(mk, deps)
    if stage == "ln":
        return ln_ops
    ffn_tails = e_l2_ffn(mk, W, ln_ops[-8:])
    if stage == "ffn":
        return ffn_tails
    attn_tails = e_l2_attn(mk, W, ffn_tails[-4:])
    return attn_tails





def e_bfly_build(mk, W, deps, kind, scr):
    p = mk.p
    down = kind == "down"
    src_name = "l3f_wcon" if down else "l3f_wexp_t"
    cpbase = 768 if down else 0
    stages = list(range(11, -1, -1)) if down else list(range(12))
    bld = [mk.f32(SM_C + i * 512, 128) for i in range(4)]
    bfcs = [mk.bfc, mk.f32(SM_C + 1024, 64)]
    btmpA = mk.f32(SM_C + 1152, 32)
    btmpB = mk.f32(SM_C + 1216, 32)
    banks = [mk.ps[0], mk.ps[1], mk.ps[2], mk.ps[3]]
    tails = list(deps)
    all_st = mk.scr_ready.setdefault("w3" if down else "w2", [])
    bias_ab = [mk.bcol("l3f_scr", 0, 32), mk.bcol("l3f_scr", 32, 32)]
    idn = mk.bmasks[:, 7 * 128:8 * 128]
    if not down:
        o_b0 = p.add("act", lambda q: q.activation(
            bias_ab[0][:], mk.bias[:, mk.bias_off["l3f_bexp"]:
                                   mk.bias_off["l3f_bexp"] + 32],
            AF.Copy), deps, tag="bfb0")
    for dh in range(2):
        t_tiles = [mk.f32(R1_C + ft * 1024, 512) for ft in range(32)]
        src = W[src_name][:, dh * 512:(dh + 1) * 512].rearrange(
            "(ft pp) m -> pp ft m", pp=128)
        dst = mk.f32(R1_C, 32 * 512).rearrange("pp (ft m) -> pp ft m", ft=32)
        o_ld = p.dma("sp", dst, src, deps=tails, sem="BLD", tag="bfld")
        last_t = [o_ld] * 32
        bank_rd = {id(b): mk.bank_last.get(id(b)) for b in banks}
        bank_rd[id(mk.ps[4])] = mk.bank_last.get(id(mk.ps[4]))
        bld_rd = [None] * 4
        bfl_rd = [None, None]
        last_dve = [None, None]
        bias_par = 0
        jb = 0
        bias_tail = o_b0 if (not down and dh == 0) else None
        for si, s in enumerate(stages):
            s_ = 1 << s
            bfc = bfcs[si % 2]
            o_bfl = p.dma("sp", bfc[:],
                          W["bfcols"][:, cpbase + s * 64:cpbase + s * 64 + 64],
                          deps=[bfl_rd[si % 2], o_ld], tag="bfcols")
            stage_last = None
            if s >= 7:
                for ft in range(32):
                    pt = ft ^ (s_ >> 7)
                    if pt < ft:
                        continue
                    bB = bld[(jb * 2) % 4]
                    bD = bld[(jb * 2 + 1) % 4]
                    def bldf(q, ft=ft, pt=pt, bB=bB, bD=bD, bfc=bfc):
                        q.tensor_scalar(bB[:], idn,
                                        bfc[:, 32 + ft:33 + ft], None,
                                        op0=ALU.mult)
                        return q.tensor_scalar(bD[:], idn,
                                               bfc[:, 32 + pt:33 + pt], None,
                                               op0=ALU.mult)
                    o_bld = p.add("pool", bldf,
                                  [o_bfl, bld_rd[(jb * 2) % 4],
                                   bld_rd[(jb * 2 + 1) % 4]], tag="bfbld")
                    bA_ = banks[(jb * 2) % 4]
                    bB_ = banks[(jb * 2 + 1) % 4]
                    def mmf(q, ft=ft, pt=pt, bB=bB, bD=bD, bA_=bA_, bB_=bB_,
                            t_tiles=t_tiles):
                        q.matmul(bA_[:, 0:512], bB[:], t_tiles[pt][:],
                                 start=True, stop=True)
                        return q.matmul(bB_[:, 0:512], bD[:], t_tiles[ft][:],
                                        start=True, stop=True)
                    o_mm = p.add("pe", mmf,
                                 [o_bld, last_t[ft], last_t[pt],
                                  bank_rd[id(bA_)], bank_rd[id(bB_)]],
                                 tag="bfmm")
                    bld_rd[(jb * 2) % 4] = o_mm
                    bld_rd[(jb * 2 + 1) % 4] = o_mm
                    def sttf(q, ft=ft, pt=pt, bA_=bA_, bB_=bB_, bfc=bfc,
                             t_tiles=t_tiles):
                        q.scalar_tensor_tensor(
                            t_tiles[ft][:], t_tiles[ft][:],
                            bfc[:, ft:ft + 1], bA_[:, 0:512],
                            op0=ALU.mult, op1=ALU.add)
                        return q.scalar_tensor_tensor(
                            t_tiles[pt][:], t_tiles[pt][:],
                            bfc[:, pt:pt + 1], bB_[:, 0:512],
                            op0=ALU.mult, op1=ALU.add)
                    o_st = p.add("dve", sttf, [o_mm], tag="bfstt")
                    last_t[ft] = o_st
                    last_t[pt] = o_st
                    bank_rd[id(bA_)] = o_st
                    bank_rd[id(bB_)] = o_st
                    stage_last = o_st
                    jb += 1
            else:
                msk = mk.bmasks[:, s * 128:(s + 1) * 128]
                for ft in range(32):
                    bB = bld[jb % 4]
                    def bldf(q, ft=ft, bB=bB, bfc=bfc, msk=msk):
                        return q.tensor_scalar(bB[:], msk,
                                               bfc[:, 32 + ft:33 + ft],
                                               None, op0=ALU.mult)
                    o_bld = p.add("pool", bldf, [o_bfl, bld_rd[jb % 4]],
                                  tag="bfbld")
                    bank = banks[jb % 4]
                    def mmf(q, ft=ft, bB=bB, bank=bank, t_tiles=t_tiles):
                        return q.matmul(bank[:, 0:512], bB[:],
                                        t_tiles[ft][:], start=True,
                                        stop=True)
                    o_mm = p.add("pe", mmf,
                                 [o_bld, last_t[ft], bank_rd[id(bank)]],
                                 tag="bfmm")
                    bld_rd[jb % 4] = o_mm
                    def sttf(q, ft=ft, bank=bank, bfc=bfc, t_tiles=t_tiles):
                        return q.scalar_tensor_tensor(
                            t_tiles[ft][:], t_tiles[ft][:],
                            bfc[:, ft:ft + 1], bank[:, 0:512],
                            op0=ALU.mult, op1=ALU.add)
                    o_st = p.add("dve", sttf, [o_mm], tag="bfstt")
                    last_t[ft] = o_st
                    bank_rd[id(bank)] = o_st
                    stage_last = o_st
                    jb += 1
            if not down and dh == 0:
                src_b = bias_ab[bias_par]
                dst_b = bias_ab[1 - bias_par]
                bias_par = 1 - bias_par
                if s < 7:
                    def b1(q, bfc=bfc, src_b=src_b):
                        return q.tensor_tensor(btmpB[:], src_b[:],
                                               bfc[:, 32:64], op=ALU.mult)
                    o_b1 = p.add("dve", b1, [o_bfl, bias_tail], tag="bfb1")
                    def bpm(q, msk=msk):
                        return q.tensor_scalar(bld[jb % 4][:], msk, 1.0,
                                               None, op0=ALU.mult)
                    o_bp = p.add("dve", bpm, [bld_rd[jb % 4]], tag="bfbp")
                    bperm = bld[jb % 4]
                    def bmm(q, bperm=bperm):
                        return q.matmul(mk.ps[4][:, 0:32], bperm[:],
                                        btmpB[:], start=True, stop=True)
                    o_bm = p.add("pe", bmm,
                                 [o_b1, o_bp, bank_rd[id(mk.ps[4])]],
                                 tag="bfbm")
                    bld_rd[jb % 4] = o_bm
                    def b23(q, bfc=bfc, src_b=src_b, dst_b=dst_b):
                        q.tensor_tensor(btmpA[:], src_b[:], bfc[:, 0:32],
                                        op=ALU.mult)
                        mk.dve_gap(q)
                        return q.tensor_tensor(dst_b[:], btmpA[:],
                                               mk.ps[4][:, 0:32],
                                               op=ALU.add)
                    o_b23 = p.add("dve", b23, [o_bm, bias_tail], tag="bfb23")
                    bank_rd[id(mk.ps[4])] = o_b23
                    bias_tail = o_b23
                else:
                    K = s_ >> 7
                    a = 32 // (2 * K)
                    sv = src_b[:].rearrange("p (a two c) -> p a two c",
                                            two=2, c=K)
                    bv = bfc[:, 32:64].rearrange("p (a two c) -> p a two c",
                                                 two=2, c=K)
                    tv = btmpB[:].rearrange("p (a two c) -> p a two c",
                                            two=2, c=K)
                    def b1(q, sv=sv, bv=bv, tv=tv):
                        q.tensor_tensor(tv[:, :, 0, :], sv[:, :, 1, :],
                                        bv[:, :, 0, :], op=ALU.mult)
                        return q.tensor_tensor(tv[:, :, 1, :],
                                               sv[:, :, 0, :],
                                               bv[:, :, 1, :], op=ALU.mult)
                    o_b1 = p.add("dve", b1, [o_bfl, bias_tail], tag="bfb1")
                    def b23(q, bfc=bfc, src_b=src_b, dst_b=dst_b):
                        q.tensor_tensor(btmpA[:], src_b[:], bfc[:, 0:32],
                                        op=ALU.mult)
                        mk.dve_gap(q)
                        return q.tensor_tensor(dst_b[:], btmpA[:],
                                               btmpB[:], op=ALU.add)
                    o_b23 = p.add("dve", b23, [o_b1, bias_tail], tag="bfb23")
                    bias_tail = o_b23
            bfl_rd[si % 2] = stage_last if bias_tail is None else bias_tail
            last_dve[si % 2] = stage_last
        prev_stage = [x for x in last_t] + ([bias_tail] if bias_tail else [])
        for b in banks + [mk.ps[4]]:
            mk.bank_last[id(b)] = bank_rd[id(b)]
        if down:
            st_ops = []
            for ft in range(32):
                st_ops.append(p.dma(
                    "pool", scr[ft * 128:(ft + 1) * 128,
                                dh * 512:(dh + 1) * 512],
                    t_tiles[ft][:], deps=[last_t[ft]], tag="bfst"))
            tails = st_ops
            all_st += st_ops[-1:]
        else:
            stg = mk.bf(SM_C + 2048, 4096)
            st_ops = []
            for dtl in range(4):
                tp_ops = []
                for ft in range(32):
                    bank = mk.ps[ft % 2]
                    def tpf(q, ft=ft, dtl=dtl, bank=bank, t_tiles=t_tiles):
                        return q.transpose(bank[:, 0:128],
                                           t_tiles[ft][:, dtl * 128:(dtl + 1) * 128],
                                           mk.ident_f[:])
                    dd = [last_t[ft], bank_rd[id(bank)]] + st_ops[-1:] + \
                        tp_ops[-2:]
                    o_tp = p.add("pe", tpf, dd, tag="bftp")
                    o_cp = p.add("act", lambda q, ft=ft, bank=bank, stg=stg:
                                 q.activation(stg[:, ft * 128:(ft + 1) * 128],
                                              bank[:, 0:128], AF.Copy),
                                 [o_tp], tag="bftc")
                    bank_rd[id(bank)] = o_cp
                    tp_ops.append(o_cp)
                st_ops.append(p.dma(
                    "pool", scr[dh * 512 + dtl * 128: dh * 512 + (dtl + 1) * 128, :],
                    stg[:], deps=tp_ops, tag="bfst"))
            for b in banks:
                mk.bank_last[id(b)] = bank_rd[id(b)]
            tails = st_ops
            all_st += st_ops[-1:]
    if not down:
        def b2f(q):
            return q.tensor_tensor(
                bias_ab[1][:], bias_ab[0][:],
                mk.bias[:, mk.bias_off["l3f_upb"]:mk.bias_off["l3f_upb"] + 32],
                op=ALU.add)
        tails = tails + [p.add("dve", b2f, tails + prev_stage, tag="b2f")]
    return tails


def e_l3_ffn(mk, W, deps, scr_w2, scr_w3, build):
    p = mk.p
    XR = xr_tiles(mk)
    xn = reg_tiles(mk, R0_C)
    ws = mk.ws
    tails = list(deps)
    if build:
        tails = e_bfly_build(mk, W, tails, "up", scr_w2)
        tails = e_bfly_build(mk, W, tails, "down", scr_w3)
        def b3f(q):
            return q.tensor_tensor(
                mk.bcol("l3f_scr", 64, 8),
                mk.bias[:, mk.bias_off["l3f_bcon"]:mk.bias_off["l3f_bcon"] + 8],
                mk.bias[:, mk.bias_off["l3f_dnb"]:mk.bias_off["l3f_dnb"] + 8],
                op=ALU.add)
        tails = tails + [p.add("dve", b3f, tails, tag="b3f")]
    b2col = mk.bcol("l3f_scr", 32, 32)
    b3col = mk.bcol("l3f_scr", 64, 8)
    scr2ap = scr_w2
    scr3ap = scr_w3
    out_tails = []
    hbuf = [mk.bf(R1_C + ft * 512, 512) for ft in range(32)]
    for tb in range(NCH):
        def ev_h(q, bank, mt, nch, tb=tb):
            return q.activation(hbuf[mt][:], bank[:, 0:512], AF.Gelu,
                                bias=b2col[:, mt:mt + 1])
        grid, evs = e_linear_fm(mk, ws, scr2ap, D, 4096, xn, ev_h,
                                tails, [mk.ps[0], mk.ps[1], mk.ps[4]],
                                mcols=512,
                                tag="l3h", nch_range=[tb], src_bf16=True,
                                w_deps=mk.scr_ready.get("w2", ()))
        tbe = mk.f32(SM_C, 512)
        evs2 = []
        ystate = {}
        for s0 in range(8):
            o_ld, slab, slot = ws.load(scr3ap, s0 * 128, 128, 4096,
                                       deps=mk.scr_ready.get("w3", ()),
                                       src_bf16=True)

            def mms(q, bank, slab=slab):
                ii = None
                for kk in range(32):
                    ii = q.matmul(bank[:, 0:512],
                                  slab[:, kk * 128:(kk + 1) * 128],
                                  hbuf[kk][:], start=(kk == 0),
                                  stop=(kk == 31))
                return ii

            def evac(q, bank, mt=s0, tb=tb):
                q.tensor_scalar(tbe[:], bank[:, 0:512],
                                b3col[:, mt:mt + 1], None, op0=ALU.add)
                return q.tensor_tensor(XR[mt][:, tb * 512:(tb + 1) * 512],
                                       XR[mt][:, tb * 512:(tb + 1) * 512],
                                       tbe[:], op=ALU.add)
            sevs = mm_jobs(mk, [dict(mms=mms, deps=[o_ld] + list(evs),
                                     evac=evac, evac_q="dve")],
                           [mk.ps[2], mk.ps[3], mk.ps[5]], tag="l3y",
                           state=ystate)
            ws.release(slot, sevs[-1])
            evs2 += sevs
        out_tails = evs2
        tails = evs2[-2:]
    return out_tails


def e_gated_linear(mk, w_ap, m_ap, K, M, rhs, evac, deps, banks,
                   tag="gl", nch_range=None, m_base=0, evac_q="act"):
    p = mk.p
    ws = mk.ws
    kt = K // 128
    nmt = M // 128
    if nch_range is None:
        nch_range = range(NCH)
    gated = mk.bf(SM_C + 4096, 2048)
    evs = []
    glast = None
    for s0 in range(0, nmt, 2):
        o_lw, slw, slot_w = ws.load(w_ap, m_base + s0 * 128, 256, K,
                                    deps=(), cast=False)
        o_lm, slm, slot_m = ws.load(m_ap, m_base + s0 * 128, 256, K,
                                    deps=(), cast=False)
        g_ops = []
        for kk in range(kt):
            def gf(q, kk=kk, slw=slw, slm=slm):
                return q.scalar_tensor_tensor(
                    gated[:, kk * 256:(kk + 1) * 256],
                    slm[:, kk * 256:(kk + 1) * 256], 0.0,
                    slw[:, kk * 256:(kk + 1) * 256],
                    op0=ALU.is_gt, op1=ALU.mult)
            g_ops.append(p.add("dve", gf,
                               [o_lw, o_lm, glast] + list(deps),
                               tag=tag + "_g"))
        jobs = []
        for mi in range(2):
            mt = s0 + mi
            if mt >= nmt:
                continue
            for nch in nch_range:
                def mms(q, bank, mi=mi, n=nch):
                    ii = None
                    for kk in range(kt):
                        ii = q.matmul(bank[:, 0:512],
                                      gated[:, kk * 256 + mi * 128:
                                            kk * 256 + mi * 128 + 128],
                                      rhs[kk][:, n * 512:(n + 1) * 512],
                                      start=(kk == 0), stop=(kk == kt - 1))
                    return ii
                jobs.append(dict(
                    mms=mms, deps=g_ops,
                    evac=(lambda q, bank, mt=mt, n=nch: evac(q, bank, mt, n)),
                    evac_q=evac_q))
        jevs = mm_jobs(mk, jobs, banks, tag=tag)
        glast = jevs[-1]
        ws.release(slot_w, jevs[-1])
        ws.release(slot_m, jevs[-1])
        evs += jevs
    return evs


def e_l3_attn(mk, W, deps, scr_xr):
    p = mk.p
    XR = xr_tiles(mk)
    xn = reg_tiles(mk, R0_C)
    H = 16

    def gbf(q):
        return q.scalar_tensor_tensor(
            mk.bcol("scratch", 8, 8), mk.bcol("l3_obm", 0, 8), 0.0,
            mk.bcol("l3_ob", 0, 8), op0=ALU.is_gt, op1=ALU.mult)
    o_gb = p.add("dve", gbf, deps, tag="l3gb")

    qf = reg_tiles(mk, R1_C)
    kf = reg_tiles(mk, R2_C)
    def evq(q, bank, mt, nch):
        return q.activation(qf[mt][:, nch * 512:(nch + 1) * 512],
                            bank[:, 0:512], AF.Copy)
    def evk(q, bank, mt, nch):
        return q.activation(kf[mt][:, nch * 512:(nch + 1) * 512],
                            bank[:, 0:512], AF.Copy)
    evs_q = e_gated_linear(mk, W["l3_qkv_t"], W["l3_qkvm_t"], D, D, xn, evq,
                           deps, [mk.ps[0], mk.ps[1], mk.ps[2], mk.ps[3]],
                           tag="l3q", m_base=0)
    evs_k = e_gated_linear(mk, W["l3_qkv_t"], W["l3_qkvm_t"], D, D, xn, evk,
                           evs_q[-1:], [mk.ps[0], mk.ps[1], mk.ps[2],
                                        mk.ps[3]], tag="l3k",
                           m_base=D)

    sp_ops = []
    for dt in range(DT):
        sp_ops.append(p.dma("sp", scr_xr[dt * 128:(dt + 1) * 128, :], XR[dt],
                            deps=deps, tag="xrspill"))

    vext = [mk.bf(XR_C + tt * 1040, 1040) for tt in range(16)]
    ones_ops = []
    for tt in range(16):
        dst = vext[tt].rearrange("pp (h c) -> pp h c", h=16)[:, :, 64:65]
        ones_ops.append(p.add(
            "dve", lambda q, dst=dst: q.memset(dst, 1.0),
            sp_ops, tag="vones"))
    v_evs = []
    vstate = {}
    vlast = None
    gated = mk.bf(SM_C + 4096, 2048)
    ws = mk.ws
    for s0 in range(4):
        o_lw, slw, slot_w = ws.load(W["l3_qkv_t"], 2 * D + s0 * 256, 256, D,
                                    deps=(), cast=False)
        o_lm, slm, slot_m = ws.load(W["l3_qkvm_t"], 2 * D + s0 * 256, 256, D,
                                    deps=(), cast=False)
        g_ops = []
        for kk in range(DT):
            def gf(q, kk=kk, slw=slw, slm=slm):
                return q.scalar_tensor_tensor(
                    gated[:, kk * 256:(kk + 1) * 256],
                    slm[:, kk * 256:(kk + 1) * 256], 0.0,
                    slw[:, kk * 256:(kk + 1) * 256],
                    op0=ALU.is_gt, op1=ALU.mult)
            g_ops.append(p.add("dve", gf,
                               [o_lw, o_lm, vlast] + list(evs_k[-1:]),
                               tag="l3v_g"))
        jobs = []
        for tt in range(16):
            def mms(q, bank, tt=tt):
                ii = None
                for kk in range(DT):
                    ii = q.matmul(bank[:, 0:256],
                                  xn[kk][:, tt * 128:(tt + 1) * 128],
                                  gated[:, kk * 256:(kk + 1) * 256],
                                  start=(kk == 0), stop=(kk == DT - 1))
                return ii
            h0 = (s0 * 256) // 64
            dst = vext[tt].rearrange("pp (h c) -> pp h c", h=16)[
                :, h0:h0 + 4, 0:64]
            def evac(q, bank, tt=tt, dst=dst):
                return q.tensor_copy(
                    dst, bank[:, 0:256].rearrange("pp (a c) -> pp a c", a=4))
            jobs.append(dict(mms=mms, deps=g_ops + sp_ops + ones_ops,
                             evac=evac, evac_q="dve"))
        jevs = mm_jobs(mk, jobs, [mk.ps[0], mk.ps[1], mk.ps[2], mk.ps[3]],
                       tag="l3v",
                       state=vstate)
        vlast = jevs[-1]
        ws.release(slot_w, jevs[-1])
        ws.release(slot_m, jevs[-1])
        v_evs += jevs

    esc = [mk.bf(XR_C + 16640 + kt * 512, 512) for kt in range(16)]
    ctx = reg_tiles(mk, R0_C)
    ctx_evs = []
    sbanks = [mk.ps[0], mk.ps[1], mk.ps[2], mk.ps[3]]
    lastexp = {}
    lastctx = [None, None]
    lastrep = [None, None]
    last_cmm = None
    prev_cev = None
    prev_cevs = [None, None]
    rcprs = [mk.rows[0:1, 0:512], mk.rows2[0:1, 0:512]]
    reps = [mk.f32(SM_C, 512, rows=64), mk.f32(SM_C + 1024, 512, rows=64)]
    repbanks = [mk.ps[6], mk.ps[7]]
    pend = [None]

    def flush3(pe_):
        if pe_ is None:
            return
        h, qb, cb, r0, par, o_cmm, o_rcp = pe_
        rcpr, rep, repbank = rcprs[par], reps[par], repbanks[par]
        o_rep = p.add("pe", lambda q, rcpr=rcpr, repbank=repbank:
                      q.matmul(repbank[0:64, 0:512],
                               mk.ones_row_f[0:1, 0:64], rcpr,
                               start=True, stop=True),
                      [o_rcp, prev_cevs[par]], tag="l3rep")
        o_repc = p.add("act", lambda q, rep=rep, repbank=repbank:
                       q.activation(rep[:], repbank[0:64, 0:512],
                                    AF.Copy), [o_rep], tag="l3repc")
        lastrep[par] = o_repc

        def cev(q, h=h, qb=qb, cb=cb, r0=r0, rep=rep):
            return q.tensor_tensor(
                ctx[h // 2][r0:r0 + 64, qb * 512:(qb + 1) * 512],
                cb[0:64, 0:512], rep[:], op=ALU.mult)
        o_cev = p.add("dve", cev, [o_repc, o_cmm] + v_evs[-1:],
                      tag="l3cev")
        lastctx[qb % 2] = o_cev
        prev_cevs[par] = o_cev
        ctx_evs.append(o_cev)

    for h in range(H):
        r0 = (h % 2) * 64
        for qb in range(NCH):
            par = (h * NCH + qb) % 2
            rcpr = rcprs[par]
            exp_ops = []
            for kt in range(16):
                b = sbanks[kt % 4]
                def smm(q, bank, h=h, qb=qb, kt=kt, r0=r0):
                    return q.matmul(
                        bank[:, 0:512],
                        kf[h // 2][r0:r0 + 64, kt * 128:(kt + 1) * 128],
                        qf[h // 2][r0:r0 + 64, qb * 512:(qb + 1) * 512],
                        start=True, stop=True, tile_position=(r0, 0))
                dd = [v_evs[-1], lastexp.get(kt % 4), last_cmm,
                      evs_q[-1], evs_k[-1]]
                o_smm = p.add("pe", (lambda q, f=smm, bb=b: f(q, bb)), dd,
                              tag="l3smm")
                def sexp(q, bank, kt=kt):
                    return q.activation(esc[kt][:], bank[:, 0:512], AF.Exp,
                                        scale=1.0 / 32.0)
                o_exp = p.add("act", (lambda q, f=sexp, bb=b: f(q, bb)),
                              [o_smm], tag="l3exp")
                lastexp[kt % 4] = o_exp
                exp_ops.append(o_exp)
            flush3(pend[0])
            pend[0] = None
            cb = mk.ps[4 + (qb % 2)]
            o_cmm = None
            for kt in range(16):
                def cmm1(q, h=h, cb=cb, kt=kt):
                    return q.matmul(cb[0:65, 0:512],
                                    vext[kt][:, h * 65:h * 65 + 65],
                                    esc[kt][:], start=(kt == 0),
                                    stop=(kt == 15), skip_group_check=True)
                dd = [exp_ops[kt]] + ([lastctx[qb % 2]] + v_evs[-1:]
                                      if kt == 0 else [])
                o_cmm = p.add("pe", cmm1, dd, tag="l3cmm")
            last_cmm = o_cmm
            o_rcp = p.add("dve", lambda q, cb=cb, rcpr=rcpr: q.reciprocal(
                rcpr, cb[64:65, 0:512]), [o_cmm, lastrep[par]], tag="l3rcp")
            pend[0] = (h, qb, cb, r0, par, o_cmm, o_rcp)
    flush3(pend[0])
    pend[0] = None

    rs_ops = []
    for dt in range(DT):
        rs_ops.append(p.dma("sp", XR[dt], scr_xr[dt * 128:(dt + 1) * 128, :],
                            deps=ctx_evs, tag="xrrest"))

    tbo = mk.f32(SM_C, 512)
    def evo(q, bank, mt, nch):
        q.tensor_scalar(tbo[:], bank[:, 0:512],
                        mk.bcol("scratch", 8 + mt, 1), None, op0=ALU.add)
        return q.tensor_tensor(XR[mt][:, nch * 512:(nch + 1) * 512],
                               XR[mt][:, nch * 512:(nch + 1) * 512],
                               tbo[:], op=ALU.add)
    evs_o = e_gated_linear(mk, W["l3_out_t"], W["l3_outm_t"], D, D, ctx, evo,
                           rs_ops + ctx_evs + [o_gb],
                           [mk.ps[0], mk.ps[1], mk.ps[2], mk.ps[3]],
                           tag="l3o", evac_q="dve")
    return evs_o


def e_layer3(mk, W, deps, stage="all"):
    ln_ops = e_layernorm(mk, deps)
    scr_w2 = mk.scr["w2"]
    scr_w3 = mk.scr["w3"]
    scr_xr = mk.scr["xr"]
    ffn_tails = e_l3_ffn(mk, W, ln_ops[-8:], scr_w2, scr_w3,
                         build=not mk.l3_built)
    mk.l3_built = True
    if stage == "ffn":
        return ffn_tails
    return e_l3_attn(mk, W, ffn_tails, scr_xr)





def e_ln_l0ffn(mk, deps, xn_c0=R0_C):
    p = mk.p
    XR = xr_tiles(mk)
    xn = reg_tiles(mk, xn_c0)
    sqb = [mk.f32(SM_C + i * 1024, 512) for i in range(2)]
    ntmp = [mk.f32(SM_C + 4096 + i * 1024, 512) for i in range(2)]
    mrep = mk.f32(SM_C + 2048, 512)
    rrep = mk.f32(SM_C + 3072, 512)
    svar = mk.f32(SM_C + 6144 - 1024, 512) if False else None
    out_ops = []
    ps1, ps2 = mk.ps[6], mk.ps[7]
    prev_chunk = None
    for nch in range(NCH):
        varb = ntmp[1]
        mm_ops = []
        for dt in range(DT):
            buf = sqb[dt % 2]
            dd = list(deps) + [mm_ops[-2] if len(mm_ops) >= 2 else prev_chunk]
            o_sq = p.add("act", lambda q, b=buf, d=dt, n=nch: q.activation(
                b[:], XR[d][:, n * 512:(n + 1) * 512], AF.Square), dd,
                tag="l0n_sq")
            def mmfn(q, b=buf, d=dt, n=nch):
                q.matmul(ps2[:, 0:512], mk.ones_f[:], b[:],
                         start=(d == 0), stop=(d == DT - 1))
                return q.matmul(ps1[:, 0:512], mk.ones_f[:],
                                XR[d][:, n * 512:(n + 1) * 512],
                                start=(d == 0), stop=(d == DT - 1))
            o_mm = p.add("pe", mmfn, list(deps) + [o_sq, prev_chunk],
                         tag="l0n_mm")
            mm_ops.append(o_mm)

        def statfn(q, varb=varb):
            q.tensor_scalar(mrep[:], ps1[:, 0:512], 1.0 / D, None,
                            op0=ALU.mult)
            q.tensor_tensor(varb[:], mrep[:], mrep[:], op=ALU.mult)
            q.scalar_tensor_tensor(varb[:], ps2[:, 0:512], 1.0 / D, varb[:],
                                   op0=ALU.mult, op1=ALU.subtract)
            q.tensor_scalar(rrep[:], varb[:], 1e-5, None, op0=ALU.add)
            return q.reciprocal(rrep[:], rrep[:])
        o_st = p.add("dve", statfn, [mm_ops[-1]], tag="l0n_st")
        o_rs = p.add("act", lambda q: q.activation(
            rrep[:], rrep[:], AF.Sqrt), [o_st], tag="l0n_rs")
        o_sv = p.add("act", lambda q, varb=varb: q.activation(
            varb[:], varb[:], AF.Sqrt), [o_rs], tag="l0n_sv")

        def sfn(q, varb=varb):
            q.tensor_tensor(varb[:], varb[:], rrep[:], op=ALU.mult)
            q.tensor_scalar(varb[:], varb[:], 1e-8, None, op0=ALU.add)
            q.reciprocal(varb[:], varb[:])
            return q.tensor_tensor(rrep[:], rrep[:], varb[:], op=ALU.mult)
        o_sf = p.add("dve", sfn, [o_sv], tag="l0n_sf")
        for dt in range(DT):
            def nfn(q, d=dt, n=nch):
                q.tensor_tensor(ntmp[0][:], XR[d][:, n * 512:(n + 1) * 512],
                                mrep[:], op=ALU.subtract)
                return q.tensor_tensor(xn[d][:, n * 512:(n + 1) * 512],
                                       ntmp[0][:], rrep[:], op=ALU.mult)
            out_ops.append(p.add("dve", nfn, [o_sf], tag="l0n_n"))
        prev_chunk = out_ops[-1]
    return out_ops


def e_l0_attn(mk, W, deps, scr_xr, until="all"):
    p = mk.p
    XR = xr_tiles(mk)
    xn = reg_tiles(mk, R0_C)
    ws = mk.ws
    H = 16

    sp_ops = [p.dma("sp", scr_xr[dt * 128:(dt + 1) * 128, :], XR[dt],
                    deps=deps, tag="l0spill") for dt in range(DT)]
    cosb = mk.bf(XR_C + 24832, 2048)
    sinb = mk.bf(XR_C + 26880, 2048)
    trilb = mk.bf(XR_C + 28928, 2048)
    o_cos = p.dma("pool", cosb, W["rope_cos"], deps=sp_ops, tag="l0rope")
    o_sin = p.dma("pool", sinb, W["rope_sin"], deps=sp_ops, tag="l0rope")
    o_tril = p.dma("pool", trilb, W["tril4"], deps=sp_ops, tag="l0tril")

    latb = mk.bf(SM_C, 2048)
    def evl(q, bank, mt, nch):
        return q.activation(latb[:, nch * 512:(nch + 1) * 512],
                            bank[:, 0:512], AF.Copy)
    grid_l, evs_l = e_linear_fm(mk, ws, W["l0_wkvd_t"], D, 128, xn, evl,
                                deps, [mk.ps[0], mk.ps[1], mk.ps[2],
                                       mk.ps[3]], mcols=128,
                                tag="l0lat")

    if until == "lat":
        return evs_l
    rt = mk.f32(SM_C + 4096, 512)
    rsh = mk.f32(SM_C + 5120, 512)

    def rope_evac(dst_tiles):
        def ev(q, bank, mt, nch):
            q.tensor_tensor(rt[:], bank[:, 0:512],
                            cosb[:, nch * 512:(nch + 1) * 512], op=ALU.mult)
            for a, b in ((0, 32), (32, 0), (64, 96), (96, 64)):
                q.tensor_copy(rsh[a:a + 32, :], bank[b:b + 32, 0:512])
            q.tensor_tensor(rsh[:], rsh[:],
                            sinb[:, nch * 512:(nch + 1) * 512], op=ALU.mult)
            return q.tensor_tensor(
                dst_tiles[mt][:, nch * 512:(nch + 1) * 512],
                rt[:], rsh[:], op=ALU.add)
        return ev

    qf = reg_tiles(mk, R1_C)
    kf = reg_tiles(mk, R2_C)
    grid_q, evs_q = e_linear_fm(mk, ws, W["l0_wq_t"], D, D, xn,
                                rope_evac(qf), [o_cos, o_sin] + evs_l[-1:],
                                [mk.ps[0], mk.ps[1], mk.ps[2], mk.ps[3]],
                                evac_q="dve",
                                tag="l0q")
    if until == "q":
        return evs_q
    grid_k, evs_k = e_linear_fm(mk, ws, W["l0_wkvu_t"], 128, D, [latb],
                                rope_evac(kf), evs_q[-1:],
                                [mk.ps[0], mk.ps[1], mk.ps[2], mk.ps[3]],
                                evac_q="dve",
                                tag="l0k")
    if until == "k":
        return evs_k

    vext = [mk.bf(XR_C + tt * 1040, 1040) for tt in range(16)]
    ones_ops = []
    if until not in ("v0", "v1"):
        for tt in range(16):
            dst = vext[tt].rearrange("pp (h c) -> pp h c", h=16)[:, :, 64:65]
            ones_ops.append(p.add("dve",
                                  lambda q, dst=dst: q.memset(dst, 1.0),
                                  sp_ops, tag="l0vones"))
    v_evs = []
    vstate = {}
    nhalf = 1 if until == "v0" else 2
    no_ones = until in ("v0", "v1")
    for half in range(nhalf):
        o_ld, slab, slot = ws.load(W["l0_wkvu_t"], D + half * 512, 512, 128,
                                   deps=())
        jobs = []
        for tt in range(16):
            def mms(q, bank, tt=tt, slab=slab):
                return q.matmul(bank[:, 0:512],
                                latb[:, tt * 128:(tt + 1) * 128],
                                slab[:, 0:512], start=True, stop=True)
            h0 = half * 8
            dst = vext[tt].rearrange("pp (h c) -> pp h c", h=16)[
                :, h0:h0 + 8, 0:64]
            def evac(q, bank, tt=tt, dst=dst):
                return q.tensor_copy(
                    dst, bank[:, 0:512].rearrange("pp (a c) -> pp a c", a=8))
            jobs.append(dict(mms=mms,
                             deps=[o_ld] + list(evs_k[-2:]) +
                             ([] if no_ones else ones_ops),
                             evac=evac, evac_q="dve"))
        jevs = mm_jobs(mk, jobs, [mk.ps[0], mk.ps[1], mk.ps[2], mk.ps[3]],
                       tag="l0v",
                       state=vstate)
        ws.release(slot, jevs[-1])
        v_evs += jevs
    if until in ("v0", "v1"):
        return v_evs

    if until == "qkv":
        return v_evs
    scor = mk.bf(SM_C + 1600, 8, rows=128)
    o_lds = p.dma("pool", scor, W["l0_scorer_kt"], deps=v_evs[-1:],
                  tag="l0scor")

    def smmf(q):
        ii = None
        for nchi in range(NCH):
            for kk in range(DT):
                ii = q.matmul(mk.ps[4 + nchi][0:1, 0:512],
                              scor[:, kk:kk + 1],
                              xn[kk][:, nchi * 512:(nchi + 1) * 512],
                              start=(kk == 0), stop=(kk == DT - 1))
        return ii
    o_smm = p.add("pe", smmf, [o_lds], tag="l0stok")

    prow = mk.f32(SM_C, 512, rows=1)
    srow = mk.f32(SM_C + 1024, 32, rows=1)
    scol = mk.f32(SM_C + 1200, 1, rows=32)
    g1 = mk.f32(SM_C + 1280, 32, rows=32)
    g2 = mk.f32(SM_C + 1408, 32, rows=32)
    mcol = mk.f32(SM_C + 1536, 1, rows=32)

    def poolf(q):
        ii = None
        for nchi in range(NCH):
            q.tensor_copy(prow[:], mk.ps[4 + nchi][0:1, 0:512])
            c = 64
            while c > 1:
                h_ = c // 2
                src3 = prow[0:1, 0:8 * c].rearrange("r (b c) -> r b c", c=c)
                dst3 = prow[0:1, 0:8 * h_].rearrange("r (b c) -> r b c", c=h_)
                if c <= 16:
                    mk.dve_gap(q)
                q.tensor_tensor(dst3, src3[:, :, 0:h_], src3[:, :, h_:c],
                                op=ALU.add)
                c = h_
            mk.dve_gap(q)
            ii = q.tensor_copy(srow[0:1, nchi * 8:nchi * 8 + 8],
                               prow[0:1, 0:8])
            mk.dve_gap(q)
        return ii
    o_pool = p.add("dve", poolf, [o_smm], tag="l0pool")
    o_st = p.dma("sp", mk.scr["row"][0:1, 0:32], srow, deps=[o_pool],
                 tag="l0srowst")
    o_scol = p.dma("sp", scol,
                   mk.scr["row"][0:1, 0:32].rearrange(
                       "r (pp c) -> pp (r c)", c=1),
                   deps=[o_st], tag="l0scolld", slow=True)
    o_srep = p.add("pe", lambda q: q.matmul(
        mk.ps[7][0:32, 0:32], mk.ones_row_f[0:1, 0:32], srow,
        start=True, stop=True), [o_pool, o_scol], tag="l0srep")

    def rankf(q):
        q.tensor_scalar(g1[:], mk.ps[7][0:32, 0:32], scol, None,
                        op0=ALU.is_gt)
        q.tensor_scalar(g2[:], mk.ps[7][0:32, 0:32], scol, None,
                        op0=ALU.is_equal)
        mk.dve_gap(q)
        q.tensor_tensor(g2[:], g2[:], mk.sel[0:32, 144:176], op=ALU.mult)
        mk.dve_gap(q)
        q.tensor_tensor(g1[:], g1[:], g2[:], op=ALU.add)
        mk.dve_gap(q)
        q.tensor_reduce(mcol, g1[:], axis=mybir.AxisListType.X, op=ALU.add)
        mk.dve_gap(q)
        q.tensor_scalar(mcol, mcol, 4.0, NEG, op0=ALU.is_ge, op1=ALU.mult)
        mk.dve_gap(q)
        return q.tensor_scalar(mk.selbf[0:32, 0:16], mk.sel[0:32, 128:144],
                               mcol, None, op0=ALU.mult)
    o_rank = p.add("dve", rankf, [o_srep], tag="l0rank")
    o_km = p.add("pe", lambda q: q.matmul(
        mk.ps[7][:, 0:16], mk.selA[0:32, 0:128], mk.selbf[0:32, 0:16],
        start=True, stop=True), [o_rank], tag="l0km")
    kmask = mk.bcol("l0_scr", 0, 16)
    o_kmc = p.add("dve", lambda q: q.tensor_copy(kmask[:],
                                                 mk.ps[7][:, 0:16]),
                  [o_km], tag="l0kmc")

    if until == "topk":
        return v_evs + [o_kmc]
    esc = [mk.bf(XR_C + 16640 + kt * 512, 512) for kt in range(16)]
    ctx = qf
    diagb = mk.f32(SM_C + 4096, 512)
    invrep = mk.f32(SM_C, 512, rows=64)
    deadrep = mk.f32(SM_C + 1024, 512, rows=64)
    mvcols = [mk.bias[0:64, mk.bias_off["l0_scr"] + 16 + i:
                      mk.bias_off["l0_scr"] + 17 + i] for i in range(2)]
    ctx_evs = []
    sbanks = [mk.ps[0], mk.ps[1], mk.ps[2], mk.ps[3]]
    lastexp = {}
    lastctx = [None, None]
    last_cmm = None
    prev_cev = None
    lastrep = None
    last_dgexp = [None]
    o_mvc = None
    pend = [None]
    state = dict(prev_cev=None, lastrep=None, o_mvc=None)

    def flush0(pe_):
        if pe_ is None:
            return
        h, qb, cb, r0, o_cmm, o_inv, mvcol_h = pe_
        def repf(q):
            q.matmul(mk.ps[6][0:64, 0:512], mk.ones_row_f[0:1, 0:64],
                     mk.rows[0:1, 0:512], start=True, stop=True)
            return q.matmul(mk.ps[7][0:64, 0:512],
                            mk.ones_row_f[0:1, 0:64],
                            mk.rows2[0:1, 0:512], start=True, stop=True)
        o_rep = p.add("pe", repf,
                      [o_inv, state["prev_cev"], state["o_mvc"]],
                      tag="l0rep")

        def repc(q):
            q.activation(invrep[:], mk.ps[6][0:64, 0:512], AF.Copy)
            return q.activation(deadrep[:], mk.ps[7][0:64, 0:512],
                                AF.Copy)
        o_repc = p.add("act", repc, [o_rep], tag="l0repc")
        state["lastrep"] = o_repc

        def cev(q, h=h, qb=qb, cb=cb, r0=r0, mvcol_h=mvcol_h):
            q.tensor_tensor(invrep[:], invrep[:], cb[0:64, 0:512],
                            op=ALU.mult)
            return q.scalar_tensor_tensor(
                ctx[h // 2][r0:r0 + 64, qb * 512:(qb + 1) * 512],
                deadrep[:], mvcol_h, invrep[:],
                op0=ALU.mult, op1=ALU.add)
        o_cev = p.add("dve", cev, [o_repc, o_cmm], tag="l0cev")
        lastctx[qb % 2] = o_cev
        state["prev_cev"] = o_cev
        ctx_evs.append(o_cev)

    for h in range(H):
        r0 = (h % 2) * 64
        mvcol = mvcols[h % 2]

        def mvf(q, h=h):
            ii = None
            for tt in range(16):
                ii = q.matmul(mk.ps[6][0:65, 0:1],
                              vext[tt][:, h * 65:h * 65 + 65],
                              mk.ones_bf[:, 0:1], start=(tt == 0),
                              stop=(tt == 15))
            return ii
        o_mv = p.add("pe", mvf, [v_evs[-1], state["o_mvc"],
                                 state["lastrep"]], tag="l0mv")
        o_mvc = p.add("act", lambda q, mvcol=mvcol: q.activation(
            mvcol, mk.ps[6][0:64, 0:1], AF.Identity, scale=1.0 / 2048.0),
            [o_mv, state["prev_cev"]], tag="l0mvc")
        state["o_mvc"] = o_mvc
        for qb in range(NCH):
            nkt = qb * 4 + 4
            exp_ops = []
            for kt in range(nkt):
                b = sbanks[kt % 4]
                def smm(q, bank, h=h, qb=qb, kt=kt, r0=r0):
                    return q.matmul(
                        bank[:, 0:512],
                        kf[h // 2][r0:r0 + 64, kt * 128:(kt + 1) * 128],
                        qf[h // 2][r0:r0 + 64, qb * 512:(qb + 1) * 512],
                        start=True, stop=True, tile_position=(r0, 0))
                dd = [v_evs[-1], lastexp.get(kt % 4), last_cmm, o_kmc]
                o_smm = p.add("pe", (lambda q, f=smm, bb=b: f(q, bb)), dd,
                              tag="l0smm")
                if kt >= qb * 4:
                    o_dg = p.add(
                        "dve", lambda q, bb=b, o=kt - qb * 4:
                        q.tensor_tensor(diagb[:], bb[:, 0:512],
                                        trilb[:, o * 512:(o + 1) * 512],
                                        op=ALU.add),
                        [o_smm, o_tril, last_dgexp[0]], tag="l0diag")
                    def sexp(q, kt=kt):
                        return q.activation(esc[kt][:], diagb[:], AF.Exp,
                                            bias=kmask[:, kt:kt + 1],
                                            scale=0.125)
                    o_exp = p.add("act", sexp, [o_dg], tag="l0exp")
                    last_dgexp[0] = o_exp
                else:
                    def sexp(q, bank, kt=kt):
                        return q.activation(esc[kt][:], bank[:, 0:512],
                                            AF.Exp, bias=kmask[:, kt:kt + 1],
                                            scale=0.125)
                    o_exp = p.add("act", (lambda q, f=sexp, bb=b: f(q, bb)),
                                  [o_smm], tag="l0exp")
                lastexp[kt % 4] = o_exp
                exp_ops.append(o_exp)
            flush0(pend[0])
            pend[0] = None
            cb = mk.ps[4 + (qb % 2)]
            o_cmm = None
            for kt in range(nkt):
                def cmm1(q, h=h, cb=cb, kt=kt, nkt=nkt):
                    return q.matmul(cb[0:65, 0:512],
                                    vext[kt][:, h * 65:h * 65 + 65],
                                    esc[kt][:], start=(kt == 0),
                                    stop=(kt == nkt - 1),
                                    skip_group_check=True)
                dd = [exp_ops[kt]] + ([lastctx[qb % 2], v_evs[-1]]
                                      if kt == 0 else [])
                o_cmm = p.add("pe", cmm1, dd, tag="l0cmm")
            last_cmm = o_cmm

            def invf(q, cb=cb):
                q.tensor_scalar(mk.rows2[0:1, 0:512], cb[64:65, 0:512],
                                0.0, None, op0=ALU.is_equal)
                q.tensor_tensor(mk.rows[0:1, 0:512], cb[64:65, 0:512],
                                mk.rows2[0:1, 0:512], op=ALU.add)
                return q.reciprocal(mk.rows[0:1, 0:512],
                                    mk.rows[0:1, 0:512])
            o_inv = p.add("dve", invf, [o_cmm, state["lastrep"]],
                          tag="l0inv")
            pend[0] = (h, qb, cb, r0, o_cmm, o_inv, mvcol)
    flush0(pend[0])
    pend[0] = None

    return ctx_evs


def e_l0_outproj(mk, W, deps):
    p = mk.p
    XR = xr_tiles(mk)
    ctx = reg_tiles(mk, R1_C)

    def evo(q, bank, mt, nch):
        return q.tensor_tensor(XR[mt][:, nch * 512:(nch + 1) * 512],
                               XR[mt][:, nch * 512:(nch + 1) * 512],
                               bank[:, 0:512], op=ALU.add)
    grid_o, evs_o = e_linear_fm(mk, mk.ws, W["l0_wout_t"], D, D, ctx, evo,
                                deps, [mk.ps[0], mk.ps[1], mk.ps[2],
                                       mk.ps[3]],
                                evac_q="dve", tag="l0out")
    return evs_o


def _tern_strips(wt_ap, K, M):
    out = []
    for kt in range(K // 128):
        for half in range(0, M, 2048):
            ncols = min(2048, M - half)
            out.append((kt, half, ncols,
                        wt_ap[kt * 128:(kt + 1) * 128, half:half + ncols]))
    return out


def _interleave(a, b):
    out, ia, ib = [], 0, 0
    na, nb = len(a), len(b)
    while ia < na or ib < nb:
        if ib * na <= ia * nb and ib < nb:
            out.append(("b", b[ib])); ib += 1
        else:
            out.append(("a", a[ia])); ia += 1
    return out


def e_tern_delta2(mk, W, deps, d_up, d_dn, scale):
    p = mk.p
    ws = mk.ws
    cfg = {
        "a": dict(absb=[mk.f32(R2_C + 4096 + i * 1024, 512)
                        for i in range(2)],
                  psA=mk.ps[6], psR=mk.ps[7], dsum=mk.rows[0:1, 0:1],
                  dcol2=d_up, mm_ops=[], k=0,
                  npieces=(D // 128) * (4096 // 512)),
        "b": dict(absb=[mk.f32(R2_C + 6144 + i * 1024, 512)
                        for i in range(2)],
                  psA=mk.ps[4], psR=mk.ps[5], dsum=mk.rows2[0:1, 0:1],
                  dcol2=d_dn, mm_ops=[], k=0,
                  npieces=(4096 // 128) * (D // 512)),
    }
    merged = _interleave(_tern_strips(W["l0_wup_t"], D, 4096),
                         _tern_strips(W["l0_wdn_t"], 4096, D))
    first = True
    for which, (kt, half, ncols, src) in merged:
        c = cfg[which]
        o_ld, strip, slot = ws.load_raw(src, ncols, deps=())
        for j in range(ncols // 512):
            buf = c["absb"][c["k"] % 2]
            dd = [o_ld, c["mm_ops"][-2] if len(c["mm_ops"]) >= 2 else None]
            if first:
                dd += list(deps)
                first = False
            o_abs = p.add("act", lambda q, b=buf, s_=strip, j=j:
                          q.activation(b[:], s_[:, j * 512:(j + 1) * 512],
                                       AF.Abs), dd, tag="tdabs")
            o_mm = p.add("pe", lambda q, b=buf, c=c,
                         first_=(c["k"] == 0),
                         last=(c["k"] == c["npieces"] - 1):
                         q.matmul(c["psA"][0:1, 0:512], mk.ones_f[:, 0:1],
                                  b[:], start=first_, stop=last),
                         [o_abs] + (list(deps) if c["k"] == 0 else []),
                         tag="tdmm")
            c["mm_ops"].append(o_mm)
            c["k"] += 1
        ws.release(slot, c["mm_ops"][-1])
    outs = []
    for which in ("a", "b"):
        c = cfg[which]
        o_red = p.add("dve", lambda q, c=c: q.tensor_reduce(
            c["dsum"], c["psA"][0:1, 0:512], axis=mybir.AxisListType.X,
            op=ALU.add), [c["mm_ops"][-1]] + list(deps), tag="tdred")
        o_drep = p.add("pe", lambda q, c=c: q.matmul(
            c["psR"][:, 0:1], mk.ones_row_f[:], c["dsum"],
            start=True, stop=True), [o_red], tag="tdrep")

        def dcols(q, c=c):
            q.tensor_scalar(c["dcol2"][:, 0:1], c["psR"][:, 0:1], scale,
                            None, op0=ALU.mult)
            return q.tensor_scalar(c["dcol2"][:, 1:2], c["psR"][:, 0:1],
                                   -scale, None, op0=ALU.mult)
        outs.append(p.add("dve", dcols, [o_drep], tag="tdcol"))
    return outs


def e_tern_write2(mk, W, d_up, d_dn, scr_up, scr_dn, deps):
    p = mk.p
    ws = mk.ws
    cfg = {
        "a": dict(tern=[mk.bf(R2_C + i * 512, 512) for i in range(2)],
                  prev_t=[None, None], dcol2=d_up, scr=scr_up, tiled=False,
                  st=[], q="dve", gpos=mk.f32(R2_C + 2048, 512),
                  gneg=mk.f32(R2_C + 3072, 512)),
        "b": dict(tern=[mk.bf(R2_C + 1024 + i * 512, 512)
                        for i in range(2)],
                  prev_t=[None, None], dcol2=d_dn, scr=scr_dn, tiled=True,
                  st=[], q="pool", gpos=mk.f32(R2_C + 8192, 512),
                  gneg=mk.f32(R2_C + 9216, 512)),
    }
    merged = _interleave(_tern_strips(W["l0_wup_t"], D, 4096),
                         _tern_strips(W["l0_wdn_t"], 4096, D))
    for which, (kt, half, ncols, src) in merged:
        c = cfg[which]
        o_ld, strip, slot = ws.load_raw(src, ncols, deps=())
        last = None
        for j in range(ncols // 512):
            tb = c["tern"][j % 2]
            def tf(q, s_=strip, j=j, tb=tb, c=c):
                srcv = s_[:, j * 512:(j + 1) * 512]
                q.tensor_scalar(c["gpos"][:], srcv, c["dcol2"][:, 0:1],
                                None, op0=ALU.is_gt)
                q.tensor_scalar(c["gneg"][:], srcv, c["dcol2"][:, 1:2],
                                None, op0=ALU.is_lt)
                return q.tensor_tensor(tb[:], c["gpos"][:], c["gneg"][:],
                                       op=ALU.subtract)
            o_t = p.add(c["q"], tf, [o_ld, c["prev_t"][j % 2]] + list(deps),
                        tag="twt")
            m0 = half + j * 512
            if not c["tiled"]:
                o_st = p.dma("sp", c["scr"][kt * 128:(kt + 1) * 128,
                                            m0:m0 + 512], tb[:],
                             deps=[o_t], tag="twst")
            else:
                dst = c["scr"][m0 // 128:m0 // 128 + 4, :,
                               kt * 128:(kt + 1) * 128]
                o_st = p.dma("sp", dst.rearrange("a pp m -> pp a m"),
                             tb[:].rearrange("pp (a m) -> pp a m", a=4),
                             deps=[o_t], tag="twst")
            c["prev_t"][j % 2] = o_st
            c["st"].append(o_st)
            last = o_st
        ws.release(slot, last)
    return cfg["a"]["st"], cfg["b"]["st"]


def _tern_dcols(mk):
    d_up = mk.bias[:, mk.bias_off["l0_scr"] + 20:mk.bias_off["l0_scr"] + 22]
    d_dn = mk.bias[:, mk.bias_off["l0_scr"] + 22:mk.bias_off["l0_scr"] + 24]
    return d_up, d_dn


def e_l0_tern_delta_phase(mk, W, deps):
    d_up, d_dn = _tern_dcols(mk)
    o_d1, o_d2 = e_tern_delta2(mk, W, deps, d_up, d_dn, 0.5 / (D * 4096))
    mk.scr_ready["l0delta"] = [o_d1, o_d2]
    return [o_d1, o_d2]


def e_l0_tern_write_phase(mk, W, scr_up, scr_dn):
    d_up, d_dn = _tern_dcols(mk)
    st1, st2 = e_tern_write2(mk, W, d_up, d_dn, scr_up, scr_dn,
                             mk.scr_ready["l0delta"])
    mk.scr_ready["l0up"] = st1[-1:]
    mk.scr_ready["l0dn"] = st2[-1:]
    return st1[-1:] + st2[-1:]


def e_l0_ffn(mk, W, deps, scr_up, scr_dn):
    p = mk.p
    ws = mk.ws
    XR = xr_tiles(mk)
    xnorm_ops = list(deps)
    xnorm = reg_tiles(mk, R0_C)
    hbuf = [mk.bf(R1_C + ft * 512, 512) for ft in range(32)]
    hnrm = [mk.bf(R2_C + ft * 512, 512) for ft in range(32)]
    tails = xnorm_ops
    out_tails = []
    s2rep = mk.f32(SM_C + 4096, 512)
    sqb = mk.f32(SM_C + 5120, 512)
    for tb in range(NCH):
        def ev_h(q, bank, mt, nch):
            return q.activation(hbuf[mt][:], bank[:, 0:512], AF.Gelu)
        grid, evs = e_linear_fm(mk, ws, scr_up, D, 4096, xnorm, ev_h,
                                tails, [mk.ps[0], mk.ps[1], mk.ps[4],
                                        mk.ps[5]], mcols=512,
                                tag="l0h", nch_range=[tb], src_bf16=True,
                                w_deps=mk.scr_ready.get("l0up", ()))
        sq_ops = []
        psS = mk.ps[6]
        for ft in range(32):
            o_sq = p.add("dve", lambda q, ft=ft: q.tensor_tensor(
                sqb[:], hbuf[ft][:], hbuf[ft][:], op=ALU.mult),
                [grid[ft][tb], sq_ops[-1] if sq_ops else None], tag="l0fsq")
            o_mm = p.add("pe", lambda q, ft=ft, first=(ft == 0),
                         last=(ft == 31): q.matmul(
                psS[:, 0:512], mk.ones_f[:], sqb[:], start=first,
                stop=last), [o_sq], tag="l0fsm")
            sq_ops.append(o_mm)

        def s2f(q):
            q.tensor_scalar(s2rep[:], psS[:, 0:512], 1.0 / 4096.0, None,
                            op0=ALU.mult)
            return s2rep
        o_s2a = p.add("dve", lambda q: q.tensor_scalar(
            s2rep[:], psS[:, 0:512], 1.0 / 4096.0, None, op0=ALU.mult),
            [sq_ops[-1]], tag="l0s2a")
        o_s2b = p.add("act", lambda q: q.activation(
            s2rep[:], s2rep[:], AF.Sqrt), [o_s2a], tag="l0s2b")

        def s2c(q):
            q.tensor_scalar(s2rep[:], s2rep[:], 1e-8, None, op0=ALU.add)
            return q.reciprocal(s2rep[:], s2rep[:])
        o_s2c = p.add("dve", s2c, [o_s2b], tag="l0s2c")
        hn_ops = []
        sts_dep = list(mk.scr_ready.get("l0up", ())) + \
            list(mk.scr_ready.get("l0dn", ()))
        for ft in range(32):
            hn_ops.append(p.add("dve", lambda q, ft=ft: q.tensor_tensor(
                hnrm[ft][:], hbuf[ft][:], s2rep[:], op=ALU.mult),
                [o_s2c] + sts_dep, tag="l0hn"))
        down_evs = []
        lastb = {id(mk.ps[2]): None, id(mk.ps[3]): None}
        for mt in range(DT):
            dstsrc = scr_dn[mt]
            subs = ws._alloc(4096)
            c0 = WS_C + subs[0] * 2048
            dst = mk.bf(c0, 4096)
            o_ld = p.dma("sp", dst, dstsrc,
                         deps=list(mk.scr_ready.get("l0dn", ())) +
                         [ws.free_after[i] for i in subs],
                         sem="WS%d" % subs[0], tag="l0dld")
            bank = mk.ps[2 + mt % 2]
            def mms(q, bank=bank, dst=dst):
                ii = None
                for kk in range(32):
                    ii = q.matmul(bank[:, 0:512],
                                  dst[:, kk * 128:(kk + 1) * 128],
                                  hnrm[kk][:], start=(kk == 0),
                                  stop=(kk == 31))
                return ii
            o_mm = p.add("pe", mms, [o_ld, lastb[id(bank)]] + hn_ops[-1:],
                         tag="l0dmm")
            def evd(q, bank=bank, mt=mt, tb=tb):
                return q.tensor_tensor(
                    XR[mt][:, tb * 512:(tb + 1) * 512],
                    XR[mt][:, tb * 512:(tb + 1) * 512],
                    bank[:, 0:512], op=ALU.add)
            o_ev = p.add("dve", evd, [o_mm], tag="l0dev")
            lastb[id(bank)] = o_ev
            ws.release(subs, o_mm)
            down_evs.append(o_ev)
        tails = down_evs[-2:]
        out_tails = down_evs
    return out_tails


def e_layer0(mk, W, deps, stage="all"):
    p = mk.p
    XR = xr_tiles(mk)
    scr_xr = mk.scr["xr"]
    ln_ops = e_layernorm(mk, deps)
    if stage in ("lat", "q", "k", "v0", "v1", "qkv", "topk"):
        sub = e_l0_attn(mk, W, ln_ops[-8:], scr_xr, until=stage)
        return [p.dma("sp", XR[dt], scr_xr[dt * 128:(dt + 1) * 128, :],
                      deps=sub, tag="l0rest") for dt in range(DT)]
    ctx_evs = e_l0_attn(mk, W, ln_ops[-8:], scr_xr)
    if stage == "attn":
        rs = [p.dma("sp", XR[dt], scr_xr[dt * 128:(dt + 1) * 128, :],
                    deps=ctx_evs, tag="l0rest") for dt in range(DT)]
        return e_l0_outproj(mk, W, rs + ctx_evs)
    rs = [p.dma("sp", XR[dt], scr_xr[dt * 128:(dt + 1) * 128, :],
                deps=ctx_evs, tag="l0rest") for dt in range(DT)]
    xnorm_ops = e_ln_l0ffn(mk, rs)
    if not mk.tern_built:
        e_l0_tern_delta_phase(mk, W, xnorm_ops[-2:] + ctx_evs[-1:])
    op_evs = e_l0_outproj(mk, W, xnorm_ops[-8:] + ctx_evs)
    if not mk.tern_built:
        e_l0_tern_write_phase(mk, W, mk.scr["l0up"], mk.scr["l0dn"])
        mk.tern_built = True
    return e_l0_ffn(mk, W, op_evs[-4:] + xnorm_ops[-8:],
                    mk.scr["l0up"], mk.scr["l0dn"])



def e_l1_attn(mk, W, deps, scr_xr):
    p = mk.p
    XR = xr_tiles(mk)
    emb = reg_tiles(mk, R0_C)
    ws = mk.ws
    sp_ops = [p.dma("sp", scr_xr[dt * 128:(dt + 1) * 128, :], XR[dt],
                    deps=deps, tag="l1spill") for dt in range(DT)]
    acc = [mk.f32(XR_C + mt * 4096, 2048) for mt in range(DT)]
    z_ops = [p.add("dve", lambda q, m=mt: q.memset(acc[m][:], 0.0),
                   sp_ops, tag="l1accz") for mt in range(DT)]

    Dx = reg_tiles(mk, R1_C)

    def evd(q, bank, mt, nch):
        return q.activation(Dx[mt][:, nch * 512:(nch + 1) * 512],
                            bank[:, 0:512], AF.Identity,
                            bias=mk.bcol("l1a_db", mt))
    grid_d, evs_d = e_linear_fm(mk, ws, W["l1a_d_t"], D, D, emb, evd,
                                deps, [mk.ps[0], mk.ps[1], mk.ps[2], mk.ps[3]], tag="l1dx")

    hs = reg_tiles(mk, R2_C)
    a_t = mk.bf(SM_C, 2048)
    b_t = mk.bf(SM_C + 2048, 2048)
    out_c = [mk.bf(SM_C + mt * 512, 512) for mt in range(DT)]
    sqtmp = mk.f32(SM_C + 4096, 512)
    mrep = mk.f32(SM_C + 5120, 512)
    gcol = mk.bcol("l1a_scr", 0, 8)
    ngcol = mk.bcol("l1a_scr", 8, 8)
    wbacc = mk.bcol("l1a_scr", 16, 8)
    o_wbz = p.add("dve", lambda q: q.memset(wbacc[:], 0.0), deps,
                  tag="l1wbz")

    hw_off = mk.bias_off["l1a_headw"]
    last_scan = None
    last_accu = [None] * DT
    last_stats = None
    prev_cslabs = None
    for h in range(8):
        def gfn(q, h=h):
            q.tensor_scalar(gcol[:], mk.bcol("l1a_hng", h * 8, 8),
                            mk.bias[:, hw_off + h:hw_off + h + 1], None,
                            op0=ALU.mult)
            mk.dve_gap(q)
            q.tensor_scalar(ngcol[:], gcol[:], -1.0, None, op0=ALU.mult)
            return q.scalar_tensor_tensor(
                wbacc[:], mk.bcol("l1a_hnb", h * 8, 8),
                mk.bias[:, hw_off + h:hw_off + h + 1], wbacc[:],
                op0=ALU.mult, op1=ALU.add)
        o_g = p.add("dve", gfn, [o_wbz, last_stats], tag="l1g")

        scan_ops = []
        for dt in range(DT):
            o_la, sla, slota = ws.load(W["l1a_wa_t"][h], dt * 128, 128, D,
                                       deps=())
            o_lb, slb, slotb = ws.load(W["l1a_wb_t"][h], dt * 128, 128, D,
                                       deps=())
            ab_evs = []
            for which, (o_l, sl) in (("a", (o_la, sla)), ("b", (o_lb, slb))):
                jobs = []
                for nch in range(NCH):
                    def mms(q, bank, sl=sl, n=nch):
                        ii = None
                        for kk in range(DT):
                            ii = q.matmul(bank[:, 0:512],
                                          sl[:, kk * 128:(kk + 1) * 128],
                                          emb[kk][:, n * 512:(n + 1) * 512],
                                          start=(kk == 0), stop=(kk == DT - 1))
                        return ii
                    if which == "a":
                        def ev(q, bank, n=nch, h=h, dt=dt):
                            return q.activation(
                                a_t[:, n * 512:(n + 1) * 512], bank[:, 0:512],
                                AF.Tanh,
                                bias=mk.bcol("l1a_wab", h * 8 + dt))
                    else:
                        def ev(q, bank, n=nch, h=h, dt=dt):
                            return q.activation(
                                b_t[:, n * 512:(n + 1) * 512], bank[:, 0:512],
                                AF.Identity,
                                bias=mk.bcol("l1a_wbb", h * 8 + dt))
                    jobs.append(dict(mms=mms, deps=[o_l],
                                     evac_deps=[last_scan, last_stats],
                                     evac=ev, evac_q="act"))
                ab_evs += mm_jobs(mk, jobs,
                                  [mk.ps[0], mk.ps[1], mk.ps[2], mk.ps[3]],
                                  tag="l1ab")
            o_scan = p.add("dve", lambda q, dt=dt: q.tensor_tensor_scan(
                hs[dt][:], a_t[:], b_t[:], 0.0, op0=ALU.mult, op1=ALU.add),
                ab_evs, tag="l1scan")
            scan_ops.append(o_scan)
            last_scan = o_scan
            ws.release(slota, o_scan)
            ws.release(slotb, o_scan)

        o_c1, slc1, slotc1 = ws.load(W["l1a_c_t"][h], 0, 512, D,
                                     deps=())
        o_c2, slc2, slotc2 = ws.load(W["l1a_c_t"][h], 512, 512, D,
                                     deps=())
        cslabs = [slc1, slc2]
        head_tail = None
        for tc in range(NCH):
            evs_c = []
            jobs = []
            for mt in range(DT):
                def mms(q, bank, mt=mt, tc=tc):
                    sl = cslabs[mt // 4]
                    mi = mt % 4
                    ii = None
                    for kk in range(DT):
                        ii = q.matmul(bank[:, 0:512],
                                      sl[:, kk * 512 + mi * 128:
                                         kk * 512 + mi * 128 + 128],
                                      hs[kk][:, tc * 512:(tc + 1) * 512],
                                      start=(kk == 0), stop=(kk == DT - 1))
                    return ii

                def ev(q, bank, mt=mt, tc=tc, h=h):
                    return q.scalar_tensor_tensor(
                        out_c[mt][:], bank[:, 0:512],
                        mk.bcol("l1a_cb", h * 8 + mt),
                        Dx[mt][:, tc * 512:(tc + 1) * 512],
                        op0=ALU.add, op1=ALU.mult)
                jobs.append(dict(mms=mms,
                                 deps=[o_c1, o_c2, o_g, scan_ops[-1]],
                                 evac_deps=[last_accu[mt]],
                                 evac=ev, evac_q="dve"))
            evs_c = mm_jobs(mk, jobs,
                            [mk.ps[0], mk.ps[1], mk.ps[2], mk.ps[3]],
                            tag="l1c")
            st_mm = []
            prev_acc = last_accu[DT - 1]
            for mt in range(DT):
                o_sq = p.add("act", lambda q, m=mt: q.activation(
                    sqtmp[:], out_c[m][:], AF.Square),
                    [evs_c[mt], st_mm[-1] if st_mm else last_stats,
                     prev_acc],
                    tag="l1sq")
                def smm(q, m=mt, first=(mt == 0), lastf=(mt == DT - 1)):
                    q.matmul(mk.ps[6][:, 0:512], mk.ones_bf[:], out_c[m][:],
                             start=first, stop=lastf)
                    return q.matmul(mk.ps[7][:, 0:512], mk.ones_f[:],
                                    sqtmp[:], start=first, stop=lastf)
                o_mm = p.add("pe", smm, [o_sq, evs_c[mt]], tag="l1smm")
                st_mm.append(o_mm)

            def stf(q):
                q.tensor_scalar(mrep[:], mk.ps[6][:, 0:512], 1.0 / D, None,
                                op0=ALU.mult)
                q.tensor_tensor(sqtmp[:], mrep[:], mrep[:], op=ALU.mult)
                q.scalar_tensor_tensor(sqtmp[:], mk.ps[7][:, 0:512], 1.0 / D,
                                       sqtmp[:], op0=ALU.mult,
                                       op1=ALU.subtract)
                q.tensor_scalar(sqtmp[:], sqtmp[:], 1e-5, None, op0=ALU.add)
                return q.reciprocal(sqtmp[:], sqtmp[:])
            o_st = p.add("dve", stf, [st_mm[-1], prev_acc], tag="l1st")
            o_rs = p.add("act", lambda q: q.activation(
                sqtmp[:], sqtmp[:], AF.Sqrt), [o_st], tag="l1rs")
            o_mr = p.add("dve", lambda q: q.tensor_tensor(
                mrep[:], mrep[:], sqtmp[:], op=ALU.mult), [o_rs],
                tag="l1mr")
            for mt in range(DT):
                def accf(q, m=mt, tc=tc):
                    q.tensor_tensor(out_c[m][:], out_c[m][:], sqtmp[:],
                                    op=ALU.mult)
                    q.scalar_tensor_tensor(
                        acc[m][:, tc * 512:(tc + 1) * 512], out_c[m][:],
                        gcol[:, m:m + 1],
                        acc[m][:, tc * 512:(tc + 1) * 512],
                        op0=ALU.mult, op1=ALU.add)
                    return q.scalar_tensor_tensor(
                        acc[m][:, tc * 512:(tc + 1) * 512], mrep[:],
                        ngcol[:, m:m + 1],
                        acc[m][:, tc * 512:(tc + 1) * 512],
                        op0=ALU.mult, op1=ALU.add)
                last_accu[mt] = p.add("dve", accf,
                                      [o_mr, z_ops[mt], evs_c[mt]],
                                      tag="l1acc")
            last_stats = last_accu[DT - 1]
            head_tail = last_accu[DT - 1]
        ws.release(slotc1, head_tail)
        ws.release(slotc2, head_tail)
        prev_cslabs = head_tail

    acc_tail = [o for o in last_accu]
    sh = reg_tiles(mk, R2_C)
    sh_ops = []
    for mt in range(DT):
        def shf(q, m=mt):
            q.tensor_scalar(acc[m][:], acc[m][:], 1.0, wbacc[:, m:m + 1],
                            op0=ALU.mult, op1=ALU.add)
            q.tensor_scalar(acc[m][:], acc[m][:], 0.125, None, op0=ALU.mult)
            for nch in range(NCH):
                ii = q.scalar_tensor_tensor(
                    sh[m][:, nch * 512:(nch + 1) * 512],
                    Dx[m][:, nch * 512:(nch + 1) * 512], 0.125,
                    acc[m][:, nch * 512:(nch + 1) * 512],
                    op0=ALU.mult, op1=ALU.add)
            return ii
        sh_ops.append(p.add("dve", shf, acc_tail + [last_scan], tag="l1sh"))

    zt = acc

    def evz(q, bank, mt, nch):
        return q.activation(zt[mt][:, nch * 512:(nch + 1) * 512],
                            bank[:, 0:512], AF.Identity,
                            bias=mk.bcol("l1a_woutb", mt))
    grid_z, evs_z = e_linear_fm(mk, ws, W["l1a_wout_t"], D, D, sh, evz,
                                sh_ops, [mk.ps[0], mk.ps[1], mk.ps[2], mk.ps[3]], tag="l1z")

    ubf = Dx
    u_ops = []
    prevst = None
    for nch in range(NCH):
        st_mm = []
        for mt in range(DT):
            o_sq = p.add("act", lambda q, m=mt, n=nch: q.activation(
                sqtmp[:], zt[m][:, n * 512:(n + 1) * 512], AF.Square),
                [grid_z[mt][nch], st_mm[-1] if st_mm else prevst],
                tag="l1zsq")
            def smm(q, m=mt, n=nch, first=(mt == 0), lastf=(mt == DT - 1)):
                q.matmul(mk.ps[6][:, 0:512], mk.ones_f[:],
                         zt[m][:, n * 512:(n + 1) * 512],
                         start=first, stop=lastf)
                return q.matmul(mk.ps[7][:, 0:512], mk.ones_f[:], sqtmp[:],
                                start=first, stop=lastf)
            o_mm = p.add("pe", smm, [o_sq, grid_z[mt][nch]], tag="l1zmm")
            st_mm.append(o_mm)

        def stf(q):
            q.tensor_scalar(mrep[:], mk.ps[6][:, 0:512], 1.0 / D, None,
                            op0=ALU.mult)
            q.tensor_tensor(sqtmp[:], mrep[:], mrep[:], op=ALU.mult)
            q.scalar_tensor_tensor(sqtmp[:], mk.ps[7][:, 0:512], 1.0 / D,
                                   sqtmp[:], op0=ALU.mult, op1=ALU.subtract)
            q.tensor_scalar(sqtmp[:], sqtmp[:], 1e-5, None, op0=ALU.add)
            return q.reciprocal(sqtmp[:], sqtmp[:])
        o_st = p.add("dve", stf, [st_mm[-1]], tag="l1zst")
        o_rs = p.add("act", lambda q: q.activation(
            sqtmp[:], sqtmp[:], AF.Sqrt), [o_st], tag="l1zrs")
        tmp2 = mk.bf(SM_C, 1024).bitcast(F32)
        for mt in range(DT):
            def uf(q, m=mt, n=nch):
                q.tensor_tensor(tmp2[:], zt[m][:, n * 512:(n + 1) * 512],
                                mrep[:], op=ALU.subtract)
                q.tensor_tensor(tmp2[:], tmp2[:], sqtmp[:], op=ALU.mult)
                q.scalar_tensor_tensor(
                    zt[m][:, n * 512:(n + 1) * 512], tmp2[:],
                    mk.bcol("l1a_ng", m),
                    zt[m][:, n * 512:(n + 1) * 512],
                    op0=ALU.mult, op1=ALU.add)
                return q.tensor_scalar(
                    ubf[m][:, n * 512:(n + 1) * 512],
                    zt[m][:, n * 512:(n + 1) * 512], 1.0,
                    mk.bcol("l1a_nb", m), op0=ALU.mult, op1=ALU.add)
            u_ops.append(p.add("dve", uf, [o_rs], tag="l1u"))
        prevst = u_ops[-1]

    hbuf = [mk.bf(R2_C + ft * 512, 512) for ft in range(32)]
    hout_ops = []
    tails = u_ops[-8:]
    for tb in range(NCH):
        def ev_h(q, bank, mt, nch):
            return q.activation(hbuf[mt][:], bank[:, 0:512], AF.Gelu,
                                bias=mk.bcol("l1a_f1b", mt))
        grid_h, evs_h = e_linear_fm(mk, ws, W["l1a_ffn1_t"], D, 4096, ubf,
                                    ev_h, tails, [mk.ps[0], mk.ps[1],
                                                  mk.ps[4]],
                                    tag="l1f1", nch_range=[tb])
        tmp3 = mk.f32(SM_C + 4096, 512)
        evs2 = []
        f2state = {}
        for s0 in range(8):
            o_ld, slab, slot = ws.load(W["l1a_ffn2_t"], s0 * 128, 128, 4096,
                                       deps=())

            def mms(q, bank, slab=slab):
                ii = None
                for kk in range(32):
                    ii = q.matmul(bank[:, 0:512],
                                  slab[:, kk * 128:(kk + 1) * 128],
                                  hbuf[kk][:], start=(kk == 0),
                                  stop=(kk == 31))
                return ii

            def ev2(q, bank, mt=s0, tb=tb):
                q.tensor_scalar(tmp3[:], bank[:, 0:512],
                                mk.bcol("l1a_f2b", mt),
                                mk.bcol("l1a_nb", mt), op0=ALU.add,
                                op1=ALU.add)
                return q.tensor_tensor(
                    ubf[mt][:, tb * 512:(tb + 1) * 512], tmp3[:],
                    zt[mt][:, tb * 512:(tb + 1) * 512], op=ALU.add)
            sevs = mm_jobs(mk, [dict(mms=mms, deps=[o_ld] + list(evs_h[-2:]),
                                     evac=ev2, evac_q="dve")],
                           [mk.ps[2], mk.ps[3], mk.ps[5]], tag="l1f2",
                           state=f2state)
            ws.release(slot, sevs[-1])
            evs2 += sevs
        hout_ops += evs2
        tails = evs2[-2:]

    rs_ops = [p.dma("sp", XR[dt], scr_xr[dt * 128:(dt + 1) * 128, :],
                    deps=hout_ops, tag="l1rest") for dt in range(DT)]
    tmp4 = mk.f32(SM_C + 4096, 512)

    def evp(q, bank, mt, nch):
        q.tensor_scalar(tmp4[:], bank[:, 0:512], mk.bcol("l1a_pb", mt),
                        None, op0=ALU.add)
        return q.tensor_tensor(XR[mt][:, nch * 512:(nch + 1) * 512],
                               XR[mt][:, nch * 512:(nch + 1) * 512],
                               tmp4[:], op=ALU.add)
    grid_p, evs_p = e_linear_fm(mk, ws, W["l1a_proj_t"], D, D, ubf, evp,
                                rs_ops + hout_ops, [mk.ps[0], mk.ps[1], mk.ps[2], mk.ps[3]],
                                evac_q="dve", tag="l1p")
    return evs_p



def e_linear_fm2(mk, ws, wt_ap, K, M, rhs, evac, deps, banks, mcols=512,
                 evac_q="act", tag="lin", nch_range=None, mt_range=None,
                 src_bf16=False, rhs_coloff=0, w_deps=()):
    kt = K // 128
    nmt = M // 128
    mps = mcols // 128
    if nch_range is None:
        nch_range = range(NCH)
    evs = []
    state = {}
    for s0 in range(0, nmt, mps):
        if mt_range is not None and not any(
                s0 <= mt < s0 + mps for mt in mt_range):
            continue
        o_ld, slab, slot = ws.load(wt_ap, s0 * 128, mcols, K, deps=w_deps,
                                   src_bf16=src_bf16)
        jobs = []
        for mi in range(mps):
            mt = s0 + mi
            if mt >= nmt or (mt_range is not None and mt not in mt_range):
                continue
            for nch in nch_range:
                def mms(q, bank, slab=slab, mi=mi, n=nch):
                    ii = None
                    for kk in range(kt):
                        ii = q.matmul(
                            bank[:, 0:512],
                            slab[:, kk * mcols + mi * 128:
                                 kk * mcols + mi * 128 + 128],
                            rhs[kk][:, n * 512 - rhs_coloff:
                                    (n + 1) * 512 - rhs_coloff],
                            start=(kk == 0), stop=(kk == kt - 1))
                    return ii
                jobs.append(dict(
                    mms=mms, deps=[o_ld] + list(deps),
                    evac=(lambda q, bank, mt=mt, n=nch: evac(q, bank, mt, n)),
                    evac_q=evac_q))
        sevs = mm_jobs(mk, jobs, banks, tag=tag, state=state)
        ws.release(slot, jobs[-1]["_mm"])
        evs += sevs
    return evs


def e_l1_ffn(mk, W, deps, scr_xr, scr_emb, scr_gate):
    p = mk.p
    XR = xr_tiles(mk)
    emb = reg_tiles(mk, R0_C)
    ws = mk.ws
    H, HD = 16, 64
    sqtmp = mk.f32(SM_C + 4096, 512)
    srep = mk.f32(SM_C + 5120, 512)

    h1 = reg_tiles(mk, R1_C)
    h1_ops = []
    prevc = None
    for nch in range(NCH):
        st_mm = []
        for mt in range(DT):
            o_sq = p.add("dve", lambda q, m=mt, n=nch: q.tensor_tensor(
                sqtmp[:], emb[m][:, n * 512:(n + 1) * 512],
                emb[m][:, n * 512:(n + 1) * 512], op=ALU.mult),
                list(deps) + [st_mm[-1] if st_mm else prevc], tag="l1fsq")
            o_mm = p.add("pe", lambda q, first=(mt == 0),
                         lastf=(mt == DT - 1): q.matmul(
                mk.ps[6][:, 0:512], mk.ones_f[:], sqtmp[:],
                start=first, stop=lastf), [o_sq], tag="l1fsm")
            st_mm.append(o_mm)

        def stf(q):
            q.tensor_scalar(srep[:], mk.ps[6][:, 0:512], 1.0 / D,
                            1.1920929e-07, op0=ALU.mult, op1=ALU.add)
            return q.reciprocal(srep[:], srep[:])
        o_st = p.add("dve", stf, [st_mm[-1]], tag="l1fst")
        o_rs = p.add("act", lambda q: q.activation(
            srep[:], srep[:], AF.Sqrt), [o_st], tag="l1frs")
        for mt in range(DT):
            def nf(q, m=mt, n=nch):
                q.tensor_tensor(sqtmp[:], emb[m][:, n * 512:(n + 1) * 512],
                                srep[:], op=ALU.mult)
                return q.tensor_scalar(h1[m][:, n * 512:(n + 1) * 512],
                                       sqtmp[:], mk.bcol("l1f_n1", m), None,
                                       op0=ALU.mult)
            h1_ops.append(p.add("dve", nf, [o_rs], tag="l1fn"))
        prevc = h1_ops[-1]

    sp_ops = [p.dma("sp", scr_xr[dt * 128:(dt + 1) * 128, :], XR[dt],
                    deps=list(deps), tag="l1fspill") for dt in range(DT)]
    em_ops = [p.dma("sp", scr_emb[dt * 128:(dt + 1) * 128, :],
                    emb[dt][:], deps=h1_ops, tag="l1fembsp")
              for dt in range(DT)]

    def qk_path(wname, ncol, dst_tiles, dd):
        evs_all = []
        tmpq = mk.f32(SM_C, 512)
        sq2 = mk.f32(SM_C + 1024, 512)
        grep = mk.f32(SM_C + 2048, 512)

        kt = DT
        state = {}
        for s0 in range(0, 8, 4):
            o_ld, slab, slot = ws.load(W[wname], s0 * 128, 512, D, deps=())
            jobs = []
            for mi in range(4):
                mt = s0 + mi
                for nch in range(NCH):
                    def mms(q, bank, slab=slab, mi=mi, n=nch):
                        ii = None
                        for kk in range(DT):
                            ii = q.matmul(bank[:, 0:512],
                                          slab[:, kk * 512 + mi * 128:
                                               kk * 512 + mi * 128 + 128],
                                          h1[kk][:, n * 512:(n + 1) * 512],
                                          start=(kk == 0),
                                          stop=(kk == DT - 1))
                        return ii

                    def ev1(q, bank, mt=mt, n=nch):
                        return q.activation(tmpq[:], bank[:, 0:512], AF.Copy)
                    jobs.append(dict(mms=mms, deps=[o_ld] + list(dd),
                                     evac=ev1, evac_q="act"))
            def flushq(pe_):
                if pe_ is None:
                    return
                mt, nch, o_sq = pe_
                o_gm = p.add("pe", lambda q: q.matmul(
                    mk.ps[6][:, 0:512], mk.ones64_f[:], sq2[:],
                    start=True, stop=True),
                    [o_sq, state.get("prev_rm")], tag="l1qk_gm")

                def rmsf(q):
                    q.tensor_scalar(grep[:], mk.ps[6][:, 0:512], 1.0 / HD,
                                    1.1920929e-07, op0=ALU.mult,
                                    op1=ALU.add)
                    return q.reciprocal(grep[:], grep[:])
                o_rm = p.add("dve", rmsf, [o_gm], tag="l1qk_rm")
                state["prev_rm"] = o_rm
                o_rs = p.add("act", lambda q: q.activation(
                    grep[:], grep[:], AF.Sqrt), [o_rm], tag="l1qk_rs")

                def outf(q, mt=mt, nch=nch):
                    q.tensor_tensor(sq2[:], tmpq[:], grep[:], op=ALU.mult)
                    return q.tensor_scalar(
                        dst_tiles[mt][:, nch * 512:(nch + 1) * 512],
                        sq2[:], mk.bcol(ncol), None, op0=ALU.mult)
                o_out = p.add("dve", outf, [o_rs], tag="l1qk_out")
                evs_all.append(o_out)

            for j, job in enumerate(jobs):
                mt, nch = s0 + j // NCH, j % NCH
                b = [mk.ps[0], mk.ps[1], mk.ps[2],
                     mk.ps[3]][(state.get("j", 0)) % 4]
                o_mm = p.add("pe", (lambda q, f=job["mms"], bb=b: f(q, bb)),
                             list(job["deps"]) +
                             [mk.bank_last.get(id(b))], tag="l1qk_mm")
                flushq(state.get("pend"))
                state["pend"] = None
                o_ev = p.add("act", (lambda q, f=job["evac"], bb=b:
                                     f(q, bb)),
                             [o_mm, evs_all[-1] if evs_all else None],
                             tag="l1qk_ev")

                o_sq = p.add("act", lambda q: q.activation(
                    sq2[:], tmpq[:], AF.Square), [o_ev], tag="l1qk_sq")
                state["pend"] = (mt, nch, o_sq)
                mk.bank_last[id(b)] = o_sq
                state["j"] = state.get("j", 0) + 1
            ws.release(slot, evs_all[-1] if evs_all else None)
        flushq(state.get("pend"))
        state["pend"] = None
        return evs_all

    qf = reg_tiles(mk, R2_C)
    kf = reg_tiles(mk, R0_C)
    evs_q = qk_path("l1f_wq", "l1f_qn", qf, h1_ops[-8:])
    evs_k = qk_path("l1f_wk", "l1f_kn", kf, em_ops + evs_q[-1:])

    vext = [mk.bf(XR_C + tt * 1040, 1040) for tt in range(16)]
    ones_ops = []
    for tt in range(16):
        dst = vext[tt].rearrange("pp (h c) -> pp h c", h=16)[:, :, 64:65]
        ones_ops.append(p.add("dve", lambda q, dst=dst: q.memset(dst, 1.0),
                              sp_ops, tag="l1fvones"))
    v_evs = []
    vstate = {}
    for half in range(2):
        o_ld, slab, slot = ws.load(W["l1f_wv"], half * 512, 512, D,
                                   deps=())
        jobs = []
        for tt in range(16):
            def mms(q, bank, tt=tt, slab=slab):
                ii = None
                for kk in range(DT):
                    ii = q.matmul(bank[:, 0:512],
                                  h1[kk][:, tt * 128:(tt + 1) * 128],
                                  slab[:, kk * 512:(kk + 1) * 512],
                                  start=(kk == 0), stop=(kk == DT - 1))
                return ii
            h0 = half * 8
            dst = vext[tt].rearrange("pp (h c) -> pp h c", h=16)[
                :, h0:h0 + 8, 0:64]

            def evac(q, bank, tt=tt, dst=dst):
                return q.tensor_copy(
                    dst, bank[:, 0:512].rearrange("pp (a c) -> pp a c", a=8))
            jobs.append(dict(mms=mms,
                             deps=[o_ld] + list(evs_k[-2:]) + ones_ops,
                             evac=evac, evac_q="dve"))
        jevs = mm_jobs(mk, jobs, [mk.ps[0], mk.ps[1], mk.ps[2], mk.ps[3]],
                       tag="l1fv",
                       state=vstate)
        ws.release(slot, jevs[-1])
        v_evs += jevs

    esc = [mk.bf(XR_C + 16640 + kt_ * 512, 512) for kt_ in range(16)]
    trilb = mk.bf(XR_C + 28928, 2048)
    o_tril = p.dma("pool", trilb, W["tril4"], deps=sp_ops, tag="l1ftril")
    ctx = h1
    diagb = mk.f32(SM_C + 4096, 512)
    invreps = [mk.f32(SM_C, 512, rows=64), mk.f32(SM_C + 1024, 512, rows=64)]
    rowbufs = [mk.rows[0:1, 0:512], mk.rows2[0:1, 0:512]]
    repbanks2 = [mk.ps[6], mk.ps[7]]
    ctx_evs = []
    sbanks = [mk.ps[0], mk.ps[1], mk.ps[2], mk.ps[3]]
    lastexp = {}
    lastctx = [None, None]
    last_cmm = None
    prev_cev = None
    lastrep = [None, None]
    prev_cevs2 = [None, None]
    last_dgexp = [None]
    pend = [None]

    def flush1f(pe_):
        if pe_ is None:
            return
        h, qb, cb, r0, par, o_cmm, o_inv = pe_
        invrep, rowb, repbank = invreps[par], rowbufs[par], repbanks2[par]
        o_rep = p.add("pe", lambda q, rowb=rowb, repbank=repbank:
                      q.matmul(repbank[0:64, 0:512],
                               mk.ones_row_f[0:1, 0:64],
                               rowb, start=True, stop=True),
                      [o_inv, prev_cevs2[par]], tag="l1frep")
        o_repc = p.add("act", lambda q, invrep=invrep, repbank=repbank:
                       q.activation(invrep[:], repbank[0:64, 0:512],
                                    AF.Copy), [o_rep], tag="l1frepc")
        lastrep[par] = o_repc

        def cev(q, h=h, qb=qb, cb=cb, r0=r0, invrep=invrep):
            return q.tensor_tensor(
                ctx[h // 2][r0:r0 + 64, qb * 512:(qb + 1) * 512],
                cb[0:64, 0:512], invrep[:], op=ALU.mult)
        o_cev = p.add("dve", cev, [o_repc, o_cmm], tag="l1fcev")
        lastctx[qb % 2] = o_cev
        prev_cevs2[par] = o_cev
        ctx_evs.append(o_cev)

    for h in range(H):
        r0 = (h % 2) * 64
        for qb in range(NCH):
            par = (h * NCH + qb) % 2
            rowb = rowbufs[par]
            nkt = qb * 4 + 4
            exp_ops = []
            for kt_ in range(nkt):
                b = sbanks[kt_ % 4]

                def smm(q, bank, h=h, qb=qb, kt_=kt_, r0=r0):
                    return q.matmul(
                        bank[:, 0:512],
                        kf[h // 2][r0:r0 + 64, kt_ * 128:(kt_ + 1) * 128],
                        qf[h // 2][r0:r0 + 64, qb * 512:(qb + 1) * 512],
                        start=True, stop=True, tile_position=(r0, 0))
                dd = [v_evs[-1], lastexp.get(kt_ % 4), last_cmm, evs_q[-1],
                      evs_k[-1]]
                o_smm = p.add("pe", (lambda q, f=smm, bb=b: f(q, bb)), dd,
                              tag="l1fsmm")
                if kt_ >= qb * 4:
                    o_dg = p.add(
                        "dve", lambda q, bb=b, o=kt_ - qb * 4:
                        q.tensor_tensor(diagb[:], bb[:, 0:512],
                                        trilb[:, o * 512:(o + 1) * 512],
                                        op=ALU.add),
                        [o_smm, o_tril, last_dgexp[0]], tag="l1fdiag")
                    o_exp = p.add("act", lambda q, kt_=kt_: q.activation(
                        esc[kt_][:], diagb[:], AF.Exp, scale=0.125),
                        [o_dg], tag="l1fexp")
                    last_dgexp[0] = o_exp
                else:
                    o_exp = p.add("act", (lambda q, bb=b, kt_=kt_:
                                          q.activation(esc[kt_][:],
                                                       bb[:, 0:512], AF.Exp,
                                                       scale=0.125)),
                                  [o_smm], tag="l1fexp")
                lastexp[kt_ % 4] = o_exp
                exp_ops.append(o_exp)
            flush1f(pend[0])
            pend[0] = None
            cb = mk.ps[4 + (qb % 2)]
            o_cmm = None
            for kt_ in range(nkt):
                def cmm1(q, h=h, cb=cb, kt_=kt_, nkt=nkt):
                    return q.matmul(cb[0:65, 0:512],
                                    vext[kt_][:, h * 65:h * 65 + 65],
                                    esc[kt_][:], start=(kt_ == 0),
                                    stop=(kt_ == nkt - 1),
                                    skip_group_check=True)
                dd = [exp_ops[kt_]] + ([lastctx[qb % 2], v_evs[-1]]
                                       if kt_ == 0 else [])
                o_cmm = p.add("pe", cmm1, dd, tag="l1fcmm")
            last_cmm = o_cmm
            o_inv = p.add("dve", lambda q, cb=cb, rowb=rowb: q.reciprocal(
                rowb, cb[64:65, 0:512]),
                [o_cmm, lastrep[par]], tag="l1finv")
            pend[0] = (h, qb, cb, r0, par, o_cmm, o_inv)
    flush1f(pend[0])
    pend[0] = None

    em_ld = [p.dma("sp", kf[dt][:], scr_emb[dt * 128:(dt + 1) * 128, :],
                   deps=ctx_evs, tag="l1fembld") for dt in range(DT)]
    embr = kf
    xmid = qf

    def evwo(q, bank, mt, nch):
        return q.tensor_tensor(xmid[mt][:, nch * 512:(nch + 1) * 512],
                               bank[:, 0:512],
                               embr[mt][:, nch * 512:(nch + 1) * 512],
                               op=ALU.add)
    evs_wo = e_linear_fm2(mk, ws, W["l1f_wo"], D, D, ctx, evwo,
                          em_ld + ctx_evs,
                          [mk.ps[0], mk.ps[1], mk.ps[2], mk.ps[3]],
                          evac_q="dve", tag="l1fwo", mt_range=range(2, 8))
    cp_ops = []
    for mt in range(2):
        cp_ops.append(p.add("dve", lambda q, m=mt: q.tensor_copy(
            xmid[m][:], embr[m][:]), em_ld + ctx_evs + [evs_q[-1]],
            tag="l1fxmcp"))

    rs_ops = [p.dma("sp", XR[dt], scr_xr[dt * 128:(dt + 1) * 128, :],
                    deps=ctx_evs + evs_wo + cp_ops, tag="l1frest")
              for dt in range(DT)]
    xm_add = []
    for mt in range(DT):
        def xaf(q, m=mt):
            ii = None
            for nch in range(NCH):
                ii = q.tensor_tensor(XR[m][:, nch * 512:(nch + 1) * 512],
                                     XR[m][:, nch * 512:(nch + 1) * 512],
                                     xmid[m][:, nch * 512:(nch + 1) * 512],
                                     op=ALU.add)
            return ii
        xm_add.append(p.add("dve", xaf, rs_ops + evs_wo + cp_ops,
                            tag="l1fxma"))

    xf = kf
    xf_ops = []
    prevc = None
    for nch in range(NCH):
        st_mm = []
        for mt in range(DT):
            o_sq = p.add("dve", lambda q, m=mt, n=nch: q.tensor_tensor(
                sqtmp[:], xmid[m][:, n * 512:(n + 1) * 512],
                xmid[m][:, n * 512:(n + 1) * 512], op=ALU.mult),
                evs_wo + cp_ops + [st_mm[-1] if st_mm else prevc] + xm_add,
                tag="l1xfsq")
            o_mm = p.add("pe", lambda q, first=(mt == 0),
                         lastf=(mt == DT - 1): q.matmul(
                mk.ps[6][:, 0:512], mk.ones_f[:], sqtmp[:],
                start=first, stop=lastf), [o_sq], tag="l1xfsm")
            st_mm.append(o_mm)

        def stf(q):
            q.tensor_scalar(srep[:], mk.ps[6][:, 0:512], 1.0 / D,
                            1.1920929e-07, op0=ALU.mult, op1=ALU.add)
            return q.reciprocal(srep[:], srep[:])
        o_st = p.add("dve", stf, [st_mm[-1]], tag="l1xfst")
        o_rs = p.add("act", lambda q: q.activation(
            srep[:], srep[:], AF.Sqrt), [o_st], tag="l1xfrs")
        for mt in range(DT):
            def nf(q, m=mt, n=nch):
                q.tensor_tensor(sqtmp[:], xmid[m][:, n * 512:(n + 1) * 512],
                                srep[:], op=ALU.mult)
                return q.tensor_scalar(xf[m][:, n * 512:(n + 1) * 512],
                                       sqtmp[:], mk.bcol("l1f_n2", m), None,
                                       op0=ALU.mult)
            xf_ops.append(p.add("dve", nf, [o_rs], tag="l1xfn"))
        prevc = xf_ops[-1]

    gsl = mk.bf(SM_C, 32)
    o_gld = p.dma("pool", gsl, W["l1f_gate_kt"], deps=xf_ops[-8:],
                  tag="l1gld")
    scro = mk.bias_off["l1f_scr"]
    probs = mk.bias[:, scro:scro + 64]
    sums = mk.bias[:, scro + 64:scro + 80]
    cnt = mk.bias[:, scro + 80:scro + 96]
    g_mm = []
    for tt in range(16):
        def gmm(q, tt=tt):
            ii = None
            for kk in range(DT):
                ii = q.matmul(mk.ps[7][:, tt * 4:tt * 4 + 4],
                              xf[kk][:, tt * 128:(tt + 1) * 128],
                              gsl[:, kk * 4:(kk + 1) * 4],
                              start=(kk == 0), stop=(kk == DT - 1))
            return ii
        g_mm.append(p.add("pe", gmm,
                          [o_gld, xf_ops[-1]] + lastrep, tag="l1gmm"))
    o_gexp = p.add("act", lambda q: q.activation(
        probs[:], mk.ps[7][:, 0:64], AF.Exp), g_mm, tag="l1gexp")

    def topf(q):
        pv = probs[:].rearrange("p (tt e) -> p tt e", e=4)
        for e in range(4):
            first = True
            for e2 in range(4):
                if e2 == e:
                    continue
                op = ALU.is_ge if e2 < e else ALU.is_gt
                if first:
                    q.tensor_tensor(cnt[:], pv[:, :, e2:e2 + 1],
                                    pv[:, :, e:e + 1], op=op)
                    first = False
                else:
                    q.tensor_tensor(sums[:], pv[:, :, e2:e2 + 1],
                                    pv[:, :, e:e + 1], op=op)
                    mk.dve_gap(q)
                    q.tensor_tensor(cnt[:], cnt[:], sums[:], op=ALU.add)
                mk.dve_gap(q)
            q.tensor_scalar(cnt[:], cnt[:], 2.0, None, op0=ALU.is_lt)
            mk.dve_gap(q)
            q.tensor_tensor(pv[:, :, e:e + 1], pv[:, :, e:e + 1], cnt[:],
                            op=ALU.mult)
            mk.dve_gap(q)
        q.tensor_tensor(sums[:], pv[:, :, 0:1], pv[:, :, 1:2], op=ALU.add)
        mk.dve_gap(q)
        q.tensor_tensor(cnt[:], pv[:, :, 2:3], pv[:, :, 3:4], op=ALU.add)
        mk.dve_gap(q)
        q.tensor_tensor(sums[:], sums[:], cnt[:], op=ALU.add)
        mk.dve_gap(q)
        q.tensor_scalar(sums[:], sums[:], 1e-8, None, op0=ALU.add)
        mk.dve_gap(q)
        q.reciprocal(sums[:], sums[:])
        mk.dve_gap(q)
        for e in range(4):
            q.tensor_tensor(pv[:, :, e:e + 1], pv[:, :, e:e + 1], sums[:],
                            op=ALU.mult)
            mk.dve_gap(q)
        return q.tensor_copy(mk.wtsbf[:], probs[:])
    o_top = p.add("dve", topf, [o_gexp], tag="l1top")
    o_wtp = p.add("pe", lambda q: q.transpose(
        mk.ps_bf[6][0:64, 0:128], mk.wtsbf[:], mk.ident[:]), [o_top],
        tag="l1wtp")
    o_wtc = p.add("act", lambda q: q.activation(
        mk.wtt[:], mk.ps_bf[6][0:64, 0:128], AF.Copy), [o_wtp], tag="l1wtc")
    o_wst = p.dma("sp", scr_gate, mk.wtt[:], deps=[o_wtc], tag="l1wst")

    hA = [mk.bf(R2_C + ft * 1024, 1024) for ft in range(16)]
    hB = [mk.bf(R1_C + ft * 1024, 1024) for ft in range(16)]
    wrep = mk.bf(SM_C, 2048)
    tmpo = mk.f32(SM_C + 4096, 512)
    moe_tails = list([o_wst]) + xm_add
    for e in range(4):
        src_ap = scr_gate.rearrange("(tt e) p -> e tt p", e=4)[e]
        o_wr = p.dma("sp", wrep[:].rearrange("pp (tt c) -> pp tt c", tt=16),
                     src_ap.partition_broadcast(128),
                     deps=moe_tails, tag="l1wrep")
        wr_dep = [o_wr]

        def down_ev(q, bank, mt, nch):
            q.tensor_tensor(tmpo[:], bank[:, 0:512],
                            wrep[:, nch * 512:(nch + 1) * 512], op=ALU.mult)
            return q.tensor_tensor(XR[mt][:, nch * 512:(nch + 1) * 512],
                                   XR[mt][:, nch * 512:(nch + 1) * 512],
                                   tmpo[:], op=ALU.add)
        for tb in range(2):
            nr = [2 * tb, 2 * tb + 1]
            co = tb * 1024
            if e == 0:
                def evh(q, bank, mt, nch, co=co):
                    return q.activation(
                        hA[mt][:, nch * 512 - co:(nch + 1) * 512 - co],
                        bank[:, 0:512], AF.Gelu)
                eh = e_linear_fm2(mk, ws, W["l1f_e0_up"], D, 2048, xf, evh,
                                  moe_tails, [mk.ps[0], mk.ps[1], mk.ps[4]],
                                  tag="e0u", nch_range=nr)
                ed = e_linear_fm2(mk, ws, W["l1f_e0_down"], 2048, D, hA,
                                  down_ev, eh[-2:] + wr_dep,
                                  [mk.ps[2], mk.ps[3], mk.ps[5]],
                                  evac_q="dve",
                                  tag="e0d", nch_range=nr,
                                  mt_range=range(2, 8), mcols=256,
                                  rhs_coloff=co)
            elif e == 1:
                def evh1(q, bank, mt, nch, co=co):
                    return q.activation(
                        hA[mt][:, nch * 512 - co:(nch + 1) * 512 - co],
                        bank[:, 0:512], AF.Silu)
                eh = e_linear_fm2(mk, ws, W["l1f_e1_w1"], D, 2048, xf, evh1,
                                  moe_tails, [mk.ps[0], mk.ps[1], mk.ps[4]],
                                  tag="e1a", nch_range=nr)

                def evh2(q, bank, mt, nch, co=co):
                    return q.tensor_tensor(
                        hA[mt][:, nch * 512 - co:(nch + 1) * 512 - co],
                        bank[:, 0:512],
                        hA[mt][:, nch * 512 - co:(nch + 1) * 512 - co],
                        op=ALU.mult)
                eh2 = e_linear_fm2(mk, ws, W["l1f_e1_w2"], D, 2048, xf,
                                   evh2, eh[-2:], [mk.ps[0], mk.ps[1],
                                                   mk.ps[4]],
                                   evac_q="dve", tag="e1b", nch_range=nr)
                ed = e_linear_fm2(mk, ws, W["l1f_e1_down"], 2048, D, hA,
                                  down_ev, eh2[-2:] + wr_dep,
                                  [mk.ps[2], mk.ps[3], mk.ps[5]],
                                  evac_q="dve",
                                  tag="e1d", nch_range=nr,
                                  mt_range=range(2, 8), mcols=256,
                                  rhs_coloff=co)
            elif e == 2:
                def evg(dst, co=co):
                    def f(q, bank, mt, nch):
                        return q.activation(
                            dst[mt][:, nch * 512 - co:(nch + 1) * 512 - co],
                            bank[:, 0:512], AF.Gelu)
                    return f
                eh = e_linear_fm2(mk, ws, W["l1f_e2_l1"], D, 2048, xf,
                                  evg(hA), moe_tails, [mk.ps[0], mk.ps[1],
                                                       mk.ps[4]],
                                  tag="e2a", nch_range=nr)
                eh2 = e_linear_fm2(mk, ws, W["l1f_e2_l2"], 2048, 2048, hA,
                                   evg(hB), eh[-2:], [mk.ps[0], mk.ps[1],
                                                      mk.ps[4]],
                                   tag="e2b", nch_range=nr, mcols=256,
                                   rhs_coloff=co)
                eh3 = e_linear_fm2(mk, ws, W["l1f_e2_l3"], 2048, 2048, hB,
                                   evg(hA), eh2[-2:], [mk.ps[0], mk.ps[1],
                                                       mk.ps[4]],
                                   tag="e2c", nch_range=nr, mcols=256,
                                   rhs_coloff=co)
                ed = e_linear_fm2(mk, ws, W["l1f_e2_l4"], 2048, D, hA,
                                  down_ev, eh3[-2:] + wr_dep,
                                  [mk.ps[2], mk.ps[3], mk.ps[5]],
                                  evac_q="dve",
                                  tag="e2d", nch_range=nr,
                                  mt_range=range(2, 8), mcols=256,
                                  rhs_coloff=co)
            else:
                def evg3(dst, co=co):
                    def f(q, bank, mt, nch):
                        return q.activation(
                            dst[mt][:, nch * 512 - co:(nch + 1) * 512 - co],
                            bank[:, 0:512], AF.Gelu)
                    return f
                hD = hA[:8]
                eh = e_linear_fm2(mk, ws, W["l1f_e3_down"], D, D, xf,
                                  evg3(hD), moe_tails, [mk.ps[0], mk.ps[1],
                                                        mk.ps[4]],
                                  tag="e3a", nch_range=nr)
                eh2 = e_linear_fm2(mk, ws, W["l1f_e3_up"], D, 2048, hD,
                                   evg3(hB), eh[-2:], [mk.ps[0], mk.ps[1],
                                                       mk.ps[4]],
                                   tag="e3b", nch_range=nr, rhs_coloff=co)
                ed = e_linear_fm2(mk, ws, W["l1f_e3_out"], 2048, D, hB,
                                  down_ev, eh2[-2:] + wr_dep,
                                  [mk.ps[2], mk.ps[3], mk.ps[5]],
                                  evac_q="dve",
                                  tag="e3d", nch_range=nr,
                                  mt_range=range(2, 8), mcols=256,
                                  rhs_coloff=co)
            moe_tails = ed
    return moe_tails


def e_layer1(mk, W, deps, stage="all"):
    p = mk.p
    scr_xr = mk.scr["xr"]
    ln_ops = e_layernorm(mk, deps)
    at = e_l1_attn(mk, W, ln_ops[-8:], scr_xr)
    if stage == "attn":
        return at
    return e_l1_ffn(mk, W, at, scr_xr, mk.scr["emb"], mk.scr["gate"])




LAYER_EMIT = {0: e_layer0, 1: e_layer1, 2: e_layer2, 3: e_layer3}


def build_program(shapes, bias_off, layers, nbatch=2, stage="all",
                  dump=False):
    nc = bass.Bass("TRN2", target_bir_lowering=False)
    xin = nc.dram_tensor("xin", [nbatch, D, L], F32, kind="ExternalInput").ap()
    xout = nc.dram_tensor("xout", [nbatch, D, L], F32,
                          kind="ExternalOutput").ap()
    dbg = {}
    if dump:
        for nm in ["r0", "r1", "r2"]:
            dbg[nm] = nc.dram_tensor("dbg_" + nm, [128, 16384], BF16,
                                     kind="ExternalOutput").ap()
        dbg["bias"] = nc.dram_tensor("dbg_bias", [128, 260], F32,
                                     kind="ExternalOutput").ap()
        dbg["rows"] = nc.dram_tensor("dbg_rows", [1, 512], F32,
                                     kind="ExternalOutput").ap()
    W = {}
    for name, shp in shapes.items():
        W[name] = nc.dram_tensor(name, list(shp), F32,
                                 kind="ExternalInput").ap()
    ctx = contextlib.ExitStack()
    with ctx:
        p = Prog(nc, ctx)
        mk = make_mk(nc, ctx, p, bias_off,
                     bias_w=shapes["biasbuf"][1] + 40)
        if 3 in layers:
            mk.scr["w2"] = nc.dram_tensor("scr_w2", [D, 4096], BF16,
                                          kind="Internal").ap()
            mk.scr["w3"] = nc.dram_tensor("scr_w3", [4096, D], BF16,
                                          kind="Internal").ap()
        if 3 in layers or 0 in layers or 1 in layers:
            mk.scr["xr"] = nc.dram_tensor("scr_xr", [D, L], F32,
                                          kind="Internal").ap()
        if 1 in layers:
            mk.scr["emb"] = nc.dram_tensor("scr_emb", [D, L], BF16,
                                           kind="Internal").ap()
            mk.scr["gate"] = nc.dram_tensor("scr_gate", [64, 128], BF16,
                                            kind="Internal").ap()
        if 0 in layers:
            mk.scr["row"] = nc.dram_tensor("scr_row", [1, 64], F32,
                                           kind="Internal").ap()
            mk.scr["l0up"] = nc.dram_tensor("scr_l0up", [D, 4096], BF16,
                                            kind="Internal").ap()
            mk.scr["l0dn"] = nc.dram_tensor("scr_l0dn", [8, 128, 4096], BF16,
                                            kind="Internal").ap()
        ini = []
        ini.append(p.dma("pool", mk.ident[:], W["ident"], tag="ini"))
        ini.append(p.dma("sp", mk.ident_f[:], W["ident"], tag="ini"))
        ini.append(p.dma("sp", mk.ones_f[:], W["ones"], tag="ini"))
        ini.append(p.dma("pool", mk.ones_bf[:], W["ones"], tag="ini"))
        ini.append(p.dma("sp", mk.ones_row_f[:], W["ones"][0:1, :], tag="ini"))
        if "ones64" in W:
            ini.append(p.dma("sp", mk.ones64_f[:], W["ones64"], tag="ini"))
        if "l0sel" in W:
            ini.append(p.dma("sp", mk.sel[:], W["l0sel"], tag="ini"))
            ini.append(p.dma("pool", mk.selA[:],
                             W["l0sel"][:, 0:128], tag="ini"))
        if "bmasks" in W:
            ini.append(p.dma(
                "pool", mk.bmasks[:].rearrange("p (s m) -> p s m", s=8),
                W["bmasks"].rearrange("s p m -> p s m"), tag="ini"))
        ini.append(p.dma("sp", mk.bias[:, 0:W["biasbuf"].shape[1]],
                         W["biasbuf"], tag="ini"))
        for i, nm in enumerate(ROWPACK):
            ini.append(p.dma("pool", mk.brow[:, i * D:(i + 1) * D],
                             W["rowpack"][i:i + 1, :].partition_broadcast(128),
                             tag="ini"))
        store_ops = []
        for b in range(nbatch):
            XR = xr_tiles(mk)
            xl = []
            ld_deps = ini + (p.tails() if b else [])
            for dt in range(DT):
                xl.append(p.dma("sp", XR[dt],
                                xin[b, dt * 128:(dt + 1) * 128, :],
                                deps=ld_deps, tag="xload"))
            deps = xl
            for li in layers:
                deps = LAYER_EMIT[li](mk, W, deps, stage=stage)
                deps = deps[-2:] + p.tails("pe", "act", "dve", "pool", "sp")
            for dt in range(DT):
                store_ops.append(p.dma("sp",
                                       xout[b, dt * 128:(dt + 1) * 128, :],
                                       XR[dt], deps=p.tails("pe", "act", "dve"),
                                       tag="xstore"))
        if dump:
            if "w2" in mk.scr:
                dbg["w2"] = nc.dram_tensor("dbg_w2", [256, 4096], BF16,
                                           kind="ExternalOutput").ap()
                dbg["w3"] = nc.dram_tensor("dbg_w3", [256, D], BF16,
                                           kind="ExternalOutput").ap()
                store_ops.append(p.dma("sp", dbg["w2"],
                                       mk.scr["w2"][0:256, :],
                                       deps=p.tails("pe", "act", "dve"),
                                       tag="dbgstore"))
                store_ops.append(p.dma("sp", dbg["w3"],
                                       mk.scr["w3"][0:256, :],
                                       deps=p.tails("pe", "act", "dve"),
                                       tag="dbgstore"))
            for nm, c0 in (("r0", R0_C), ("r1", R1_C), ("r2", R2_C)):
                store_ops.append(p.dma("sp", dbg[nm], mk.bf(c0, 16384),
                                       deps=p.tails("pe", "act", "dve"),
                                       tag="dbgstore"))
            store_ops.append(p.dma("sp", dbg["bias"],
                                   mk.bias[:, 0:260],
                                   deps=p.tails("pe", "act", "dve"),
                                   tag="dbgstore"))
            store_ops.append(p.dma("sp", dbg["rows"], mk.rows[0:1, :],
                                   deps=p.tails("pe", "act", "dve"),
                                   tag="dbgstore"))
        p.add("sp", lambda q: q.nop(), store_ops, tag="fin")
        block = ctx.enter_context(nc.Block())
        p.finalize(block)
    return nc


def kernel(**inputs):
    W, bias_off = prep_weights(inputs)
    shapes = {k: v.shape for k, v in W.items()}
    x = np.ascontiguousarray(
        np.asarray(inputs["L0_attn_x"], np.float32).transpose(0, 2, 1))
    nc = build_program(shapes, bias_off, [0, 1, 2, 3], nbatch=2)
    in_maps = []
    for c in range(8):
        m = dict(W)
        m["xin"] = np.ascontiguousarray(x[2 * c:2 * c + 2])
        in_maps.append(m)
    res = run_bass_kernel_spmd(nc, in_maps, core_ids=list(range(8)))
    out = np.concatenate([r["xout"] for r in res.results], axis=0)
    return np.ascontiguousarray(out.transpose(0, 2, 1))


# version 10 (final)
# speedup vs baseline: 1.0348x; 1.0348x over previous
import contextlib
import numpy as np

import concourse.bass as bass
import concourse.mybir as mybir
from concourse.bass_utils import run_bass_kernel_spmd
from concourse.alu_op_type import AluOpType as ALU

F32 = mybir.dt.float32
BF16 = mybir.dt.bfloat16
AF = mybir.ActivationFunctionType

D = 1024
L = 2048
DT = 8
NCH = 4
NEG = -1.0e30

_QUEUES = ("pe", "act", "dve", "pool", "sp")
_BLOCK_ATTR = {"pe": "tensor", "act": "scalar", "dve": "vector",
               "pool": "gpsimd", "sp": "sync"}


class Op:
    __slots__ = ("qname", "fn", "deps", "seq", "needed", "sem", "val", "incn",
                 "tag", "_dmawait")

    def __init__(self, qname, fn, deps, seq, sem, incn, tag):
        self.qname, self.fn, self.deps, self.seq = qname, fn, deps, seq
        self.needed, self.sem, self.val, self.incn = False, sem, None, incn
        self.tag = tag


class Prog:
    def __init__(self, nc, ctx):
        self.nc, self.ctx = nc, ctx
        self.ops = []
        self.stream = {q: [] for q in _QUEUES}
        self.sems = {}

    def add(self, qname, fn, deps=(), sem=None, incn=1, tag=""):
        deps = [d for d in deps if d is not None]
        op = Op(qname, fn, deps, len(self.ops), sem or qname, incn, tag)
        self.ops.append(op)
        self.stream[qname].append(op)
        return op

    def dma(self, qname, dst, src, deps=(), sem=None, tag="dma", slow=False):
        assert qname in ("pool", "sp")
        return self.add(
            qname,
            lambda q: q.dma_start(dst, src, allow_slow_non_contiguous=slow),
            deps, sem=sem or ("D" + qname), incn=16, tag=tag)

    def tails(self, *qnames):
        if not qnames:
            qnames = _QUEUES
        return [self.stream[q][-1] for q in qnames if self.stream[q]]

    def finalize(self, block):
        semq = {}
        for op in self.ops:
            qq = semq.setdefault(op.sem, op.qname)
            assert qq == op.qname, ("sem on two queues", op.sem, qq, op.qname)
        for op in self.ops:
            for d in op.deps:
                if d.qname != op.qname or d.incn == 16:
                    d.needed = True
            if op.incn == 16:
                op.needed = True
        semval = {}
        for q in _QUEUES:
            for op in self.stream[q]:
                if op.needed:
                    semval[op.sem] = semval.get(op.sem, 0) + op.incn
                    op.val = semval[op.sem]
        issued = {}
        for op in self.ops:
            snap = None
            for dep in op.deps:
                if dep.incn == 16:
                    if snap is None:
                        snap = dict(issued)
                    op_snap = getattr(op, "_dmawait", None)
                    if op_snap is None:
                        op_snap = {}
                        op._dmawait = op_snap
                    op_snap[dep.sem] = max(op_snap.get(dep.sem, 0),
                                           snap.get(dep.sem, dep.val),
                                           dep.val)
            if op.incn == 16:
                issued[op.sem] = issued.get(op.sem, 0) + 16
        for s in semval:
            if s not in self.sems:
                self.sems[s] = self.ctx.enter_context(
                    self.nc.semaphore("S_" + s))
        for q in _QUEUES:
            ops = self.stream[q]
            if not ops:
                continue

            def emit(eng, ops=ops):
                hw = {}
                for op in ops:
                    for dep in op.deps:
                        if dep.qname == op.qname and dep.incn != 16:
                            continue
                        assert dep.val is not None, (dep.tag, "->", op.tag)
                        v = dep.val
                        if dep.incn == 16:
                            v = op._dmawait[dep.sem]
                        if hw.get(dep.sem, 0) < v:
                            eng.wait_ge(self.sems[dep.sem], v)
                            hw[dep.sem] = v
                    instr = op.fn(eng)
                    if op.needed:
                        assert instr is not None, (op.tag, op.qname)
                        instr.then_inc(self.sems[op.sem], op.incn)
            getattr(block, _BLOCK_ATTR[q])(emit)



def _col(v):
    v = np.asarray(v, np.float32).reshape(-1)
    return np.ascontiguousarray(v.reshape(-1, 128).T)


class BiasPack:
    def __init__(self):
        self.cols, self.off, self.w = [], {}, 0

    def add(self, name, arr):
        arr = np.asarray(arr, np.float32)
        assert arr.shape[0] == 128, (name, arr.shape)
        self.off[name] = self.w
        self.w += arr.shape[1]
        self.cols.append(arr)

    def pack(self):
        return np.ascontiguousarray(np.concatenate(self.cols, axis=1))


ROWPACK = ["l2_bv", "l2_bfq", "l2_bfk"]


def prep_weights(inp):
    w = {}
    bp = BiasPack()
    t = lambda a: np.ascontiguousarray(np.asarray(a, np.float32).T)

    w["l1a_wa_t"] = np.ascontiguousarray(
        np.asarray(inp["L1_attn_Wa_w"], np.float32).transpose(0, 2, 1))
    w["l1a_wb_t"] = np.ascontiguousarray(
        np.asarray(inp["L1_attn_Wb_w"], np.float32).transpose(0, 2, 1))
    w["l1a_c_t"] = np.ascontiguousarray(
        np.asarray(inp["L1_attn_C_w"], np.float32).transpose(0, 2, 1))
    w["l1a_d_t"] = t(inp["L1_attn_D_w"])
    w["l1a_wout_t"] = t(inp["L1_attn_Wout_w"])
    w["l1a_ffn1_t"] = t(inp["L1_attn_ffn1_w"])
    w["l1a_ffn2_t"] = t(inp["L1_attn_ffn2_w"])
    w["l1a_proj_t"] = t(inp["L1_attn_proj_w"])
    bp.add("l1a_wab", _col(np.asarray(inp["L1_attn_Wa_b"]).reshape(-1)))
    bp.add("l1a_wbb", _col(np.asarray(inp["L1_attn_Wb_b"]).reshape(-1)))
    bp.add("l1a_cb", _col(np.asarray(inp["L1_attn_C_b"]).reshape(-1)))
    bp.add("l1a_db", _col(inp["L1_attn_D_b"]))
    bp.add("l1a_woutb", _col(inp["L1_attn_Wout_b"]))
    bp.add("l1a_hng", _col(np.asarray(inp["L1_attn_hn_g"]).reshape(-1)))
    bp.add("l1a_hnb", _col(np.asarray(inp["L1_attn_hn_b"]).reshape(-1)))
    bp.add("l1a_headw", np.ascontiguousarray(np.broadcast_to(
        np.asarray(inp["L1_attn_head_w"], np.float32)[None, :], (128, 8))))
    bp.add("l1a_f1b", _col(inp["L1_attn_ffn1_b"]))
    bp.add("l1a_f2b", _col(inp["L1_attn_ffn2_b"]))
    bp.add("l1a_pb", _col(inp["L1_attn_proj_b"]))
    bp.add("l1a_ng", _col(inp["L1_attn_norm_g"]))
    bp.add("l1a_nb", _col(inp["L1_attn_norm_b"]))
    bp.add("l1a_scr", np.zeros((128, 32), np.float32))

    for nm in ["wq", "wk", "wv", "wo", "e0_up", "e0_down", "e1_w1", "e1_w2",
               "e1_down", "e2_l1", "e2_l2", "e2_l3", "e2_l4", "e3_down",
               "e3_up", "e3_out"]:
        w["l1f_" + nm] = t(inp["L1_ffn_" + nm])
    w["l1f_gate_kt"] = np.ascontiguousarray(
        np.asarray(inp["L1_ffn_gate_w"], np.float32).T.reshape(
            8, 128, 4).transpose(1, 0, 2).reshape(128, 32))
    bp.add("l1f_n1", _col(inp["L1_ffn_n1_w"]))
    bp.add("l1f_n2", _col(inp["L1_ffn_n2_w"]))
    qn = np.asarray(inp["L1_ffn_qn_w"], np.float32)
    kn = np.asarray(inp["L1_ffn_kn_w"], np.float32)
    bp.add("l1f_qn", np.concatenate([qn, qn]).reshape(-1, 1)[0:128].reshape(128, 1) if False else np.ascontiguousarray(np.concatenate([qn, qn]).reshape(128, 1)))
    bp.add("l1f_kn", np.ascontiguousarray(np.concatenate([kn, kn]).reshape(128, 1)))
    bp.add("l1f_scr", np.zeros((128, 96), np.float32))

    for nm in ["Wq", "Wk", "Wv", "Wfq", "Wfk"]:
        w["l2_" + nm] = t(inp["L2_attn_" + nm])
    w["l2_w"] = t(inp["L2_ffn_weight"])
    bp.add("l2_bq", _col(inp["L2_attn_bq"]))
    bp.add("l2_bk", _col(inp["L2_attn_bk"]))
    bp.add("l2_alpha", np.broadcast_to(
        np.asarray(inp["L2_ffn_alpha"], np.float32).reshape(1, 1),
        (128, 1)).copy())
    bp.add("l2_bias", _col(inp["L2_ffn_bias"]))
    bp.add("scratch", np.zeros((128, 32), np.float32))
    rows = {"l2_bv": inp["L2_attn_bv"], "l2_bfq": inp["L2_attn_bfq"],
            "l2_bfk": inp["L2_attn_bfk"]}
    w["rowpack"] = np.ascontiguousarray(np.stack(
        [np.asarray(rows[k], np.float32).reshape(D) for k in ROWPACK], 0))
    w["l3_qkv_t"] = t(inp["L3_attn_qkv_weight"])
    w["l3_qkvm_t"] = t(inp["L3_attn_qkv_weight_mask"])
    w["l3_out_t"] = t(inp["L3_attn_out_weight"])
    w["l3_outm_t"] = t(inp["L3_attn_out_weight_mask"])
    bp.add("l3_ob", _col(inp["L3_attn_out_bias"]))
    bp.add("l3_obm", _col(inp["L3_attn_out_bias_mask"]))

    w["l3f_wexp_t"] = t(inp["L3_ffn_W_exp"])
    w["l3f_wcon"] = np.ascontiguousarray(
        np.asarray(inp["L3_ffn_W_con"], np.float32))
    bp.add("l3f_bexp", _col(inp["L3_ffn_b_exp"]))
    bp.add("l3f_upb", _col(inp["L3_ffn_up_bias"]))
    bp.add("l3f_bcon", _col(inp["L3_ffn_b_con"]))
    bp.add("l3f_dnb", _col(inp["L3_ffn_down_bias"]))
    bp.add("l3f_scr", np.zeros((128, 80), np.float32))
    masks = []
    for s in range(7):
        m = np.zeros((128, 128), np.float32)
        idx = np.arange(128)
        m[idx ^ (1 << s), idx] = 1.0
        masks.append(m)
    masks.append(np.eye(128, dtype=np.float32))
    w["bmasks"] = np.ascontiguousarray(np.stack(masks, 0))

    def bfly_cols(wts, down):
        wts = np.asarray(wts, np.float32)
        i = np.arange(4096)
        out = np.zeros((128, 12 * 64), np.float32)
        for s in range(12):
            s_ = 1 << s
            k = (i // (2 * s_)) * s_ + (i % s_)
            bit = (i >> s) & 1
            alpha = wts[s, k, bit, bit]
            if not down:
                beta = wts[s, k, 1 - bit, bit] if s < 7 else                     wts[s, k, bit, 1 - bit]
            else:
                beta = wts[s, k, bit, 1 - bit] if s < 7 else                     wts[s, k, 1 - bit, bit]
            for ft in range(32):
                out[:, s * 64 + ft] = alpha[ft * 128:(ft + 1) * 128]
                out[:, s * 64 + 32 + ft] = beta[ft * 128:(ft + 1) * 128]
        return out
    w["bfcols"] = np.ascontiguousarray(np.concatenate(
        [bfly_cols(inp["L3_ffn_up_weights"], False),
         bfly_cols(inp["L3_ffn_down_weights"], True)], axis=1))

    w["l0_wq_t"] = t(inp["L0_attn_w_q"])
    w["l0_wkvd_t"] = t(inp["L0_attn_w_kv_down"])
    w["l0_wkvu_t"] = t(inp["L0_attn_w_kv_up"])
    w["l0_wout_t"] = t(inp["L0_attn_w_out"])
    w["l0_scorer_kt"] = np.ascontiguousarray(
        np.asarray(inp["L0_attn_w_scorer"], np.float32).reshape(8, 128).T)
    w["l0_wup_t"] = t(inp["L0_ffn_w_up"])
    w["l0_wdn_t"] = t(inp["L0_ffn_w_down"])
    bp.add("l0_scr", np.zeros((128, 32), np.float32))
    pos = np.arange(L, dtype=np.float64)
    invf = 1.0 / (100000.0 ** (np.arange(0, 64, 2, dtype=np.float64) / 64.0))
    fr = pos[:, None] * invf[None, :]
    cosт = np.cos(fr).T.astype(np.float32)
    sinт = np.sin(fr).T.astype(np.float32)
    cos64 = np.concatenate([cosт, cosт], 0)
    sin64p = np.concatenate([-sinт, sinт], 0)
    w["rope_cos"] = np.ascontiguousarray(np.concatenate([cos64, cos64], 0))
    w["rope_sin"] = np.ascontiguousarray(np.concatenate([sin64p, sin64p], 0))
    tril = np.zeros((128, 4 * 512), np.float32)
    for o in range(4):
        pp = np.arange(128)[:, None]
        jj = np.arange(512)[None, :]
        tril[:, o * 512:(o + 1) * 512] = np.where(o * 128 + pp <= jj, 0.0,
                                                  NEG)
    w["tril4"] = tril
    A32 = np.zeros((32, 128), np.float32)
    for j in range(32):
        A32[j, :] = ((np.arange(128) // 64) == (j % 2)).astype(np.float32)
    D32 = np.zeros((32, 16), np.float32)
    for j in range(32):
        D32[j, j // 2] = 1.0
    LT32 = np.tril(np.ones((32, 32), np.float32), -1)
    w["l0sel"] = np.ascontiguousarray(
        np.concatenate([A32, D32, LT32], axis=1))

    o64 = np.zeros((128, 128), np.float32)
    o64[:64, :64] = 1.0
    o64[64:, 64:] = 1.0
    w["ones64"] = o64
    w["biasbuf"] = bp.pack()
    w["ident"] = np.eye(128, dtype=np.float32)
    w["ones"] = np.ones((128, 128), np.float32)
    return w, bp.off



XR_C = 0
R0_C = 32768
R1_C = 49152
R2_C = 65536
SM_C = 81920
WS_C = 88064
ARENA = 96256
WSLOT = 4096


class MK:
    def __init__(self, nc, ctx, p, bias_off, bias_w=None):
        self.nc, self.p, self.bias_off = nc, p, bias_off
        sb = lambda n_, s, dt: ctx.enter_context(nc.sbuf_tensor(n_, s, dt))
        self.ident = sb("identb", [128, 128], BF16)
        self.ones_f = sb("ones_f", [128, 128], F32)
        self.ones_bf = sb("ones_bf", [128, 128], BF16)
        self.ones_row_f = sb("ones_row_f", [1, 128], F32)
        self.rows = sb("rows_f", [1, 512], F32)
        bw = bias_w or ((max(bias_off.values()) + 40) if bias_off else 8)
        self.bias = sb("biasbuf_s", [128, bw], F32)
        self.brow = sb("brow_s", [128, len(ROWPACK) * D], BF16)
        self.gapbuf = sb("gapbuf", [128, 64], F32)
        self.bmasks = sb("bmasks_s", [128, 8 * 128], BF16)
        self.ident_f = sb("ident_f", [128, 128], F32)
        self.bfc = sb("bfc_s", [128, 64], F32)
        self.sel = sb("sel_s", [32, 176], F32)
        self.selbf = sb("selbf_s", [32, 16], BF16)
        self.selA = sb("selA_s", [32, 128], BF16)
        self.rows2 = sb("rows2_f", [1, 512], F32)
        self.ones64_f = sb("ones64_f", [128, 128], F32)
        self.wtsbf = sb("wtsbf", [128, 64], BF16)
        self.wtt = sb("wtt", [64, 128], BF16)
        self.scr = {}
        self.scr_ready = {}
        self.l3_built = False
        self.tern_built = False
        self.bank_last = {}
        self.big = sb("big", [128, ARENA], BF16)
        self.bigf = self.big[:].bitcast(F32)
        self.ps = [ctx.enter_context(nc.psum_tensor(f"ps{i}", [128, 512], F32))
                   for i in range(8)]
        self.ps_bf = [b[:].bitcast(BF16) for b in self.ps]
        self.ws = None

    def bf(self, c0, ncols, rows=128, r0=0):
        assert c0 + ncols <= ARENA, (c0, ncols)
        return self.big[r0:r0 + rows, c0:c0 + ncols]

    def f32(self, c0, nf32, rows=128, r0=0):
        assert c0 % 2 == 0 and c0 + 2 * nf32 <= ARENA, (c0, nf32)
        return self.bigf[r0:r0 + rows, c0 // 2: c0 // 2 + nf32]

    def dve_gap(self, q):
        q.tensor_scalar(self.gapbuf[:], self.ones_f[:, 0:64], 1.0, None,
                        op0=ALU.mult)

    def bcol(self, name, i=0, n=1):
        o = self.bias_off[name]
        return self.bias[:, o + i: o + i + n]

    def browr(self, name, c0=0, n=D):
        i = ROWPACK.index(name)
        return self.brow[:, i * D + c0: i * D + c0 + n]


def xr_tiles(mk):
    return [mk.f32(XR_C + dt * 4096, 2048) for dt in range(DT)]


def reg_tiles(mk, c0):
    return [mk.bf(c0 + dt * 2048, 2048) for dt in range(DT)]



class WStream:

    def __init__(self, mk):
        self.mk = mk
        self.free_after = [None, None, None, None]
        self.turn1 = 0
        self.turn2 = 0

    def _alloc(self, ncols_bf16):
        if ncols_bf16 <= 2048:
            s = self.turn1
            self.turn1 = (self.turn1 + 1) % 4
            return [s]
        assert ncols_bf16 <= 4096
        s = 2 * self.turn2
        self.turn2 ^= 1
        return [s, s + 1]

    def load(self, wt_ap, m0, mcols, K, deps=(), cast=True, src_bf16=False):
        mk = self.mk
        kt = K // 128
        ncols = kt * mcols * (1 if (cast or src_bf16) else 2)
        subs = self._alloc(ncols)
        c0 = WS_C + subs[0] * 2048
        sl = wt_ap[:, m0:m0 + mcols].rearrange("(kt p) m -> p kt m", p=128)
        dd = list(deps) + [self.free_after[i] for i in subs]
        if src_bf16:
            dst = mk.bf(c0, kt * mcols).rearrange("p (kt m) -> p kt m", kt=kt)
            op = mk.p.dma("sp", dst, sl, deps=dd, sem="WS%d" % subs[0],
                          tag="wloadb")
            slab = mk.bf(c0, kt * mcols)
        elif cast:
            dst = mk.bf(c0, kt * mcols).rearrange("p (kt m) -> p kt m", kt=kt)
            op = mk.p.dma("pool", dst, sl, deps=dd, sem="WP%d" % subs[0],
                          tag="wload")
            slab = mk.bf(c0, kt * mcols)
        else:
            dst = mk.f32(c0, kt * mcols).rearrange("p (kt m) -> p kt m", kt=kt)
            op = mk.p.dma("sp", dst, sl, deps=dd, sem="WS%d" % subs[0],
                          tag="wloadf")
            slab = mk.f32(c0, kt * mcols)
        return op, slab, tuple(subs)

    def load_raw(self, src_ap, nf32, deps=()):
        mk = self.mk
        subs = self._alloc(nf32 * 2)
        c0 = WS_C + subs[0] * 2048
        dst = mk.f32(c0, nf32)
        op = mk.p.dma("sp", dst, src_ap,
                      deps=list(deps) + [self.free_after[i] for i in subs],
                      sem="WS%d" % subs[0], tag="wstrip")
        return op, dst, tuple(subs)

    def release(self, slot, op):
        for s in (slot if isinstance(slot, (tuple, list)) else (slot,)):
            self.free_after[s] = op


def make_mk(nc, ctx, p, bias_off, bias_w=None):
    mk = MK(nc, ctx, p, bias_off, bias_w)
    mk.ws = WStream(mk)
    return mk


def mm_jobs(mk, jobs, banks, tag="", state=None):
    p = mk.p
    if state is None:
        state = {}
    glast = mk.bank_last
    last = {id(b): glast.get(id(b)) for b in banks}
    j0 = state.setdefault("j", 0)
    out = []
    for j, job in enumerate(jobs, start=j0):
        b = banks[j % len(banks)]
        o_mm = p.add("pe", (lambda q, jb=job, bb=b: jb["mms"](q, bb)),
                     list(job.get("deps", ())) + [last[id(b)]],
                     tag=tag + "_mm")
        job["_mm"] = o_mm
        o_ev = p.add(job.get("evac_q", "dve"),
                     (lambda q, jb=job, bb=b: jb["evac"](q, bb)),
                     [o_mm] + list(job.get("evac_deps", ())), tag=tag + "_ev")
        last[id(b)] = o_ev
        glast[id(b)] = o_ev
        out.append(o_ev)
    state["j"] = j0 + len(jobs)
    return out


def e_linear_fm(mk, ws, wt_ap, K, M, rhs, evac, deps, banks,
                mcols=512, evac_q="act", tag="lin", nch_range=None,
                src_bf16=False, w_deps=()):
    kt = K // 128
    nmt = M // 128
    mps = mcols // 128
    if nch_range is None:
        nch_range = range(NCH)
    grid = [[None] * NCH for _ in range(nmt)]
    evs = []
    state = {}
    for s0 in range(0, nmt, mps):
        o_ld, slab, slot = ws.load(wt_ap, s0 * 128, mcols, K, deps=w_deps,
                                   src_bf16=src_bf16)
        jobs, meta = [], []
        for mi in range(mps):
            mt = s0 + mi
            if mt >= nmt:
                continue
            for nch in nch_range:
                def mms(q, bank, slab=slab, mi=mi, n=nch):
                    ii = None
                    for kk in range(kt):
                        ii = q.matmul(bank[:, 0:512],
                                      slab[:, kk * mcols + mi * 128:
                                           kk * mcols + mi * 128 + 128],
                                      rhs[kk][:, n * 512:(n + 1) * 512],
                                      start=(kk == 0), stop=(kk == kt - 1))
                    return ii
                jobs.append(dict(
                    mms=mms, deps=[o_ld] + list(deps),
                    evac=(lambda q, bank, mt=mt, n=nch: evac(q, bank, mt, n)),
                    evac_q=evac_q))
                meta.append((mt, nch))
        sevs = mm_jobs(mk, jobs, banks, tag=tag, state=state)
        for (mt, nch), ev in zip(meta, sevs):
            grid[mt][nch] = ev
        ws.release(slot, jobs[-1]["_mm"])
        evs += sevs
    return grid, evs



def e_layernorm(mk, deps, xn_c0=R0_C, keep_c0=None):
    p = mk.p
    XR = xr_tiles(mk)
    xn = reg_tiles(mk, xn_c0)
    sqb = [mk.f32(SM_C + i * 1024, 512) for i in range(2)]
    ntmp = [mk.f32(SM_C + 4096 + i * 1024, 512) for i in range(2)]
    out_ops = []
    ps1, ps2 = mk.ps[6], mk.ps[7]
    prev_chunk = None
    for nch in range(NCH):
        if keep_c0 is not None:
            mrep = mk.f32(keep_c0 + nch * 1024, 512)
            rrep = mk.f32(keep_c0 + 4096 + nch * 1024, 512)
        else:
            mrep = mk.f32(SM_C + 2048, 512)
            rrep = mk.f32(SM_C + 3072, 512)
        mm_ops = []
        for dt in range(DT):
            buf = sqb[dt % 2]
            dd = list(deps) + [mm_ops[-2] if len(mm_ops) >= 2 else prev_chunk]
            o_sq = p.add("act", lambda q, b=buf, d=dt, n=nch: q.activation(
                b[:], XR[d][:, n * 512:(n + 1) * 512], AF.Square), dd,
                tag="ln_sq")
            def mmfn(q, b=buf, d=dt, n=nch):
                q.matmul(ps2[:, 0:512], mk.ones_f[:], b[:],
                         start=(d == 0), stop=(d == DT - 1))
                return q.matmul(ps1[:, 0:512], mk.ones_f[:],
                                XR[d][:, n * 512:(n + 1) * 512],
                                start=(d == 0), stop=(d == DT - 1))
            o_mm = p.add("pe", mmfn,
                         list(deps) + [o_sq, prev_chunk], tag="ln_mm")
            mm_ops.append(o_mm)

        def statfn(q, mrep=mrep, rrep=rrep):
            q.tensor_scalar(mrep[:], ps1[:, 0:512], 1.0 / D, None,
                            op0=ALU.mult)
            q.tensor_tensor(rrep[:], mrep[:], mrep[:], op=ALU.mult)
            q.scalar_tensor_tensor(rrep[:], ps2[:, 0:512], 1.0 / D, rrep[:],
                                   op0=ALU.mult, op1=ALU.subtract)
            q.tensor_scalar(rrep[:], rrep[:], 1e-5, None, op0=ALU.add)
            return q.reciprocal(rrep[:], rrep[:])
        o_st = p.add("dve", statfn, [mm_ops[-1]], tag="ln_stat")
        o_rs = p.add("act", lambda q, rrep=rrep: q.activation(
            rrep[:], rrep[:], AF.Sqrt), [o_st], tag="ln_sqrt")
        for dt in range(DT):
            def nfn(q, d=dt, n=nch, mrep=mrep, rrep=rrep):
                q.tensor_tensor(ntmp[d % 2][:],
                                XR[d][:, n * 512:(n + 1) * 512],
                                mrep[:], op=ALU.subtract)
                return q.tensor_tensor(xn[d][:, n * 512:(n + 1) * 512],
                                       ntmp[d % 2][:], rrep[:], op=ALU.mult)
            out_ops.append(p.add("dve", nfn, [o_rs], tag="ln_norm"))
        prev_chunk = out_ops[-1]
    return out_ops



def e_l2_ffn(mk, W, deps):
    p = mk.p
    XR = xr_tiles(mk)
    xn = reg_tiles(mk, R0_C)
    ws = mk.ws
    psA = mk.ps[6]
    absb = [mk.f32(SM_C + i * 512, 128) for i in range(2)]
    mm_ops = []
    k = 0
    for s0 in range(4):
        o_ld, slab, slot = ws.load(W["l2_w"], s0 * 256, 256, D, deps=(),
                                   cast=False)
        for kt in range(DT):
            for mh in range(2):
                buf = absb[k % 2]
                dd = [o_ld, mm_ops[-2] if len(mm_ops) >= 2 else None] + \
                    (list(deps) if k < 2 else [])
                o_abs = p.add("act", lambda q, b=buf, sl=slab, kk=kt, m=mh:
                              q.activation(b[:], sl[:, kk * 256 + m * 128:
                                                    kk * 256 + m * 128 + 128],
                                           AF.Abs), dd, tag="l2_abs")
                o_mm = p.add("pe", lambda q, b=buf, first=(k == 0),
                             last=(k == 63):
                             q.matmul(psA[0:1, 0:128], mk.ones_f[:, 0:1],
                                      b[:], start=first, stop=last),
                             [o_abs], tag="l2_absmm")
                mm_ops.append(o_mm)
                k += 1
        ws.release(slot, mm_ops[-1])
    dsum = mk.rows[0:1, 0:1]
    o_red = p.add("dve", lambda q: q.tensor_reduce(
        dsum, psA[0:1, 0:128], axis=mybir.AxisListType.X, op=ALU.add),
        [mm_ops[-1]], tag="l2_dred")
    dcol = mk.bcol("scratch", 0, 2)
    psB = mk.ps[7]
    o_drep = p.add("pe", lambda q: q.matmul(
        psB[:, 0:1], mk.ones_row_f[:], dsum, start=True, stop=True),
        [o_red], tag="l2_drep")

    def dcols(q):
        q.tensor_scalar(dcol[:, 0:1], psB[:, 0:1], 0.7 / (D * D), None,
                        op0=ALU.mult)
        return q.tensor_scalar(dcol[:, 1:2], psB[:, 0:1], -0.7 / (D * D),
                               None, op0=ALU.mult)
    o_dcol = p.add("dve", dcols, [o_drep], tag="l2_dcol")

    tw = [mk.bf(R1_C + kt * 1024, 1024) for kt in range(DT)]
    gpos = mk.f32(SM_C + 1024, 128)
    gneg = mk.f32(SM_C + 1280, 128)
    t_last = None
    tern_ops = []
    for s0 in range(4):
        o_ld, slab, slot = ws.load(W["l2_w"], s0 * 256, 256, D,
                                   deps=(), cast=False)
        for kt in range(DT):
            for mh in range(2):
                mcol = s0 * 256 + mh * 128
                def tfn(q, sl=slab, kk=kt, mm=mh, mcol=mcol):
                    src = sl[:, kk * 256 + mm * 128: kk * 256 + mm * 128 + 128]
                    q.tensor_scalar(gpos[:], src, dcol[:, 0:1], None,
                                    op0=ALU.is_gt)
                    q.tensor_scalar(gneg[:], src, dcol[:, 1:2], None,
                                    op0=ALU.is_lt)
                    return q.tensor_tensor(tw[kk][:, mcol:mcol + 128],
                                           gpos[:], gneg[:], op=ALU.subtract)
                t_last = p.add("dve", tfn,
                               [o_ld, o_dcol] +
                               (list(deps) if t_last is None else []),
                               tag="l2_tern")
                tern_ops.append(t_last)
        ws.release(slot, t_last)

    alpha = mk.bcol("l2_alpha")
    tbe = mk.f32(SM_C + 1536, 512)
    jobs = []
    for mt in range(DT):
        for nch in range(NCH):
            def mms(q, bank, m=mt, n=nch):
                ii = None
                for kk in range(DT):
                    ii = q.matmul(bank[:, 0:512],
                                  tw[kk][:, m * 128:(m + 1) * 128],
                                  xn[kk][:, n * 512:(n + 1) * 512],
                                  start=(kk == 0), stop=(kk == DT - 1))
                return ii
            def evac(q, bank, m=mt, n=nch):
                q.tensor_scalar(tbe[:], bank[:, 0:512], alpha,
                                mk.bcol("l2_bias", m), op0=ALU.mult,
                                op1=ALU.add)
                return q.tensor_tensor(XR[m][:, n * 512:(n + 1) * 512],
                                       XR[m][:, n * 512:(n + 1) * 512],
                                       tbe[:], op=ALU.add)
            jobs.append(dict(mms=mms, deps=[t_last] + list(deps),
                             evac=evac, evac_q="dve"))
    return mm_jobs(mk, jobs, [mk.ps[0], mk.ps[1], mk.ps[2], mk.ps[3]],
                   tag="l2f")


def e_l2_attn(mk, W, deps):
    p = mk.p
    XR = xr_tiles(mk)
    xn = reg_tiles(mk, R0_C)
    ws = mk.ws
    H = 16

    sg = [mk.f32(SM_C + i * 2048, 1024) for i in range(2)]
    tmpe = [mk.bf(SM_C + 4096 + i * 1024, 1024) for i in range(2)]
    ssum = mk.bcol("scratch", 2, 2)
    srcp = mk.bcol("scratch", 4, 2)

    path_out = {}
    for path in ("q", "k"):
        reg_c0 = R1_C if path == "q" else R2_C
        s1 = reg_tiles(mk, reg_c0)
        bname = "l2_bq" if path == "q" else "l2_bk"
        wname = "l2_Wq" if path == "q" else "l2_Wk"
        fname = "l2_Wfq" if path == "q" else "l2_Wfk"
        frow = "l2_bfq" if path == "q" else "l2_bfk"

        def ev1(q, bank, mt, nch, s1=s1, bname=bname):
            return q.activation(s1[mt][:, nch * 512:(nch + 1) * 512],
                                bank[:, 0:512], AF.Identity,
                                bias=mk.bcol(bname, mt))
        grid1, evs1 = e_linear_fm(mk, ws, W[wname], D, D, xn, ev1, deps,
                                  [mk.ps[0], mk.ps[1]], tag="l2s1" + path)

        oldA, slabA, slotA = ws.load(W[fname], 0, 512, D, deps=())
        oldB, slabB, slotB = ws.load(W[fname], 512, 512, D, deps=())
        slabs = [(oldA, slabA), (oldB, slabB)]
        out_ops = []
        last_sm = [None, None]
        banks = [mk.ps[0], mk.ps[1], mk.ps[2], mk.ps[3]]
        lastb = {id(b): mk.bank_last.get(id(b)) for b in banks}
        last_mm = None
        for tt in range(16):
            evs2 = []
            for half in range(2):
                o_ld, slab = slabs[half]
                b = banks[(tt * 2 + half) % 4]
                def mms(q, bank, tt=tt, slab=slab, s1=s1):
                    ii = None
                    for kk in range(DT):
                        ii = q.matmul(bank[:, 0:512],
                                      s1[kk][:, tt * 128:(tt + 1) * 128],
                                      slab[:, kk * 512:(kk + 1) * 512],
                                      start=(kk == 0), stop=(kk == DT - 1))
                    return ii
                o_mm = p.add("pe", (lambda q, f=mms, bb=b: f(q, bb)),
                             [o_ld, lastb[id(b)], last_sm[tt % 2]] +
                             [grid1[kk][tt // 4] for kk in range(DT)],
                             tag="l2s2mm")
                last_mm = o_mm
                def evc(q, bank, tt=tt, half=half, frow=frow):
                    return q.tensor_tensor(
                        sg[tt % 2][:, half * 512:(half + 1) * 512],
                        bank[:, 0:512],
                        mk.browr(frow, half * 512, 512), op=ALU.add)
                o_ev = p.add("dve", (lambda q, f=evc, bb=b: f(q, bb)),
                             [o_mm], tag="l2s2ev")
                lastb[id(b)] = o_ev
                evs2.append(o_ev)
            def sexp(q, tt=tt):
                return q.activation(tmpe[tt % 2][:], sg[tt % 2][:], AF.Exp,
                                    accum_out=ssum[:, tt % 2:tt % 2 + 1])
            o_exp = p.add("act", sexp, evs2, tag="l2exp")
            if path == "q":
                prev = None
                for kk in range(DT):
                    psb = mk.ps_bf[4 + (kk % 2)]
                    def tp1(q, tt=tt, kk=kk, psb=psb):
                        return q.transpose(psb[:, 0:128],
                                           tmpe[tt % 2][:, kk * 128:(kk + 1) * 128],
                                           mk.ident[:])
                    o_tp = p.add("pe", tp1,
                                 [o_exp, out_ops[-2] if len(out_ops) >= 2 else None],
                                 tag="l2qtp")
                    def cp1(q, tt=tt, kk=kk, psb=psb, s1=s1):
                        return q.tensor_copy(
                            s1[kk][:, tt * 128:(tt + 1) * 128], psb[:, 0:128])
                    o_cp = p.add("dve", cp1, [o_tp], tag="l2qcp")
                    out_ops.append(o_cp)
                last_sm[tt % 2] = out_ops[-1]
            else:
                dst = mk.bf(reg_c0, 16384).rearrange(
                    "p (kt l) -> p kt l", kt=DT)[:, :, tt * 128:tt * 128 + 128]
                src3 = tmpe[tt % 2][:].rearrange("p (kt m) -> p kt m", kt=DT)
                def kdiv(q, tt=tt, dst=dst, src3=src3):
                    q.reciprocal(srcp[:, tt % 2:tt % 2 + 1],
                                 ssum[:, tt % 2:tt % 2 + 1])
                    mk.dve_gap(q)
                    return q.tensor_scalar(dst, src3,
                                           srcp[:, tt % 2:tt % 2 + 1], None,
                                           op0=ALU.mult)
                o_div = p.add("dve", kdiv, [o_exp], tag="l2kdiv")
                out_ops.append(o_div)
                last_sm[tt % 2] = o_div
        ws.release(slotA, last_mm)
        ws.release(slotB, last_mm)
        for b in banks:
            mk.bank_last[id(b)] = lastb[id(b)]
        path_out[path] = out_ops

    oldA, slabA, slotA = ws.load(W["l2_Wv"], 0, 512, D, deps=path_out["k"][-1:])
    oldB, slabB, slotB = ws.load(W["l2_Wv"], 512, 512, D, deps=path_out["k"][-1:])
    v_ops = []
    lastv = [None, None]
    last_mm = None
    for tt in range(16):
        mmops = []
        for half, (o_ld, slab) in enumerate([(oldA, slabA), (oldB, slabB)]):
            b = mk.ps[half]
            def mms(q, bank, tt=tt, slab=slab):
                ii = None
                for kk in range(DT):
                    ii = q.matmul(bank[:, 0:512],
                                  xn[kk][:, tt * 128:(tt + 1) * 128],
                                  slab[:, kk * 512:(kk + 1) * 512],
                                  start=(kk == 0), stop=(kk == DT - 1))
                return ii
            o_mm = p.add("pe", (lambda q, f=mms, bb=b: f(q, bb)),
                         [o_ld, lastv[half]], tag="l2v_mm")
            mmops.append(o_mm)
            last_mm = o_mm
        for half in range(2):
            b = mk.ps[half]
            dst = mk.bf(R0_C, 16384).rearrange(
                "p (kt l) -> p kt l", kt=DT)[:, half * 4:half * 4 + 4,
                                             tt * 128:tt * 128 + 128]
            def evc(q, bank, tt=tt, half=half, dst=dst):
                return q.tensor_tensor(
                    dst, bank[:, 0:512].rearrange("p (a m) -> p a m", a=4),
                    mk.browr("l2_bv", half * 512, 512).rearrange(
                        "p (a m) -> p a m", a=4), op=ALU.add)
            o_ev = p.add("dve", (lambda q, f=evc, bb=b: f(q, bb)),
                         mmops, tag="l2v_ev")
            lastv[half] = o_ev
            v_ops.append(o_ev)
    ws.release(slotA, last_mm)
    ws.release(slotB, last_mm)

    S_sb = mk.bf(WS_C, 1024)
    zcol = mk.f32(WS_C + 2048, 8)
    zcol_bf = mk.bf(WS_C + 2080, 8)
    ktm = mk.bf(R2_C, 16384).rearrange("p (kt l) -> p kt l", kt=DT)
    vtm = mk.bf(R0_C, 16384).rearrange("p (kt l) -> p kt l", kt=DT)
    qfm = reg_tiles(mk, R1_C)

    psS = mk.ps[6]
    S_ops = []
    for h in range(H):
        def smm(q, h=h):
            ii = None
            for tt in range(16):
                ii = q.matmul(psS[0:64, 0:64],
                              ktm[:, h // 2, tt * 128 + (h % 2) * 64:
                                  tt * 128 + (h % 2) * 64 + 64],
                              vtm[:, h // 2, tt * 128 + (h % 2) * 64:
                                  tt * 128 + (h % 2) * 64 + 64],
                              start=(tt == 0), stop=(tt == 15))
            return ii
        o_mm = p.add("pe", smm,
                     v_ops + path_out["k"] + [S_ops[-1] if S_ops else None],
                     tag="l2S_mm")
        r0 = (h % 2) * 64
        def sev(q, h=h, r0=r0):
            return q.activation(S_sb[r0:r0 + 64, h * 64:(h + 1) * 64],
                                psS[0:64, 0:64], AF.Copy)
        S_ops.append(p.add("act", sev, [o_mm], tag="l2S_ev"))

    psZ = mk.ps[7]
    def zmm(q):
        ii = None
        for kt in range(DT):
            for tt in range(16):
                ii = q.matmul(psZ[:, kt:kt + 1],
                              ktm[:, kt, tt * 128:(tt + 1) * 128],
                              mk.ones_bf[:, 0:1],
                              start=(tt == 0), stop=(tt == 15))
        return ii
    o_zmm = p.add("pe", zmm, path_out["k"] + [S_ops[-1]], tag="l2z_mm")
    def zcp(q):
        q.tensor_copy(zcol[:, 0:8], psZ[:, 0:8])
        return q.tensor_copy(zcol_bf[:, 0:8], psZ[:, 0:8])
    o_zcp = p.add("dve", zcp, [o_zmm], tag="l2zcp")

    tails = []
    psC1, psC2 = mk.ps[0], mk.ps[1]
    lastc = {id(psC1): None, id(psC2): None}
    dinvs = [mk.rows[0:1, 0:512], mk.rows2[0:1, 0:512]]
    reps2 = [mk.f32(SM_C, 512, rows=64), mk.f32(SM_C + 2048, 512, rows=64)]
    psDs = [mk.ps[6], mk.ps[7]]
    psRs = [mk.ps[5], mk.ps[4]]
    tbv = mk.f32(SM_C + 1024, 512)
    lastrep2 = [None, None]
    prev_cevp = [None, None]
    prev_rcp2 = [None, None]
    prev_repmm2 = [None, None]
    pend = [None]

    def flush2(pe_):
        if pe_ is None:
            return
        h, n, par, bC, o_rcp, o_cmm = pe_
        def repmm(q, par=par):
            return q.matmul(psRs[par][0:64, 0:512],
                            mk.ones_row_f[0:1, 0:64],
                            dinvs[par], start=True, stop=True)
        o_rep = p.add("pe", repmm, [o_rcp, lastrep2[par]], tag="l2rep")
        prev_repmm2[par] = o_rep
        o_repc = p.add("act", lambda q, par=par: q.activation(
            reps2[par][:], psRs[par][0:64, 0:512], AF.Copy),
            [o_rep, prev_cevp[par]], tag="l2repc")
        lastrep2[par] = o_repc
        def cev(q, bank, h=h, n=n, par=par):
            dt_, r0 = h // 2, (h % 2) * 64
            q.tensor_tensor(tbv[r0:r0 + 64, :], bank[0:64, 0:512],
                            reps2[par][:], op=ALU.mult)
            return q.tensor_tensor(
                XR[dt_][r0:r0 + 64, n * 512:(n + 1) * 512],
                XR[dt_][r0:r0 + 64, n * 512:(n + 1) * 512],
                tbv[r0:r0 + 64, :], op=ALU.add)
        o_cev = p.add("dve", (lambda q, f=cev, bb=bC: f(q, bb)),
                      [o_cmm, o_repc], tag="l2cev")
        lastc[id(bC)] = o_cev
        prev_cevp[par] = o_cev
        tails.append(o_cev)

    for h in range(H):
        for nch in range(NCH):
            par = (h * NCH + nch) % 2
            def dmm(q, h=h, n=nch, par=par):
                return q.matmul(
                    psDs[par][0:1, 0:512],
                    zcol_bf[(h % 2) * 64:(h % 2) * 64 + 64, h // 2:h // 2 + 1],
                    qfm[h // 2][(h % 2) * 64:(h % 2) * 64 + 64,
                                n * 512:(n + 1) * 512],
                    start=True, stop=True, tile_position=((h % 2) * 64, 0))
            o_dmm = p.add("pe", dmm,
                          [o_zcp, path_out["q"][-1], prev_rcp2[par]],
                          tag="l2den")
            o_rcp = p.add("dve", lambda q, par=par: q.reciprocal(
                dinvs[par], psDs[par][0:1, 0:512]),
                [o_dmm, prev_repmm2[par]], tag="l2drcp")
            prev_rcp2[par] = o_rcp
            flush2(pend[0])
            pend[0] = None
            bC = psC1 if par == 0 else psC2
            def cmm(q, bank, h=h, n=nch):
                return q.matmul(bank[0:64, 0:512],
                                S_sb[(h % 2) * 64:(h % 2) * 64 + 64,
                                     h * 64:(h + 1) * 64],
                                qfm[h // 2][(h % 2) * 64:(h % 2) * 64 + 64,
                                            n * 512:(n + 1) * 512],
                                start=True, stop=True,
                                tile_position=((h % 2) * 64, 0))
            o_cmm = p.add("pe", (lambda q, f=cmm, bb=bC: f(q, bb)),
                          [S_ops[h], lastc[id(bC)]], tag="l2ctx")
            pend[0] = (h, nch, par, bC, o_rcp, o_cmm)
    flush2(pend[0])
    pend[0] = None
    ws.free_after[0] = tails[-1]
    ws.free_after[1] = tails[-1]
    return tails


def e_layer2(mk, W, deps, stage="all"):
    ln_ops = e_layernorm(mk, deps)
    if stage == "ln":
        return ln_ops
    ffn_tails = e_l2_ffn(mk, W, ln_ops[-8:])
    if stage == "ffn":
        return ffn_tails
    attn_tails = e_l2_attn(mk, W, ffn_tails[-4:])
    return attn_tails





def e_bfly_build(mk, W, deps, kind, scr):
    p = mk.p
    down = kind == "down"
    src_name = "l3f_wcon" if down else "l3f_wexp_t"
    cpbase = 768 if down else 0
    stages = list(range(11, -1, -1)) if down else list(range(12))
    bld = [mk.f32(SM_C + i * 512, 128) for i in range(4)]
    bfcs = [mk.bfc, mk.f32(SM_C + 1024, 64)]
    btmpA = mk.f32(SM_C + 1152, 32)
    btmpB = mk.f32(SM_C + 1216, 32)
    banks = [mk.ps[0], mk.ps[1], mk.ps[2], mk.ps[3]]
    tails = list(deps)
    all_st = mk.scr_ready.setdefault("w3" if down else "w2", [])
    bias_ab = [mk.bcol("l3f_scr", 0, 32), mk.bcol("l3f_scr", 32, 32)]
    idn = mk.bmasks[:, 7 * 128:8 * 128]
    if not down:
        o_b0 = p.add("act", lambda q: q.activation(
            bias_ab[0][:], mk.bias[:, mk.bias_off["l3f_bexp"]:
                                   mk.bias_off["l3f_bexp"] + 32],
            AF.Copy), deps, tag="bfb0")
    for dh in range(2):
        t_tiles = [mk.f32(R1_C + ft * 1024, 512) for ft in range(32)]
        src = W[src_name][:, dh * 512:(dh + 1) * 512].rearrange(
            "(ft pp) m -> pp ft m", pp=128)
        dst = mk.f32(R1_C, 32 * 512).rearrange("pp (ft m) -> pp ft m", ft=32)
        o_ld = p.dma("sp", dst, src, deps=tails, sem="BLD", tag="bfld")
        last_t = [o_ld] * 32
        bank_rd = {id(b): mk.bank_last.get(id(b)) for b in banks}
        bank_rd[id(mk.ps[4])] = mk.bank_last.get(id(mk.ps[4]))
        bld_rd = [None] * 4
        bfl_rd = [None, None]
        last_dve = [None, None]
        bias_par = 0
        jb = 0
        bias_tail = o_b0 if (not down and dh == 0) else None
        for si, s in enumerate(stages):
            s_ = 1 << s
            bfc = bfcs[si % 2]
            o_bfl = p.dma("sp", bfc[:],
                          W["bfcols"][:, cpbase + s * 64:cpbase + s * 64 + 64],
                          deps=[bfl_rd[si % 2], o_ld], tag="bfcols")
            stage_last = None
            if s >= 7:
                for ft in range(32):
                    pt = ft ^ (s_ >> 7)
                    if pt < ft:
                        continue
                    bB = bld[(jb * 2) % 4]
                    bD = bld[(jb * 2 + 1) % 4]
                    def bldf(q, ft=ft, pt=pt, bB=bB, bD=bD, bfc=bfc):
                        q.tensor_scalar(bB[:], idn,
                                        bfc[:, 32 + ft:33 + ft], None,
                                        op0=ALU.mult)
                        return q.tensor_scalar(bD[:], idn,
                                               bfc[:, 32 + pt:33 + pt], None,
                                               op0=ALU.mult)
                    o_bld = p.add("pool", bldf,
                                  [o_bfl, bld_rd[(jb * 2) % 4],
                                   bld_rd[(jb * 2 + 1) % 4]], tag="bfbld")
                    bA_ = banks[(jb * 2) % 4]
                    bB_ = banks[(jb * 2 + 1) % 4]
                    def mmf(q, ft=ft, pt=pt, bB=bB, bD=bD, bA_=bA_, bB_=bB_,
                            t_tiles=t_tiles):
                        q.matmul(bA_[:, 0:512], bB[:], t_tiles[pt][:],
                                 start=True, stop=True)
                        return q.matmul(bB_[:, 0:512], bD[:], t_tiles[ft][:],
                                        start=True, stop=True)
                    o_mm = p.add("pe", mmf,
                                 [o_bld, last_t[ft], last_t[pt],
                                  bank_rd[id(bA_)], bank_rd[id(bB_)]],
                                 tag="bfmm")
                    bld_rd[(jb * 2) % 4] = o_mm
                    bld_rd[(jb * 2 + 1) % 4] = o_mm
                    def sttf(q, ft=ft, pt=pt, bA_=bA_, bB_=bB_, bfc=bfc,
                             t_tiles=t_tiles):
                        q.scalar_tensor_tensor(
                            t_tiles[ft][:], t_tiles[ft][:],
                            bfc[:, ft:ft + 1], bA_[:, 0:512],
                            op0=ALU.mult, op1=ALU.add)
                        return q.scalar_tensor_tensor(
                            t_tiles[pt][:], t_tiles[pt][:],
                            bfc[:, pt:pt + 1], bB_[:, 0:512],
                            op0=ALU.mult, op1=ALU.add)
                    o_st = p.add("dve", sttf, [o_mm], tag="bfstt")
                    last_t[ft] = o_st
                    last_t[pt] = o_st
                    bank_rd[id(bA_)] = o_st
                    bank_rd[id(bB_)] = o_st
                    stage_last = o_st
                    jb += 1
            else:
                msk = mk.bmasks[:, s * 128:(s + 1) * 128]
                for ft in range(32):
                    bB = bld[jb % 4]
                    def bldf(q, ft=ft, bB=bB, bfc=bfc, msk=msk):
                        return q.tensor_scalar(bB[:], msk,
                                               bfc[:, 32 + ft:33 + ft],
                                               None, op0=ALU.mult)
                    o_bld = p.add("pool", bldf, [o_bfl, bld_rd[jb % 4]],
                                  tag="bfbld")
                    bank = banks[jb % 4]
                    def mmf(q, ft=ft, bB=bB, bank=bank, t_tiles=t_tiles):
                        return q.matmul(bank[:, 0:512], bB[:],
                                        t_tiles[ft][:], start=True,
                                        stop=True)
                    o_mm = p.add("pe", mmf,
                                 [o_bld, last_t[ft], bank_rd[id(bank)]],
                                 tag="bfmm")
                    bld_rd[jb % 4] = o_mm
                    def sttf(q, ft=ft, bank=bank, bfc=bfc, t_tiles=t_tiles):
                        return q.scalar_tensor_tensor(
                            t_tiles[ft][:], t_tiles[ft][:],
                            bfc[:, ft:ft + 1], bank[:, 0:512],
                            op0=ALU.mult, op1=ALU.add)
                    o_st = p.add("dve", sttf, [o_mm], tag="bfstt")
                    last_t[ft] = o_st
                    bank_rd[id(bank)] = o_st
                    stage_last = o_st
                    jb += 1
            if not down and dh == 0:
                src_b = bias_ab[bias_par]
                dst_b = bias_ab[1 - bias_par]
                bias_par = 1 - bias_par
                if s < 7:
                    def b1(q, bfc=bfc, src_b=src_b):
                        return q.tensor_tensor(btmpB[:], src_b[:],
                                               bfc[:, 32:64], op=ALU.mult)
                    o_b1 = p.add("dve", b1, [o_bfl, bias_tail], tag="bfb1")
                    def bpm(q, msk=msk):
                        return q.tensor_scalar(bld[jb % 4][:], msk, 1.0,
                                               None, op0=ALU.mult)
                    o_bp = p.add("dve", bpm, [bld_rd[jb % 4]], tag="bfbp")
                    bperm = bld[jb % 4]
                    def bmm(q, bperm=bperm):
                        return q.matmul(mk.ps[4][:, 0:32], bperm[:],
                                        btmpB[:], start=True, stop=True)
                    o_bm = p.add("pe", bmm,
                                 [o_b1, o_bp, bank_rd[id(mk.ps[4])]],
                                 tag="bfbm")
                    bld_rd[jb % 4] = o_bm
                    def b23(q, bfc=bfc, src_b=src_b, dst_b=dst_b):
                        q.tensor_tensor(btmpA[:], src_b[:], bfc[:, 0:32],
                                        op=ALU.mult)
                        mk.dve_gap(q)
                        return q.tensor_tensor(dst_b[:], btmpA[:],
                                               mk.ps[4][:, 0:32],
                                               op=ALU.add)
                    o_b23 = p.add("dve", b23, [o_bm, bias_tail], tag="bfb23")
                    bank_rd[id(mk.ps[4])] = o_b23
                    bias_tail = o_b23
                else:
                    K = s_ >> 7
                    a = 32 // (2 * K)
                    sv = src_b[:].rearrange("p (a two c) -> p a two c",
                                            two=2, c=K)
                    bv = bfc[:, 32:64].rearrange("p (a two c) -> p a two c",
                                                 two=2, c=K)
                    tv = btmpB[:].rearrange("p (a two c) -> p a two c",
                                            two=2, c=K)
                    def b1(q, sv=sv, bv=bv, tv=tv):
                        q.tensor_tensor(tv[:, :, 0, :], sv[:, :, 1, :],
                                        bv[:, :, 0, :], op=ALU.mult)
                        return q.tensor_tensor(tv[:, :, 1, :],
                                               sv[:, :, 0, :],
                                               bv[:, :, 1, :], op=ALU.mult)
                    o_b1 = p.add("dve", b1, [o_bfl, bias_tail], tag="bfb1")
                    def b23(q, bfc=bfc, src_b=src_b, dst_b=dst_b):
                        q.tensor_tensor(btmpA[:], src_b[:], bfc[:, 0:32],
                                        op=ALU.mult)
                        mk.dve_gap(q)
                        return q.tensor_tensor(dst_b[:], btmpA[:],
                                               btmpB[:], op=ALU.add)
                    o_b23 = p.add("dve", b23, [o_b1, bias_tail], tag="bfb23")
                    bias_tail = o_b23
            bfl_rd[si % 2] = stage_last if bias_tail is None else bias_tail
            last_dve[si % 2] = stage_last
        prev_stage = [x for x in last_t] + ([bias_tail] if bias_tail else [])
        for b in banks + [mk.ps[4]]:
            mk.bank_last[id(b)] = bank_rd[id(b)]
        if down:
            st_ops = []
            for ft in range(32):
                st_ops.append(p.dma(
                    "pool", scr[ft * 128:(ft + 1) * 128,
                                dh * 512:(dh + 1) * 512],
                    t_tiles[ft][:], deps=[last_t[ft]], tag="bfst"))
            tails = st_ops
            all_st += st_ops[-1:]
        else:
            stg = mk.bf(SM_C + 2048, 4096)
            st_ops = []
            for dtl in range(4):
                tp_ops = []
                for ft in range(32):
                    bank = mk.ps[ft % 2]
                    def tpf(q, ft=ft, dtl=dtl, bank=bank, t_tiles=t_tiles):
                        return q.transpose(bank[:, 0:128],
                                           t_tiles[ft][:, dtl * 128:(dtl + 1) * 128],
                                           mk.ident_f[:])
                    dd = [last_t[ft], bank_rd[id(bank)]] + st_ops[-1:] + \
                        tp_ops[-2:]
                    o_tp = p.add("pe", tpf, dd, tag="bftp")
                    o_cp = p.add("act", lambda q, ft=ft, bank=bank, stg=stg:
                                 q.activation(stg[:, ft * 128:(ft + 1) * 128],
                                              bank[:, 0:128], AF.Copy),
                                 [o_tp], tag="bftc")
                    bank_rd[id(bank)] = o_cp
                    tp_ops.append(o_cp)
                st_ops.append(p.dma(
                    "pool", scr[dh * 512 + dtl * 128: dh * 512 + (dtl + 1) * 128, :],
                    stg[:], deps=tp_ops, tag="bfst"))
            for b in banks:
                mk.bank_last[id(b)] = bank_rd[id(b)]
            tails = st_ops
            all_st += st_ops[-1:]
    if not down:
        def b2f(q):
            return q.tensor_tensor(
                bias_ab[1][:], bias_ab[0][:],
                mk.bias[:, mk.bias_off["l3f_upb"]:mk.bias_off["l3f_upb"] + 32],
                op=ALU.add)
        tails = tails + [p.add("dve", b2f, tails + prev_stage, tag="b2f")]
    return tails


def e_l3_ffn(mk, W, deps, scr_w2, scr_w3, build):
    p = mk.p
    XR = xr_tiles(mk)
    xn = reg_tiles(mk, R0_C)
    ws = mk.ws
    tails = list(deps)
    if build:
        tails = e_bfly_build(mk, W, tails, "up", scr_w2)
        tails = e_bfly_build(mk, W, tails, "down", scr_w3)
        def b3f(q):
            return q.tensor_tensor(
                mk.bcol("l3f_scr", 64, 8),
                mk.bias[:, mk.bias_off["l3f_bcon"]:mk.bias_off["l3f_bcon"] + 8],
                mk.bias[:, mk.bias_off["l3f_dnb"]:mk.bias_off["l3f_dnb"] + 8],
                op=ALU.add)
        tails = tails + [p.add("dve", b3f, tails, tag="b3f")]
    b2col = mk.bcol("l3f_scr", 32, 32)
    b3col = mk.bcol("l3f_scr", 64, 8)
    scr2ap = scr_w2
    scr3ap = scr_w3
    out_tails = []
    hbuf = [mk.bf(R1_C + ft * 512, 512) for ft in range(32)]
    for tb in range(NCH):
        def ev_h(q, bank, mt, nch, tb=tb):
            return q.activation(hbuf[mt][:], bank[:, 0:512], AF.Gelu,
                                bias=b2col[:, mt:mt + 1])
        grid, evs = e_linear_fm(mk, ws, scr2ap, D, 4096, xn, ev_h,
                                tails, [mk.ps[0], mk.ps[1], mk.ps[4]],
                                mcols=512,
                                tag="l3h", nch_range=[tb], src_bf16=True,
                                w_deps=mk.scr_ready.get("w2", ()))
        tbe = mk.f32(SM_C, 512)
        evs2 = []
        ystate = {}
        for s0 in range(8):
            o_ld, slab, slot = ws.load(scr3ap, s0 * 128, 128, 4096,
                                       deps=mk.scr_ready.get("w3", ()),
                                       src_bf16=True)

            def mms(q, bank, slab=slab):
                ii = None
                for kk in range(32):
                    ii = q.matmul(bank[:, 0:512],
                                  slab[:, kk * 128:(kk + 1) * 128],
                                  hbuf[kk][:], start=(kk == 0),
                                  stop=(kk == 31))
                return ii

            def evac(q, bank, mt=s0, tb=tb):
                q.tensor_scalar(tbe[:], bank[:, 0:512],
                                b3col[:, mt:mt + 1], None, op0=ALU.add)
                return q.tensor_tensor(XR[mt][:, tb * 512:(tb + 1) * 512],
                                       XR[mt][:, tb * 512:(tb + 1) * 512],
                                       tbe[:], op=ALU.add)
            sevs = mm_jobs(mk, [dict(mms=mms, deps=[o_ld] + list(evs),
                                     evac=evac, evac_q="dve")],
                           [mk.ps[2], mk.ps[3], mk.ps[5]], tag="l3y",
                           state=ystate)
            ws.release(slot, sevs[-1])
            evs2 += sevs
        out_tails = evs2
        tails = evs2[-2:]
    return out_tails


def e_gated_linear(mk, w_ap, m_ap, K, M, rhs, evac, deps, banks,
                   tag="gl", nch_range=None, m_base=0, evac_q="act"):
    p = mk.p
    ws = mk.ws
    kt = K // 128
    nmt = M // 128
    if nch_range is None:
        nch_range = range(NCH)
    gated = mk.bf(SM_C + 4096, 2048)
    evs = []
    glast = None
    for s0 in range(0, nmt, 2):
        o_lw, slw, slot_w = ws.load(w_ap, m_base + s0 * 128, 256, K,
                                    deps=(), cast=False)
        o_lm, slm, slot_m = ws.load(m_ap, m_base + s0 * 128, 256, K,
                                    deps=(), cast=False)
        g_ops = []
        for kk in range(kt):
            def gf(q, kk=kk, slw=slw, slm=slm):
                return q.scalar_tensor_tensor(
                    gated[:, kk * 256:(kk + 1) * 256],
                    slm[:, kk * 256:(kk + 1) * 256], 0.0,
                    slw[:, kk * 256:(kk + 1) * 256],
                    op0=ALU.is_gt, op1=ALU.mult)
            g_ops.append(p.add("dve", gf,
                               [o_lw, o_lm, glast] + list(deps),
                               tag=tag + "_g"))
        jobs = []
        for mi in range(2):
            mt = s0 + mi
            if mt >= nmt:
                continue
            for nch in nch_range:
                def mms(q, bank, mi=mi, n=nch):
                    ii = None
                    for kk in range(kt):
                        ii = q.matmul(bank[:, 0:512],
                                      gated[:, kk * 256 + mi * 128:
                                            kk * 256 + mi * 128 + 128],
                                      rhs[kk][:, n * 512:(n + 1) * 512],
                                      start=(kk == 0), stop=(kk == kt - 1))
                    return ii
                jobs.append(dict(
                    mms=mms, deps=g_ops,
                    evac=(lambda q, bank, mt=mt, n=nch: evac(q, bank, mt, n)),
                    evac_q=evac_q))
        jevs = mm_jobs(mk, jobs, banks, tag=tag)
        glast = jevs[-1]
        ws.release(slot_w, jevs[-1])
        ws.release(slot_m, jevs[-1])
        evs += jevs
    return evs


def e_l3_attn(mk, W, deps, scr_xr):
    p = mk.p
    XR = xr_tiles(mk)
    xn = reg_tiles(mk, R0_C)
    H = 16

    def gbf(q):
        return q.scalar_tensor_tensor(
            mk.bcol("scratch", 8, 8), mk.bcol("l3_obm", 0, 8), 0.0,
            mk.bcol("l3_ob", 0, 8), op0=ALU.is_gt, op1=ALU.mult)
    o_gb = p.add("dve", gbf, deps, tag="l3gb")

    qf = reg_tiles(mk, R1_C)
    kf = reg_tiles(mk, R2_C)
    def evq(q, bank, mt, nch):
        return q.activation(qf[mt][:, nch * 512:(nch + 1) * 512],
                            bank[:, 0:512], AF.Copy)
    def evk(q, bank, mt, nch):
        return q.activation(kf[mt][:, nch * 512:(nch + 1) * 512],
                            bank[:, 0:512], AF.Copy)
    evs_q = e_gated_linear(mk, W["l3_qkv_t"], W["l3_qkvm_t"], D, D, xn, evq,
                           deps, [mk.ps[0], mk.ps[1], mk.ps[2], mk.ps[3]],
                           tag="l3q", m_base=0)
    evs_k = e_gated_linear(mk, W["l3_qkv_t"], W["l3_qkvm_t"], D, D, xn, evk,
                           evs_q[-1:], [mk.ps[0], mk.ps[1], mk.ps[2],
                                        mk.ps[3]], tag="l3k",
                           m_base=D)

    sp_ops = []
    for dt in range(DT):
        sp_ops.append(p.dma("sp", scr_xr[dt * 128:(dt + 1) * 128, :], XR[dt],
                            deps=deps, tag="xrspill"))

    vext = [mk.bf(XR_C + tt * 1040, 1040) for tt in range(16)]
    ones_ops = []
    for tt in range(16):
        dst = vext[tt].rearrange("pp (h c) -> pp h c", h=16)[:, :, 64:65]
        ones_ops.append(p.add(
            "dve", lambda q, dst=dst: q.memset(dst, 1.0),
            sp_ops, tag="vones"))
    v_evs = []
    vstate = {}
    vlast = None
    gated = mk.bf(SM_C + 4096, 2048)
    ws = mk.ws
    for s0 in range(4):
        o_lw, slw, slot_w = ws.load(W["l3_qkv_t"], 2 * D + s0 * 256, 256, D,
                                    deps=(), cast=False)
        o_lm, slm, slot_m = ws.load(W["l3_qkvm_t"], 2 * D + s0 * 256, 256, D,
                                    deps=(), cast=False)
        g_ops = []
        for kk in range(DT):
            def gf(q, kk=kk, slw=slw, slm=slm):
                return q.scalar_tensor_tensor(
                    gated[:, kk * 256:(kk + 1) * 256],
                    slm[:, kk * 256:(kk + 1) * 256], 0.0,
                    slw[:, kk * 256:(kk + 1) * 256],
                    op0=ALU.is_gt, op1=ALU.mult)
            g_ops.append(p.add("dve", gf,
                               [o_lw, o_lm, vlast] + list(evs_k[-1:]),
                               tag="l3v_g"))
        jobs = []
        for tt in range(16):
            def mms(q, bank, tt=tt):
                ii = None
                for kk in range(DT):
                    ii = q.matmul(bank[:, 0:256],
                                  xn[kk][:, tt * 128:(tt + 1) * 128],
                                  gated[:, kk * 256:(kk + 1) * 256],
                                  start=(kk == 0), stop=(kk == DT - 1))
                return ii
            h0 = (s0 * 256) // 64
            dst = vext[tt].rearrange("pp (h c) -> pp h c", h=16)[
                :, h0:h0 + 4, 0:64]
            def evac(q, bank, tt=tt, dst=dst):
                return q.tensor_copy(
                    dst, bank[:, 0:256].rearrange("pp (a c) -> pp a c", a=4))
            jobs.append(dict(mms=mms, deps=g_ops + sp_ops + ones_ops,
                             evac=evac, evac_q="dve"))
        jevs = mm_jobs(mk, jobs, [mk.ps[0], mk.ps[1], mk.ps[2], mk.ps[3]],
                       tag="l3v",
                       state=vstate)
        vlast = jevs[-1]
        ws.release(slot_w, jevs[-1])
        ws.release(slot_m, jevs[-1])
        v_evs += jevs

    esc = [mk.bf(XR_C + 16640 + kt * 512, 512) for kt in range(16)]
    ctx = reg_tiles(mk, R0_C)
    ctx_evs = []
    sbanks = [mk.ps[0], mk.ps[1], mk.ps[2], mk.ps[3]]
    lastexp = {}
    lastctx = [None, None]
    lastrep = [None, None]
    last_cmm = None
    prev_cev = None
    prev_cevs = [None, None]
    rcprs = [mk.rows[0:1, 0:512], mk.rows2[0:1, 0:512]]
    reps = [mk.f32(SM_C, 512, rows=64), mk.f32(SM_C + 1024, 512, rows=64)]
    repbanks = [mk.ps[6], mk.ps[7]]
    pend = [None]

    def flush3(pe_):
        if pe_ is None:
            return
        h, qb, cb, r0, par, o_cmm, o_rcp = pe_
        rcpr, rep, repbank = rcprs[par], reps[par], repbanks[par]
        o_rep = p.add("pe", lambda q, rcpr=rcpr, repbank=repbank:
                      q.matmul(repbank[0:64, 0:512],
                               mk.ones_row_f[0:1, 0:64], rcpr,
                               start=True, stop=True),
                      [o_rcp, prev_cevs[par]], tag="l3rep")
        o_repc = p.add("act", lambda q, rep=rep, repbank=repbank:
                       q.activation(rep[:], repbank[0:64, 0:512],
                                    AF.Copy), [o_rep], tag="l3repc")
        lastrep[par] = o_repc

        def cev(q, h=h, qb=qb, cb=cb, r0=r0, rep=rep):
            return q.tensor_tensor(
                ctx[h // 2][r0:r0 + 64, qb * 512:(qb + 1) * 512],
                cb[0:64, 0:512], rep[:], op=ALU.mult)
        o_cev = p.add("dve", cev, [o_repc, o_cmm] + v_evs[-1:],
                      tag="l3cev")
        lastctx[qb % 2] = o_cev
        prev_cevs[par] = o_cev
        ctx_evs.append(o_cev)

    for h in range(H):
        r0 = (h % 2) * 64
        for qb in range(NCH):
            par = (h * NCH + qb) % 2
            rcpr = rcprs[par]
            exp_ops = []
            for kt in range(16):
                b = sbanks[kt % 4]
                def smm(q, bank, h=h, qb=qb, kt=kt, r0=r0):
                    return q.matmul(
                        bank[:, 0:512],
                        kf[h // 2][r0:r0 + 64, kt * 128:(kt + 1) * 128],
                        qf[h // 2][r0:r0 + 64, qb * 512:(qb + 1) * 512],
                        start=True, stop=True, tile_position=(r0, 0))
                dd = [v_evs[-1], lastexp.get(kt % 4), last_cmm,
                      evs_q[-1], evs_k[-1]]
                o_smm = p.add("pe", (lambda q, f=smm, bb=b: f(q, bb)), dd,
                              tag="l3smm")
                def sexp(q, bank, kt=kt):
                    return q.activation(esc[kt][:], bank[:, 0:512], AF.Exp,
                                        scale=1.0 / 32.0)
                o_exp = p.add("act", (lambda q, f=sexp, bb=b: f(q, bb)),
                              [o_smm], tag="l3exp")
                lastexp[kt % 4] = o_exp
                exp_ops.append(o_exp)
            flush3(pend[0])
            pend[0] = None
            cb = mk.ps[4 + (qb % 2)]
            o_cmm = None
            for kt in range(16):
                def cmm1(q, h=h, cb=cb, kt=kt):
                    return q.matmul(cb[0:65, 0:512],
                                    vext[kt][:, h * 65:h * 65 + 65],
                                    esc[kt][:], start=(kt == 0),
                                    stop=(kt == 15), skip_group_check=True)
                dd = [exp_ops[kt]] + ([lastctx[qb % 2]] + v_evs[-1:]
                                      if kt == 0 else [])
                o_cmm = p.add("pe", cmm1, dd, tag="l3cmm")
            last_cmm = o_cmm
            o_rcp = p.add("dve", lambda q, cb=cb, rcpr=rcpr: q.reciprocal(
                rcpr, cb[64:65, 0:512]), [o_cmm, lastrep[par]], tag="l3rcp")
            pend[0] = (h, qb, cb, r0, par, o_cmm, o_rcp)
    flush3(pend[0])
    pend[0] = None

    rs_ops = []
    for dt in range(DT):
        rs_ops.append(p.dma("sp", XR[dt], scr_xr[dt * 128:(dt + 1) * 128, :],
                            deps=ctx_evs, tag="xrrest"))

    tbo = mk.f32(SM_C, 512)
    def evo(q, bank, mt, nch):
        q.tensor_scalar(tbo[:], bank[:, 0:512],
                        mk.bcol("scratch", 8 + mt, 1), None, op0=ALU.add)
        return q.tensor_tensor(XR[mt][:, nch * 512:(nch + 1) * 512],
                               XR[mt][:, nch * 512:(nch + 1) * 512],
                               tbo[:], op=ALU.add)
    evs_o = e_gated_linear(mk, W["l3_out_t"], W["l3_outm_t"], D, D, ctx, evo,
                           rs_ops + ctx_evs + [o_gb],
                           [mk.ps[0], mk.ps[1], mk.ps[2], mk.ps[3]],
                           tag="l3o", evac_q="dve")
    return evs_o


def e_layer3(mk, W, deps, stage="all"):
    ln_ops = e_layernorm(mk, deps)
    scr_w2 = mk.scr["w2"]
    scr_w3 = mk.scr["w3"]
    scr_xr = mk.scr["xr"]
    ffn_tails = e_l3_ffn(mk, W, ln_ops[-8:], scr_w2, scr_w3,
                         build=not mk.l3_built)
    mk.l3_built = True
    if stage == "ffn":
        return ffn_tails
    return e_l3_attn(mk, W, ffn_tails, scr_xr)





def e_ln_l0ffn(mk, deps, xn_c0=R0_C):
    p = mk.p
    XR = xr_tiles(mk)
    xn = reg_tiles(mk, xn_c0)
    sqb = [mk.f32(SM_C + i * 1024, 512) for i in range(2)]
    ntmp = [mk.f32(SM_C + 4096 + i * 1024, 512) for i in range(2)]
    mrep = mk.f32(SM_C + 2048, 512)
    rrep = mk.f32(SM_C + 3072, 512)
    svar = mk.f32(SM_C + 6144 - 1024, 512) if False else None
    out_ops = []
    ps1, ps2 = mk.ps[6], mk.ps[7]
    prev_chunk = None
    for nch in range(NCH):
        varb = ntmp[1]
        mm_ops = []
        for dt in range(DT):
            buf = sqb[dt % 2]
            dd = list(deps) + [mm_ops[-2] if len(mm_ops) >= 2 else prev_chunk]
            o_sq = p.add("act", lambda q, b=buf, d=dt, n=nch: q.activation(
                b[:], XR[d][:, n * 512:(n + 1) * 512], AF.Square), dd,
                tag="l0n_sq")
            def mmfn(q, b=buf, d=dt, n=nch):
                q.matmul(ps2[:, 0:512], mk.ones_f[:], b[:],
                         start=(d == 0), stop=(d == DT - 1))
                return q.matmul(ps1[:, 0:512], mk.ones_f[:],
                                XR[d][:, n * 512:(n + 1) * 512],
                                start=(d == 0), stop=(d == DT - 1))
            o_mm = p.add("pe", mmfn, list(deps) + [o_sq, prev_chunk],
                         tag="l0n_mm")
            mm_ops.append(o_mm)

        def statfn(q, varb=varb):
            q.tensor_scalar(mrep[:], ps1[:, 0:512], 1.0 / D, None,
                            op0=ALU.mult)
            q.tensor_tensor(varb[:], mrep[:], mrep[:], op=ALU.mult)
            q.scalar_tensor_tensor(varb[:], ps2[:, 0:512], 1.0 / D, varb[:],
                                   op0=ALU.mult, op1=ALU.subtract)
            q.tensor_scalar(rrep[:], varb[:], 1e-5, None, op0=ALU.add)
            return q.reciprocal(rrep[:], rrep[:])
        o_st = p.add("dve", statfn, [mm_ops[-1]], tag="l0n_st")
        o_rs = p.add("act", lambda q: q.activation(
            rrep[:], rrep[:], AF.Sqrt), [o_st], tag="l0n_rs")
        o_sv = p.add("act", lambda q, varb=varb: q.activation(
            varb[:], varb[:], AF.Sqrt), [o_rs], tag="l0n_sv")

        def sfn(q, varb=varb):
            q.tensor_tensor(varb[:], varb[:], rrep[:], op=ALU.mult)
            q.tensor_scalar(varb[:], varb[:], 1e-8, None, op0=ALU.add)
            q.reciprocal(varb[:], varb[:])
            return q.tensor_tensor(rrep[:], rrep[:], varb[:], op=ALU.mult)
        o_sf = p.add("dve", sfn, [o_sv], tag="l0n_sf")
        for dt in range(DT):
            def nfn(q, d=dt, n=nch):
                q.tensor_tensor(ntmp[0][:], XR[d][:, n * 512:(n + 1) * 512],
                                mrep[:], op=ALU.subtract)
                return q.tensor_tensor(xn[d][:, n * 512:(n + 1) * 512],
                                       ntmp[0][:], rrep[:], op=ALU.mult)
            out_ops.append(p.add("dve", nfn, [o_sf], tag="l0n_n"))
        prev_chunk = out_ops[-1]
    return out_ops


def e_l0_attn(mk, W, deps, scr_xr, until="all"):
    p = mk.p
    XR = xr_tiles(mk)
    xn = reg_tiles(mk, R0_C)
    ws = mk.ws
    H = 16

    sp_ops = [p.dma("sp", scr_xr[dt * 128:(dt + 1) * 128, :], XR[dt],
                    deps=deps, tag="l0spill") for dt in range(DT)]
    cosb = mk.bf(XR_C + 24832, 2048)
    sinb = mk.bf(XR_C + 26880, 2048)
    trilb = mk.bf(XR_C + 28928, 2048)
    o_cos = p.dma("pool", cosb, W["rope_cos"], deps=sp_ops, tag="l0rope")
    o_sin = p.dma("pool", sinb, W["rope_sin"], deps=sp_ops, tag="l0rope")
    o_tril = p.dma("pool", trilb, W["tril4"], deps=sp_ops, tag="l0tril")

    latb = mk.bf(SM_C, 2048)
    def evl(q, bank, mt, nch):
        return q.activation(latb[:, nch * 512:(nch + 1) * 512],
                            bank[:, 0:512], AF.Copy)
    grid_l, evs_l = e_linear_fm(mk, ws, W["l0_wkvd_t"], D, 128, xn, evl,
                                deps, [mk.ps[0], mk.ps[1], mk.ps[2],
                                       mk.ps[3]], mcols=128,
                                tag="l0lat")

    if until == "lat":
        return evs_l
    rt = mk.f32(SM_C + 4096, 512)
    rsh = mk.f32(SM_C + 5120, 512)

    def rope_evac(dst_tiles):
        def ev(q, bank, mt, nch):
            q.tensor_tensor(rt[:], bank[:, 0:512],
                            cosb[:, nch * 512:(nch + 1) * 512], op=ALU.mult)
            for a, b in ((0, 32), (32, 0), (64, 96), (96, 64)):
                q.tensor_copy(rsh[a:a + 32, :], bank[b:b + 32, 0:512])
            q.tensor_tensor(rsh[:], rsh[:],
                            sinb[:, nch * 512:(nch + 1) * 512], op=ALU.mult)
            return q.tensor_tensor(
                dst_tiles[mt][:, nch * 512:(nch + 1) * 512],
                rt[:], rsh[:], op=ALU.add)
        return ev

    qf = reg_tiles(mk, R1_C)
    kf = reg_tiles(mk, R2_C)
    grid_q, evs_q = e_linear_fm(mk, ws, W["l0_wq_t"], D, D, xn,
                                rope_evac(qf), [o_cos, o_sin] + evs_l[-1:],
                                [mk.ps[0], mk.ps[1], mk.ps[2], mk.ps[3]],
                                evac_q="dve",
                                tag="l0q")
    if until == "q":
        return evs_q
    grid_k, evs_k = e_linear_fm(mk, ws, W["l0_wkvu_t"], 128, D, [latb],
                                rope_evac(kf), evs_q[-1:],
                                [mk.ps[0], mk.ps[1], mk.ps[2], mk.ps[3]],
                                evac_q="dve",
                                tag="l0k")
    if until == "k":
        return evs_k

    vext = [mk.bf(XR_C + tt * 1040, 1040) for tt in range(16)]
    ones_ops = []
    if until not in ("v0", "v1"):
        for tt in range(16):
            dst = vext[tt].rearrange("pp (h c) -> pp h c", h=16)[:, :, 64:65]
            ones_ops.append(p.add("dve",
                                  lambda q, dst=dst: q.memset(dst, 1.0),
                                  sp_ops, tag="l0vones"))
    v_evs = []
    vstate = {}
    nhalf = 1 if until == "v0" else 2
    no_ones = until in ("v0", "v1")
    for half in range(nhalf):
        o_ld, slab, slot = ws.load(W["l0_wkvu_t"], D + half * 512, 512, 128,
                                   deps=())
        jobs = []
        for tt in range(16):
            def mms(q, bank, tt=tt, slab=slab):
                return q.matmul(bank[:, 0:512],
                                latb[:, tt * 128:(tt + 1) * 128],
                                slab[:, 0:512], start=True, stop=True)
            h0 = half * 8
            dst = vext[tt].rearrange("pp (h c) -> pp h c", h=16)[
                :, h0:h0 + 8, 0:64]
            def evac(q, bank, tt=tt, dst=dst):
                return q.tensor_copy(
                    dst, bank[:, 0:512].rearrange("pp (a c) -> pp a c", a=8))
            jobs.append(dict(mms=mms,
                             deps=[o_ld] + list(evs_k[-2:]) +
                             ([] if no_ones else ones_ops),
                             evac=evac, evac_q="dve"))
        jevs = mm_jobs(mk, jobs, [mk.ps[0], mk.ps[1], mk.ps[2], mk.ps[3]],
                       tag="l0v",
                       state=vstate)
        ws.release(slot, jevs[-1])
        v_evs += jevs
    if until in ("v0", "v1"):
        return v_evs

    if until == "qkv":
        return v_evs
    scor = mk.bf(SM_C + 1600, 8, rows=128)
    o_lds = p.dma("pool", scor, W["l0_scorer_kt"], deps=v_evs[-1:],
                  tag="l0scor")

    def smmf(q):
        ii = None
        for nchi in range(NCH):
            for kk in range(DT):
                ii = q.matmul(mk.ps[4 + nchi][0:1, 0:512],
                              scor[:, kk:kk + 1],
                              xn[kk][:, nchi * 512:(nchi + 1) * 512],
                              start=(kk == 0), stop=(kk == DT - 1))
        return ii
    o_smm = p.add("pe", smmf, [o_lds], tag="l0stok")

    prow = mk.f32(SM_C, 512, rows=1)
    srow = mk.f32(SM_C + 1024, 32, rows=1)
    scol = mk.f32(SM_C + 1200, 1, rows=32)
    g1 = mk.f32(SM_C + 1280, 32, rows=32)
    g2 = mk.f32(SM_C + 1408, 32, rows=32)
    mcol = mk.f32(SM_C + 1536, 1, rows=32)

    def poolf(q):
        ii = None
        for nchi in range(NCH):
            q.tensor_copy(prow[:], mk.ps[4 + nchi][0:1, 0:512])
            c = 64
            while c > 1:
                h_ = c // 2
                src3 = prow[0:1, 0:8 * c].rearrange("r (b c) -> r b c", c=c)
                dst3 = prow[0:1, 0:8 * h_].rearrange("r (b c) -> r b c", c=h_)
                if c <= 16:
                    mk.dve_gap(q)
                q.tensor_tensor(dst3, src3[:, :, 0:h_], src3[:, :, h_:c],
                                op=ALU.add)
                c = h_
            mk.dve_gap(q)
            ii = q.tensor_copy(srow[0:1, nchi * 8:nchi * 8 + 8],
                               prow[0:1, 0:8])
            mk.dve_gap(q)
        return ii
    o_pool = p.add("dve", poolf, [o_smm], tag="l0pool")
    o_st = p.dma("sp", mk.scr["row"][0:1, 0:32], srow, deps=[o_pool],
                 tag="l0srowst")
    o_scol = p.dma("sp", scol,
                   mk.scr["row"][0:1, 0:32].rearrange(
                       "r (pp c) -> pp (r c)", c=1),
                   deps=[o_st], tag="l0scolld", slow=True)
    o_srep = p.add("pe", lambda q: q.matmul(
        mk.ps[7][0:32, 0:32], mk.ones_row_f[0:1, 0:32], srow,
        start=True, stop=True), [o_pool, o_scol], tag="l0srep")

    def rankf(q):
        q.tensor_scalar(g1[:], mk.ps[7][0:32, 0:32], scol, None,
                        op0=ALU.is_gt)
        q.tensor_scalar(g2[:], mk.ps[7][0:32, 0:32], scol, None,
                        op0=ALU.is_equal)
        mk.dve_gap(q)
        q.tensor_tensor(g2[:], g2[:], mk.sel[0:32, 144:176], op=ALU.mult)
        mk.dve_gap(q)
        q.tensor_tensor(g1[:], g1[:], g2[:], op=ALU.add)
        mk.dve_gap(q)
        q.tensor_reduce(mcol, g1[:], axis=mybir.AxisListType.X, op=ALU.add)
        mk.dve_gap(q)
        q.tensor_scalar(mcol, mcol, 4.0, NEG, op0=ALU.is_ge, op1=ALU.mult)
        mk.dve_gap(q)
        return q.tensor_scalar(mk.selbf[0:32, 0:16], mk.sel[0:32, 128:144],
                               mcol, None, op0=ALU.mult)
    o_rank = p.add("dve", rankf, [o_srep], tag="l0rank")
    o_km = p.add("pe", lambda q: q.matmul(
        mk.ps[7][:, 0:16], mk.selA[0:32, 0:128], mk.selbf[0:32, 0:16],
        start=True, stop=True), [o_rank], tag="l0km")
    kmask = mk.bcol("l0_scr", 0, 16)
    o_kmc = p.add("dve", lambda q: q.tensor_copy(kmask[:],
                                                 mk.ps[7][:, 0:16]),
                  [o_km], tag="l0kmc")

    if until == "topk":
        return v_evs + [o_kmc]
    esc = [mk.bf(XR_C + 16640 + kt * 512, 512) for kt in range(16)]
    ctx = qf
    diagb = mk.f32(SM_C + 4096, 512)
    invrep = mk.f32(SM_C, 512, rows=64)
    deadrep = mk.f32(SM_C + 1024, 512, rows=64)
    mvcols = [mk.bias[0:64, mk.bias_off["l0_scr"] + 16 + i:
                      mk.bias_off["l0_scr"] + 17 + i] for i in range(2)]
    ctx_evs = []
    sbanks = [mk.ps[0], mk.ps[1], mk.ps[2], mk.ps[3]]
    lastexp = {}
    lastctx = [None, None]
    last_cmm = None
    prev_cev = None
    lastrep = None
    last_dgexp = [None]
    o_mvc = None
    pend = [None]
    state = dict(prev_cev=None, lastrep=None, o_mvc=None)

    def flush0(pe_):
        if pe_ is None:
            return
        h, qb, cb, r0, o_cmm, o_inv, mvcol_h = pe_
        def repf(q):
            q.matmul(mk.ps[6][0:64, 0:512], mk.ones_row_f[0:1, 0:64],
                     mk.rows[0:1, 0:512], start=True, stop=True)
            return q.matmul(mk.ps[7][0:64, 0:512],
                            mk.ones_row_f[0:1, 0:64],
                            mk.rows2[0:1, 0:512], start=True, stop=True)
        o_rep = p.add("pe", repf,
                      [o_inv, state["prev_cev"], state["o_mvc"]],
                      tag="l0rep")

        def repc(q):
            q.activation(invrep[:], mk.ps[6][0:64, 0:512], AF.Copy)
            return q.activation(deadrep[:], mk.ps[7][0:64, 0:512],
                                AF.Copy)
        o_repc = p.add("act", repc, [o_rep], tag="l0repc")
        state["lastrep"] = o_repc

        def cev(q, h=h, qb=qb, cb=cb, r0=r0, mvcol_h=mvcol_h):
            q.tensor_tensor(invrep[:], invrep[:], cb[0:64, 0:512],
                            op=ALU.mult)
            return q.scalar_tensor_tensor(
                ctx[h // 2][r0:r0 + 64, qb * 512:(qb + 1) * 512],
                deadrep[:], mvcol_h, invrep[:],
                op0=ALU.mult, op1=ALU.add)
        o_cev = p.add("dve", cev, [o_repc, o_cmm], tag="l0cev")
        lastctx[qb % 2] = o_cev
        state["prev_cev"] = o_cev
        ctx_evs.append(o_cev)

    for h in range(H):
        r0 = (h % 2) * 64
        mvcol = mvcols[h % 2]

        def mvf(q, h=h):
            ii = None
            for tt in range(16):
                ii = q.matmul(mk.ps[6][0:65, 0:1],
                              vext[tt][:, h * 65:h * 65 + 65],
                              mk.ones_bf[:, 0:1], start=(tt == 0),
                              stop=(tt == 15))
            return ii
        o_mv = p.add("pe", mvf, [v_evs[-1], state["o_mvc"],
                                 state["lastrep"]], tag="l0mv")
        o_mvc = p.add("act", lambda q, mvcol=mvcol: q.activation(
            mvcol, mk.ps[6][0:64, 0:1], AF.Identity, scale=1.0 / 2048.0),
            [o_mv, state["prev_cev"]], tag="l0mvc")
        state["o_mvc"] = o_mvc
        for qb in range(NCH):
            nkt = qb * 4 + 4
            exp_ops = []
            for kt in range(nkt):
                b = sbanks[kt % 4]
                def smm(q, bank, h=h, qb=qb, kt=kt, r0=r0):
                    return q.matmul(
                        bank[:, 0:512],
                        kf[h // 2][r0:r0 + 64, kt * 128:(kt + 1) * 128],
                        qf[h // 2][r0:r0 + 64, qb * 512:(qb + 1) * 512],
                        start=True, stop=True, tile_position=(r0, 0))
                dd = [v_evs[-1], lastexp.get(kt % 4), last_cmm, o_kmc]
                o_smm = p.add("pe", (lambda q, f=smm, bb=b: f(q, bb)), dd,
                              tag="l0smm")
                if kt >= qb * 4:
                    o_dg = p.add(
                        "dve", lambda q, bb=b, o=kt - qb * 4:
                        q.tensor_tensor(diagb[:], bb[:, 0:512],
                                        trilb[:, o * 512:(o + 1) * 512],
                                        op=ALU.add),
                        [o_smm, o_tril, last_dgexp[0]], tag="l0diag")
                    def sexp(q, kt=kt):
                        return q.activation(esc[kt][:], diagb[:], AF.Exp,
                                            bias=kmask[:, kt:kt + 1],
                                            scale=0.125)
                    o_exp = p.add("act", sexp, [o_dg], tag="l0exp")
                    last_dgexp[0] = o_exp
                else:
                    def sexp(q, bank, kt=kt):
                        return q.activation(esc[kt][:], bank[:, 0:512],
                                            AF.Exp, bias=kmask[:, kt:kt + 1],
                                            scale=0.125)
                    o_exp = p.add("act", (lambda q, f=sexp, bb=b: f(q, bb)),
                                  [o_smm], tag="l0exp")
                lastexp[kt % 4] = o_exp
                exp_ops.append(o_exp)
            flush0(pend[0])
            pend[0] = None
            cb = mk.ps[4 + (qb % 2)]
            o_cmm = None
            for kt in range(nkt):
                def cmm1(q, h=h, cb=cb, kt=kt, nkt=nkt):
                    return q.matmul(cb[0:65, 0:512],
                                    vext[kt][:, h * 65:h * 65 + 65],
                                    esc[kt][:], start=(kt == 0),
                                    stop=(kt == nkt - 1),
                                    skip_group_check=True)
                dd = [exp_ops[kt]] + ([lastctx[qb % 2], v_evs[-1]]
                                      if kt == 0 else [])
                o_cmm = p.add("pe", cmm1, dd, tag="l0cmm")
            last_cmm = o_cmm

            def invf(q, cb=cb):
                q.tensor_scalar(mk.rows2[0:1, 0:512], cb[64:65, 0:512],
                                0.0, None, op0=ALU.is_equal)
                q.tensor_tensor(mk.rows[0:1, 0:512], cb[64:65, 0:512],
                                mk.rows2[0:1, 0:512], op=ALU.add)
                return q.reciprocal(mk.rows[0:1, 0:512],
                                    mk.rows[0:1, 0:512])
            o_inv = p.add("dve", invf, [o_cmm, state["lastrep"]],
                          tag="l0inv")
            pend[0] = (h, qb, cb, r0, o_cmm, o_inv, mvcol)
    flush0(pend[0])
    pend[0] = None

    return ctx_evs


def e_l0_outproj(mk, W, deps):
    p = mk.p
    XR = xr_tiles(mk)
    ctx = reg_tiles(mk, R1_C)

    def evo(q, bank, mt, nch):
        return q.tensor_tensor(XR[mt][:, nch * 512:(nch + 1) * 512],
                               XR[mt][:, nch * 512:(nch + 1) * 512],
                               bank[:, 0:512], op=ALU.add)
    grid_o, evs_o = e_linear_fm(mk, mk.ws, W["l0_wout_t"], D, D, ctx, evo,
                                deps, [mk.ps[0], mk.ps[1], mk.ps[2],
                                       mk.ps[3]],
                                evac_q="dve", tag="l0out")
    return evs_o


def _tern_strips(wt_ap, K, M):
    out = []
    for kt in range(K // 128):
        for half in range(0, M, 2048):
            ncols = min(2048, M - half)
            out.append((kt, half, ncols,
                        wt_ap[kt * 128:(kt + 1) * 128, half:half + ncols]))
    return out


def _interleave(a, b):
    out, ia, ib = [], 0, 0
    na, nb = len(a), len(b)
    while ia < na or ib < nb:
        if ib * na <= ia * nb and ib < nb:
            out.append(("b", b[ib])); ib += 1
        else:
            out.append(("a", a[ia])); ia += 1
    return out


def e_tern_delta2(mk, W, deps, d_up, d_dn, scale):
    p = mk.p
    ws = mk.ws
    cfg = {
        "a": dict(absb=[mk.f32(R2_C + 4096 + i * 1024, 512)
                        for i in range(2)],
                  psA=mk.ps[6], psR=mk.ps[7], dsum=mk.rows[0:1, 0:1],
                  dcol2=d_up, mm_ops=[], k=0,
                  npieces=(D // 128) * (4096 // 512)),
        "b": dict(absb=[mk.f32(R2_C + 6144 + i * 1024, 512)
                        for i in range(2)],
                  psA=mk.ps[4], psR=mk.ps[5], dsum=mk.rows2[0:1, 0:1],
                  dcol2=d_dn, mm_ops=[], k=0,
                  npieces=(4096 // 128) * (D // 512)),
    }
    merged = _interleave(_tern_strips(W["l0_wup_t"], D, 4096),
                         _tern_strips(W["l0_wdn_t"], 4096, D))
    first = True
    for which, (kt, half, ncols, src) in merged:
        c = cfg[which]
        o_ld, strip, slot = ws.load_raw(src, ncols, deps=())
        for j in range(ncols // 512):
            buf = c["absb"][c["k"] % 2]
            dd = [o_ld, c["mm_ops"][-2] if len(c["mm_ops"]) >= 2 else None]
            if first:
                dd += list(deps)
                first = False
            o_abs = p.add("act", lambda q, b=buf, s_=strip, j=j:
                          q.activation(b[:], s_[:, j * 512:(j + 1) * 512],
                                       AF.Abs), dd, tag="tdabs")
            o_mm = p.add("pe", lambda q, b=buf, c=c,
                         first_=(c["k"] == 0),
                         last=(c["k"] == c["npieces"] - 1):
                         q.matmul(c["psA"][0:1, 0:512], mk.ones_f[:, 0:1],
                                  b[:], start=first_, stop=last),
                         [o_abs] + (list(deps) if c["k"] == 0 else []),
                         tag="tdmm")
            c["mm_ops"].append(o_mm)
            c["k"] += 1
        ws.release(slot, c["mm_ops"][-1])
    outs = []
    for which in ("a", "b"):
        c = cfg[which]
        o_red = p.add("dve", lambda q, c=c: q.tensor_reduce(
            c["dsum"], c["psA"][0:1, 0:512], axis=mybir.AxisListType.X,
            op=ALU.add), [c["mm_ops"][-1]] + list(deps), tag="tdred")
        o_drep = p.add("pe", lambda q, c=c: q.matmul(
            c["psR"][:, 0:1], mk.ones_row_f[:], c["dsum"],
            start=True, stop=True), [o_red], tag="tdrep")

        def dcols(q, c=c):
            q.tensor_scalar(c["dcol2"][:, 0:1], c["psR"][:, 0:1], scale,
                            None, op0=ALU.mult)
            return q.tensor_scalar(c["dcol2"][:, 1:2], c["psR"][:, 0:1],
                                   -scale, None, op0=ALU.mult)
        outs.append(p.add("dve", dcols, [o_drep], tag="tdcol"))
    return outs


def e_tern_write2(mk, W, d_up, d_dn, scr_up, scr_dn, deps):
    p = mk.p
    ws = mk.ws
    cfg = {
        "a": dict(tern=[mk.bf(R2_C + i * 512, 512) for i in range(2)],
                  prev_t=[None, None], dcol2=d_up, scr=scr_up, tiled=False,
                  st=[], q="dve", gpos=mk.f32(R2_C + 2048, 512),
                  gneg=mk.f32(R2_C + 3072, 512)),
        "b": dict(tern=[mk.bf(R2_C + 1024 + i * 512, 512)
                        for i in range(2)],
                  prev_t=[None, None], dcol2=d_dn, scr=scr_dn, tiled=True,
                  st=[], q="dve", gpos=mk.f32(R2_C + 8192, 512),
                  gneg=mk.f32(R2_C + 9216, 512)),
    }
    merged = _interleave(_tern_strips(W["l0_wup_t"], D, 4096),
                         _tern_strips(W["l0_wdn_t"], 4096, D))
    for which, (kt, half, ncols, src) in merged:
        c = cfg[which]
        o_ld, strip, slot = ws.load_raw(src, ncols, deps=())
        last = None
        for j in range(ncols // 512):
            tb = c["tern"][j % 2]
            def tf(q, s_=strip, j=j, tb=tb, c=c):
                srcv = s_[:, j * 512:(j + 1) * 512]
                q.tensor_scalar(c["gpos"][:], srcv, c["dcol2"][:, 0:1],
                                None, op0=ALU.is_gt)
                q.tensor_scalar(c["gneg"][:], srcv, c["dcol2"][:, 1:2],
                                None, op0=ALU.is_lt)
                return q.tensor_tensor(tb[:], c["gpos"][:], c["gneg"][:],
                                       op=ALU.subtract)
            o_t = p.add(c["q"], tf, [o_ld, c["prev_t"][j % 2]] + list(deps),
                        tag="twt")
            m0 = half + j * 512
            if not c["tiled"]:
                o_st = p.dma("sp", c["scr"][kt * 128:(kt + 1) * 128,
                                            m0:m0 + 512], tb[:],
                             deps=[o_t], tag="twst")
            else:
                dst = c["scr"][m0 // 128:m0 // 128 + 4, :,
                               kt * 128:(kt + 1) * 128]
                o_st = p.dma("sp", dst.rearrange("a pp m -> pp a m"),
                             tb[:].rearrange("pp (a m) -> pp a m", a=4),
                             deps=[o_t], tag="twst")
            c["prev_t"][j % 2] = o_st
            c["st"].append(o_st)
            last = o_st
        ws.release(slot, last)
    return cfg["a"]["st"], cfg["b"]["st"]


def _tern_dcols(mk):
    d_up = mk.bias[:, mk.bias_off["l0_scr"] + 20:mk.bias_off["l0_scr"] + 22]
    d_dn = mk.bias[:, mk.bias_off["l0_scr"] + 22:mk.bias_off["l0_scr"] + 24]
    return d_up, d_dn


def e_l0_tern_delta_phase(mk, W, deps):
    d_up, d_dn = _tern_dcols(mk)
    o_d1, o_d2 = e_tern_delta2(mk, W, deps, d_up, d_dn, 0.5 / (D * 4096))
    mk.scr_ready["l0delta"] = [o_d1, o_d2]
    return [o_d1, o_d2]


def e_l0_tern_write_phase(mk, W, scr_up, scr_dn):
    d_up, d_dn = _tern_dcols(mk)
    st1, st2 = e_tern_write2(mk, W, d_up, d_dn, scr_up, scr_dn,
                             mk.scr_ready["l0delta"])
    mk.scr_ready["l0up"] = st1[-1:]
    mk.scr_ready["l0dn"] = st2[-1:]
    return st1[-1:] + st2[-1:]


def e_l0_ffn(mk, W, deps, scr_up, scr_dn):
    p = mk.p
    ws = mk.ws
    XR = xr_tiles(mk)
    xnorm_ops = list(deps)
    xnorm = reg_tiles(mk, R0_C)
    hbuf = [mk.bf(R1_C + ft * 512, 512) for ft in range(32)]
    hnrm = [mk.bf(R2_C + ft * 512, 512) for ft in range(32)]
    tails = xnorm_ops
    out_tails = []
    s2rep = mk.f32(SM_C + 4096, 512)
    sqb = mk.f32(SM_C + 5120, 512)
    for tb in range(NCH):
        def ev_h(q, bank, mt, nch):
            return q.activation(hbuf[mt][:], bank[:, 0:512], AF.Gelu)
        grid, evs = e_linear_fm(mk, ws, scr_up, D, 4096, xnorm, ev_h,
                                tails, [mk.ps[0], mk.ps[1], mk.ps[4],
                                        mk.ps[5]], mcols=512,
                                tag="l0h", nch_range=[tb], src_bf16=True,
                                w_deps=mk.scr_ready.get("l0up", ()))
        sq_ops = []
        psS = mk.ps[6]
        for ft in range(32):
            o_sq = p.add("dve", lambda q, ft=ft: q.tensor_tensor(
                sqb[:], hbuf[ft][:], hbuf[ft][:], op=ALU.mult),
                [grid[ft][tb], sq_ops[-1] if sq_ops else None], tag="l0fsq")
            o_mm = p.add("pe", lambda q, ft=ft, first=(ft == 0),
                         last=(ft == 31): q.matmul(
                psS[:, 0:512], mk.ones_f[:], sqb[:], start=first,
                stop=last), [o_sq], tag="l0fsm")
            sq_ops.append(o_mm)

        def s2f(q):
            q.tensor_scalar(s2rep[:], psS[:, 0:512], 1.0 / 4096.0, None,
                            op0=ALU.mult)
            return s2rep
        o_s2a = p.add("dve", lambda q: q.tensor_scalar(
            s2rep[:], psS[:, 0:512], 1.0 / 4096.0, None, op0=ALU.mult),
            [sq_ops[-1]], tag="l0s2a")
        o_s2b = p.add("act", lambda q: q.activation(
            s2rep[:], s2rep[:], AF.Sqrt), [o_s2a], tag="l0s2b")

        def s2c(q):
            q.tensor_scalar(s2rep[:], s2rep[:], 1e-8, None, op0=ALU.add)
            return q.reciprocal(s2rep[:], s2rep[:])
        o_s2c = p.add("dve", s2c, [o_s2b], tag="l0s2c")
        hn_ops = []
        sts_dep = list(mk.scr_ready.get("l0up", ())) + \
            list(mk.scr_ready.get("l0dn", ()))
        for ft in range(32):
            hn_ops.append(p.add("dve", lambda q, ft=ft: q.tensor_tensor(
                hnrm[ft][:], hbuf[ft][:], s2rep[:], op=ALU.mult),
                [o_s2c] + sts_dep, tag="l0hn"))
        down_evs = []
        lastb = {id(mk.ps[2]): None, id(mk.ps[3]): None}
        for mt in range(DT):
            dstsrc = scr_dn[mt]
            subs = ws._alloc(4096)
            c0 = WS_C + subs[0] * 2048
            dst = mk.bf(c0, 4096)
            o_ld = p.dma("sp", dst, dstsrc,
                         deps=list(mk.scr_ready.get("l0dn", ())) +
                         [ws.free_after[i] for i in subs],
                         sem="WS%d" % subs[0], tag="l0dld")
            bank = mk.ps[2 + mt % 2]
            def mms(q, bank=bank, dst=dst):
                ii = None
                for kk in range(32):
                    ii = q.matmul(bank[:, 0:512],
                                  dst[:, kk * 128:(kk + 1) * 128],
                                  hnrm[kk][:], start=(kk == 0),
                                  stop=(kk == 31))
                return ii
            o_mm = p.add("pe", mms, [o_ld, lastb[id(bank)]] + hn_ops[-1:],
                         tag="l0dmm")
            def evd(q, bank=bank, mt=mt, tb=tb):
                return q.tensor_tensor(
                    XR[mt][:, tb * 512:(tb + 1) * 512],
                    XR[mt][:, tb * 512:(tb + 1) * 512],
                    bank[:, 0:512], op=ALU.add)
            o_ev = p.add("dve", evd, [o_mm], tag="l0dev")
            lastb[id(bank)] = o_ev
            ws.release(subs, o_mm)
            down_evs.append(o_ev)
        tails = down_evs[-2:]
        out_tails = down_evs
    return out_tails


def e_layer0(mk, W, deps, stage="all"):
    p = mk.p
    XR = xr_tiles(mk)
    scr_xr = mk.scr["xr"]
    ln_ops = e_layernorm(mk, deps)
    if stage in ("lat", "q", "k", "v0", "v1", "qkv", "topk"):
        sub = e_l0_attn(mk, W, ln_ops[-8:], scr_xr, until=stage)
        return [p.dma("sp", XR[dt], scr_xr[dt * 128:(dt + 1) * 128, :],
                      deps=sub, tag="l0rest") for dt in range(DT)]
    ctx_evs = e_l0_attn(mk, W, ln_ops[-8:], scr_xr)
    if stage == "attn":
        rs = [p.dma("sp", XR[dt], scr_xr[dt * 128:(dt + 1) * 128, :],
                    deps=ctx_evs, tag="l0rest") for dt in range(DT)]
        return e_l0_outproj(mk, W, rs + ctx_evs)
    rs = [p.dma("sp", XR[dt], scr_xr[dt * 128:(dt + 1) * 128, :],
                deps=ctx_evs, tag="l0rest") for dt in range(DT)]
    xnorm_ops = e_ln_l0ffn(mk, rs)
    if not mk.tern_built:
        e_l0_tern_delta_phase(mk, W, xnorm_ops[-2:] + ctx_evs[-1:])
    op_evs = e_l0_outproj(mk, W, xnorm_ops[-8:] + ctx_evs)
    if not mk.tern_built:
        e_l0_tern_write_phase(mk, W, mk.scr["l0up"], mk.scr["l0dn"])
        mk.tern_built = True
    return e_l0_ffn(mk, W, op_evs[-4:] + xnorm_ops[-8:],
                    mk.scr["l0up"], mk.scr["l0dn"])



def e_l1_attn(mk, W, deps, scr_xr):
    p = mk.p
    XR = xr_tiles(mk)
    emb = reg_tiles(mk, R0_C)
    ws = mk.ws
    sp_ops = [p.dma("sp", scr_xr[dt * 128:(dt + 1) * 128, :], XR[dt],
                    deps=deps, tag="l1spill") for dt in range(DT)]
    acc = [mk.f32(XR_C + mt * 4096, 2048) for mt in range(DT)]
    z_ops = [p.add("dve", lambda q, m=mt: q.memset(acc[m][:], 0.0),
                   sp_ops, tag="l1accz") for mt in range(DT)]

    Dx = reg_tiles(mk, R1_C)

    def evd(q, bank, mt, nch):
        return q.activation(Dx[mt][:, nch * 512:(nch + 1) * 512],
                            bank[:, 0:512], AF.Identity,
                            bias=mk.bcol("l1a_db", mt))
    grid_d, evs_d = e_linear_fm(mk, ws, W["l1a_d_t"], D, D, emb, evd,
                                deps, [mk.ps[0], mk.ps[1], mk.ps[2], mk.ps[3]], tag="l1dx")

    hs = reg_tiles(mk, R2_C)
    a_t = mk.bf(SM_C, 2048)
    b_t = mk.bf(SM_C + 2048, 2048)
    out_c = [mk.bf(SM_C + mt * 512, 512) for mt in range(DT)]
    sqtmp = mk.f32(SM_C + 4096, 512)
    mrep = mk.f32(SM_C + 5120, 512)
    gcol = mk.bcol("l1a_scr", 0, 8)
    ngcol = mk.bcol("l1a_scr", 8, 8)
    wbacc = mk.bcol("l1a_scr", 16, 8)
    o_wbz = p.add("dve", lambda q: q.memset(wbacc[:], 0.0), deps,
                  tag="l1wbz")

    hw_off = mk.bias_off["l1a_headw"]
    last_scan = None
    last_accu = [None] * DT
    last_stats = None
    prev_cslabs = None
    for h in range(8):
        def gfn(q, h=h):
            q.tensor_scalar(gcol[:], mk.bcol("l1a_hng", h * 8, 8),
                            mk.bias[:, hw_off + h:hw_off + h + 1], None,
                            op0=ALU.mult)
            mk.dve_gap(q)
            q.tensor_scalar(ngcol[:], gcol[:], -1.0, None, op0=ALU.mult)
            return q.scalar_tensor_tensor(
                wbacc[:], mk.bcol("l1a_hnb", h * 8, 8),
                mk.bias[:, hw_off + h:hw_off + h + 1], wbacc[:],
                op0=ALU.mult, op1=ALU.add)
        o_g = p.add("dve", gfn, [o_wbz, last_stats], tag="l1g")

        scan_ops = []
        for dt in range(DT):
            o_la, sla, slota = ws.load(W["l1a_wa_t"][h], dt * 128, 128, D,
                                       deps=())
            o_lb, slb, slotb = ws.load(W["l1a_wb_t"][h], dt * 128, 128, D,
                                       deps=())
            ab_evs = []
            for which, (o_l, sl) in (("a", (o_la, sla)), ("b", (o_lb, slb))):
                jobs = []
                for nch in range(NCH):
                    def mms(q, bank, sl=sl, n=nch):
                        ii = None
                        for kk in range(DT):
                            ii = q.matmul(bank[:, 0:512],
                                          sl[:, kk * 128:(kk + 1) * 128],
                                          emb[kk][:, n * 512:(n + 1) * 512],
                                          start=(kk == 0), stop=(kk == DT - 1))
                        return ii
                    if which == "a":
                        def ev(q, bank, n=nch, h=h, dt=dt):
                            return q.activation(
                                a_t[:, n * 512:(n + 1) * 512], bank[:, 0:512],
                                AF.Tanh,
                                bias=mk.bcol("l1a_wab", h * 8 + dt))
                    else:
                        def ev(q, bank, n=nch, h=h, dt=dt):
                            return q.activation(
                                b_t[:, n * 512:(n + 1) * 512], bank[:, 0:512],
                                AF.Identity,
                                bias=mk.bcol("l1a_wbb", h * 8 + dt))
                    jobs.append(dict(mms=mms, deps=[o_l],
                                     evac_deps=[last_scan, last_stats],
                                     evac=ev, evac_q="act"))
                ab_evs += mm_jobs(mk, jobs,
                                  [mk.ps[0], mk.ps[1], mk.ps[2], mk.ps[3]],
                                  tag="l1ab")
            o_scan = p.add("dve", lambda q, dt=dt: q.tensor_tensor_scan(
                hs[dt][:], a_t[:], b_t[:], 0.0, op0=ALU.mult, op1=ALU.add),
                ab_evs, tag="l1scan")
            scan_ops.append(o_scan)
            last_scan = o_scan
            ws.release(slota, o_scan)
            ws.release(slotb, o_scan)

        o_c1, slc1, slotc1 = ws.load(W["l1a_c_t"][h], 0, 512, D,
                                     deps=())
        o_c2, slc2, slotc2 = ws.load(W["l1a_c_t"][h], 512, 512, D,
                                     deps=())
        cslabs = [slc1, slc2]
        head_tail = None
        for tc in range(NCH):
            evs_c = []
            jobs = []
            for mt in range(DT):
                def mms(q, bank, mt=mt, tc=tc):
                    sl = cslabs[mt // 4]
                    mi = mt % 4
                    ii = None
                    for kk in range(DT):
                        ii = q.matmul(bank[:, 0:512],
                                      sl[:, kk * 512 + mi * 128:
                                         kk * 512 + mi * 128 + 128],
                                      hs[kk][:, tc * 512:(tc + 1) * 512],
                                      start=(kk == 0), stop=(kk == DT - 1))
                    return ii

                def ev(q, bank, mt=mt, tc=tc, h=h):
                    return q.scalar_tensor_tensor(
                        out_c[mt][:], bank[:, 0:512],
                        mk.bcol("l1a_cb", h * 8 + mt),
                        Dx[mt][:, tc * 512:(tc + 1) * 512],
                        op0=ALU.add, op1=ALU.mult)
                jobs.append(dict(mms=mms,
                                 deps=[o_c1, o_c2, o_g, scan_ops[-1]],
                                 evac_deps=[last_accu[mt]],
                                 evac=ev, evac_q="dve"))
            evs_c = mm_jobs(mk, jobs,
                            [mk.ps[0], mk.ps[1], mk.ps[2], mk.ps[3]],
                            tag="l1c")
            st_mm = []
            prev_acc = last_accu[DT - 1]
            for mt in range(DT):
                o_sq = p.add("act", lambda q, m=mt: q.activation(
                    sqtmp[:], out_c[m][:], AF.Square),
                    [evs_c[mt], st_mm[-1] if st_mm else last_stats,
                     prev_acc],
                    tag="l1sq")
                def smm(q, m=mt, first=(mt == 0), lastf=(mt == DT - 1)):
                    q.matmul(mk.ps[6][:, 0:512], mk.ones_bf[:], out_c[m][:],
                             start=first, stop=lastf)
                    return q.matmul(mk.ps[7][:, 0:512], mk.ones_f[:],
                                    sqtmp[:], start=first, stop=lastf)
                o_mm = p.add("pe", smm, [o_sq, evs_c[mt]], tag="l1smm")
                st_mm.append(o_mm)

            def stf(q):
                q.tensor_scalar(mrep[:], mk.ps[6][:, 0:512], 1.0 / D, None,
                                op0=ALU.mult)
                q.tensor_tensor(sqtmp[:], mrep[:], mrep[:], op=ALU.mult)
                q.scalar_tensor_tensor(sqtmp[:], mk.ps[7][:, 0:512], 1.0 / D,
                                       sqtmp[:], op0=ALU.mult,
                                       op1=ALU.subtract)
                q.tensor_scalar(sqtmp[:], sqtmp[:], 1e-5, None, op0=ALU.add)
                return q.reciprocal(sqtmp[:], sqtmp[:])
            o_st = p.add("dve", stf, [st_mm[-1], prev_acc], tag="l1st")
            o_rs = p.add("act", lambda q: q.activation(
                sqtmp[:], sqtmp[:], AF.Sqrt), [o_st], tag="l1rs")
            o_mr = p.add("dve", lambda q: q.tensor_tensor(
                mrep[:], mrep[:], sqtmp[:], op=ALU.mult), [o_rs],
                tag="l1mr")
            for mt in range(DT):
                def accf(q, m=mt, tc=tc):
                    q.tensor_tensor(out_c[m][:], out_c[m][:], sqtmp[:],
                                    op=ALU.mult)
                    q.scalar_tensor_tensor(
                        acc[m][:, tc * 512:(tc + 1) * 512], out_c[m][:],
                        gcol[:, m:m + 1],
                        acc[m][:, tc * 512:(tc + 1) * 512],
                        op0=ALU.mult, op1=ALU.add)
                    return q.scalar_tensor_tensor(
                        acc[m][:, tc * 512:(tc + 1) * 512], mrep[:],
                        ngcol[:, m:m + 1],
                        acc[m][:, tc * 512:(tc + 1) * 512],
                        op0=ALU.mult, op1=ALU.add)
                last_accu[mt] = p.add("dve", accf,
                                      [o_mr, z_ops[mt], evs_c[mt]],
                                      tag="l1acc")
            last_stats = last_accu[DT - 1]
            head_tail = last_accu[DT - 1]
        ws.release(slotc1, head_tail)
        ws.release(slotc2, head_tail)
        prev_cslabs = head_tail

    acc_tail = [o for o in last_accu]
    sh = reg_tiles(mk, R2_C)
    sh_ops = []
    for mt in range(DT):
        def shf(q, m=mt):
            q.tensor_scalar(acc[m][:], acc[m][:], 1.0, wbacc[:, m:m + 1],
                            op0=ALU.mult, op1=ALU.add)
            q.tensor_scalar(acc[m][:], acc[m][:], 0.125, None, op0=ALU.mult)
            for nch in range(NCH):
                ii = q.scalar_tensor_tensor(
                    sh[m][:, nch * 512:(nch + 1) * 512],
                    Dx[m][:, nch * 512:(nch + 1) * 512], 0.125,
                    acc[m][:, nch * 512:(nch + 1) * 512],
                    op0=ALU.mult, op1=ALU.add)
            return ii
        sh_ops.append(p.add("dve", shf, acc_tail + [last_scan], tag="l1sh"))

    zt = acc

    def evz(q, bank, mt, nch):
        return q.activation(zt[mt][:, nch * 512:(nch + 1) * 512],
                            bank[:, 0:512], AF.Identity,
                            bias=mk.bcol("l1a_woutb", mt))
    grid_z, evs_z = e_linear_fm(mk, ws, W["l1a_wout_t"], D, D, sh, evz,
                                sh_ops, [mk.ps[0], mk.ps[1], mk.ps[2], mk.ps[3]], tag="l1z")

    ubf = Dx
    u_ops = []
    prevst = None
    for nch in range(NCH):
        st_mm = []
        for mt in range(DT):
            o_sq = p.add("act", lambda q, m=mt, n=nch: q.activation(
                sqtmp[:], zt[m][:, n * 512:(n + 1) * 512], AF.Square),
                [grid_z[mt][nch], st_mm[-1] if st_mm else prevst],
                tag="l1zsq")
            def smm(q, m=mt, n=nch, first=(mt == 0), lastf=(mt == DT - 1)):
                q.matmul(mk.ps[6][:, 0:512], mk.ones_f[:],
                         zt[m][:, n * 512:(n + 1) * 512],
                         start=first, stop=lastf)
                return q.matmul(mk.ps[7][:, 0:512], mk.ones_f[:], sqtmp[:],
                                start=first, stop=lastf)
            o_mm = p.add("pe", smm, [o_sq, grid_z[mt][nch]], tag="l1zmm")
            st_mm.append(o_mm)

        def stf(q):
            q.tensor_scalar(mrep[:], mk.ps[6][:, 0:512], 1.0 / D, None,
                            op0=ALU.mult)
            q.tensor_tensor(sqtmp[:], mrep[:], mrep[:], op=ALU.mult)
            q.scalar_tensor_tensor(sqtmp[:], mk.ps[7][:, 0:512], 1.0 / D,
                                   sqtmp[:], op0=ALU.mult, op1=ALU.subtract)
            q.tensor_scalar(sqtmp[:], sqtmp[:], 1e-5, None, op0=ALU.add)
            return q.reciprocal(sqtmp[:], sqtmp[:])
        o_st = p.add("dve", stf, [st_mm[-1]], tag="l1zst")
        o_rs = p.add("act", lambda q: q.activation(
            sqtmp[:], sqtmp[:], AF.Sqrt), [o_st], tag="l1zrs")
        tmp2 = mk.bf(SM_C, 1024).bitcast(F32)
        for mt in range(DT):
            def uf(q, m=mt, n=nch):
                q.tensor_tensor(tmp2[:], zt[m][:, n * 512:(n + 1) * 512],
                                mrep[:], op=ALU.subtract)
                q.tensor_tensor(tmp2[:], tmp2[:], sqtmp[:], op=ALU.mult)
                q.scalar_tensor_tensor(
                    zt[m][:, n * 512:(n + 1) * 512], tmp2[:],
                    mk.bcol("l1a_ng", m),
                    zt[m][:, n * 512:(n + 1) * 512],
                    op0=ALU.mult, op1=ALU.add)
                return q.tensor_scalar(
                    ubf[m][:, n * 512:(n + 1) * 512],
                    zt[m][:, n * 512:(n + 1) * 512], 1.0,
                    mk.bcol("l1a_nb", m), op0=ALU.mult, op1=ALU.add)
            u_ops.append(p.add("dve", uf, [o_rs], tag="l1u"))
        prevst = u_ops[-1]

    hbuf = [mk.bf(R2_C + ft * 512, 512) for ft in range(32)]
    hout_ops = []
    tails = u_ops[-8:]
    for tb in range(NCH):
        def ev_h(q, bank, mt, nch):
            return q.activation(hbuf[mt][:], bank[:, 0:512], AF.Gelu,
                                bias=mk.bcol("l1a_f1b", mt))
        grid_h, evs_h = e_linear_fm(mk, ws, W["l1a_ffn1_t"], D, 4096, ubf,
                                    ev_h, tails, [mk.ps[0], mk.ps[1],
                                                  mk.ps[4]],
                                    tag="l1f1", nch_range=[tb])
        tmp3 = mk.f32(SM_C + 4096, 512)
        evs2 = []
        f2state = {}
        for s0 in range(8):
            o_ld, slab, slot = ws.load(W["l1a_ffn2_t"], s0 * 128, 128, 4096,
                                       deps=())

            def mms(q, bank, slab=slab):
                ii = None
                for kk in range(32):
                    ii = q.matmul(bank[:, 0:512],
                                  slab[:, kk * 128:(kk + 1) * 128],
                                  hbuf[kk][:], start=(kk == 0),
                                  stop=(kk == 31))
                return ii

            def ev2(q, bank, mt=s0, tb=tb):
                q.tensor_scalar(tmp3[:], bank[:, 0:512],
                                mk.bcol("l1a_f2b", mt),
                                mk.bcol("l1a_nb", mt), op0=ALU.add,
                                op1=ALU.add)
                return q.tensor_tensor(
                    ubf[mt][:, tb * 512:(tb + 1) * 512], tmp3[:],
                    zt[mt][:, tb * 512:(tb + 1) * 512], op=ALU.add)
            sevs = mm_jobs(mk, [dict(mms=mms, deps=[o_ld] + list(evs_h[-2:]),
                                     evac=ev2, evac_q="dve")],
                           [mk.ps[2], mk.ps[3], mk.ps[5]], tag="l1f2",
                           state=f2state)
            ws.release(slot, sevs[-1])
            evs2 += sevs
        hout_ops += evs2
        tails = evs2[-2:]

    rs_ops = [p.dma("sp", XR[dt], scr_xr[dt * 128:(dt + 1) * 128, :],
                    deps=hout_ops, tag="l1rest") for dt in range(DT)]
    tmp4 = mk.f32(SM_C + 4096, 512)

    def evp(q, bank, mt, nch):
        q.tensor_scalar(tmp4[:], bank[:, 0:512], mk.bcol("l1a_pb", mt),
                        None, op0=ALU.add)
        return q.tensor_tensor(XR[mt][:, nch * 512:(nch + 1) * 512],
                               XR[mt][:, nch * 512:(nch + 1) * 512],
                               tmp4[:], op=ALU.add)
    grid_p, evs_p = e_linear_fm(mk, ws, W["l1a_proj_t"], D, D, ubf, evp,
                                rs_ops + hout_ops, [mk.ps[0], mk.ps[1], mk.ps[2], mk.ps[3]],
                                evac_q="dve", tag="l1p")
    return evs_p



def e_linear_fm2(mk, ws, wt_ap, K, M, rhs, evac, deps, banks, mcols=512,
                 evac_q="act", tag="lin", nch_range=None, mt_range=None,
                 src_bf16=False, rhs_coloff=0, w_deps=()):
    kt = K // 128
    nmt = M // 128
    mps = mcols // 128
    if nch_range is None:
        nch_range = range(NCH)
    evs = []
    state = {}
    for s0 in range(0, nmt, mps):
        if mt_range is not None and not any(
                s0 <= mt < s0 + mps for mt in mt_range):
            continue
        o_ld, slab, slot = ws.load(wt_ap, s0 * 128, mcols, K, deps=w_deps,
                                   src_bf16=src_bf16)
        jobs = []
        for mi in range(mps):
            mt = s0 + mi
            if mt >= nmt or (mt_range is not None and mt not in mt_range):
                continue
            for nch in nch_range:
                def mms(q, bank, slab=slab, mi=mi, n=nch):
                    ii = None
                    for kk in range(kt):
                        ii = q.matmul(
                            bank[:, 0:512],
                            slab[:, kk * mcols + mi * 128:
                                 kk * mcols + mi * 128 + 128],
                            rhs[kk][:, n * 512 - rhs_coloff:
                                    (n + 1) * 512 - rhs_coloff],
                            start=(kk == 0), stop=(kk == kt - 1))
                    return ii
                jobs.append(dict(
                    mms=mms, deps=[o_ld] + list(deps),
                    evac=(lambda q, bank, mt=mt, n=nch: evac(q, bank, mt, n)),
                    evac_q=evac_q))
        sevs = mm_jobs(mk, jobs, banks, tag=tag, state=state)
        ws.release(slot, jobs[-1]["_mm"])
        evs += sevs
    return evs


def e_l1_ffn(mk, W, deps, scr_xr, scr_emb, scr_gate):
    p = mk.p
    XR = xr_tiles(mk)
    emb = reg_tiles(mk, R0_C)
    ws = mk.ws
    H, HD = 16, 64
    sqtmp = mk.f32(SM_C + 4096, 512)
    srep = mk.f32(SM_C + 5120, 512)

    h1 = reg_tiles(mk, R1_C)
    h1_ops = []
    prevc = None
    for nch in range(NCH):
        st_mm = []
        for mt in range(DT):
            o_sq = p.add("dve", lambda q, m=mt, n=nch: q.tensor_tensor(
                sqtmp[:], emb[m][:, n * 512:(n + 1) * 512],
                emb[m][:, n * 512:(n + 1) * 512], op=ALU.mult),
                list(deps) + [st_mm[-1] if st_mm else prevc], tag="l1fsq")
            o_mm = p.add("pe", lambda q, first=(mt == 0),
                         lastf=(mt == DT - 1): q.matmul(
                mk.ps[6][:, 0:512], mk.ones_f[:], sqtmp[:],
                start=first, stop=lastf), [o_sq], tag="l1fsm")
            st_mm.append(o_mm)

        def stf(q):
            q.tensor_scalar(srep[:], mk.ps[6][:, 0:512], 1.0 / D,
                            1.1920929e-07, op0=ALU.mult, op1=ALU.add)
            return q.reciprocal(srep[:], srep[:])
        o_st = p.add("dve", stf, [st_mm[-1]], tag="l1fst")
        o_rs = p.add("act", lambda q: q.activation(
            srep[:], srep[:], AF.Sqrt), [o_st], tag="l1frs")
        for mt in range(DT):
            def nf(q, m=mt, n=nch):
                q.tensor_tensor(sqtmp[:], emb[m][:, n * 512:(n + 1) * 512],
                                srep[:], op=ALU.mult)
                return q.tensor_scalar(h1[m][:, n * 512:(n + 1) * 512],
                                       sqtmp[:], mk.bcol("l1f_n1", m), None,
                                       op0=ALU.mult)
            h1_ops.append(p.add("dve", nf, [o_rs], tag="l1fn"))
        prevc = h1_ops[-1]

    sp_ops = [p.dma("sp", scr_xr[dt * 128:(dt + 1) * 128, :], XR[dt],
                    deps=list(deps), tag="l1fspill") for dt in range(DT)]
    em_ops = [p.dma("sp", scr_emb[dt * 128:(dt + 1) * 128, :],
                    emb[dt][:], deps=h1_ops, tag="l1fembsp")
              for dt in range(DT)]

    def qk_path(wname, ncol, dst_tiles, dd):
        evs_all = []
        tmpq = mk.f32(SM_C, 512)
        sq2 = mk.f32(SM_C + 1024, 512)
        grep = mk.f32(SM_C + 2048, 512)

        kt = DT
        state = {}
        for s0 in range(0, 8, 4):
            o_ld, slab, slot = ws.load(W[wname], s0 * 128, 512, D, deps=())
            jobs = []
            for mi in range(4):
                mt = s0 + mi
                for nch in range(NCH):
                    def mms(q, bank, slab=slab, mi=mi, n=nch):
                        ii = None
                        for kk in range(DT):
                            ii = q.matmul(bank[:, 0:512],
                                          slab[:, kk * 512 + mi * 128:
                                               kk * 512 + mi * 128 + 128],
                                          h1[kk][:, n * 512:(n + 1) * 512],
                                          start=(kk == 0),
                                          stop=(kk == DT - 1))
                        return ii

                    def ev1(q, bank, mt=mt, n=nch):
                        return q.activation(tmpq[:], bank[:, 0:512], AF.Copy)
                    jobs.append(dict(mms=mms, deps=[o_ld] + list(dd),
                                     evac=ev1, evac_q="act"))
            def flushq(pe_):
                if pe_ is None:
                    return
                mt, nch, o_sq = pe_
                o_gm = p.add("pe", lambda q: q.matmul(
                    mk.ps[6][:, 0:512], mk.ones64_f[:], sq2[:],
                    start=True, stop=True),
                    [o_sq, state.get("prev_rm")], tag="l1qk_gm")

                def rmsf(q):
                    q.tensor_scalar(grep[:], mk.ps[6][:, 0:512], 1.0 / HD,
                                    1.1920929e-07, op0=ALU.mult,
                                    op1=ALU.add)
                    return q.reciprocal(grep[:], grep[:])
                o_rm = p.add("dve", rmsf, [o_gm], tag="l1qk_rm")
                state["prev_rm"] = o_rm
                o_rs = p.add("act", lambda q: q.activation(
                    grep[:], grep[:], AF.Sqrt), [o_rm], tag="l1qk_rs")

                def outf(q, mt=mt, nch=nch):
                    q.tensor_tensor(sq2[:], tmpq[:], grep[:], op=ALU.mult)
                    return q.tensor_scalar(
                        dst_tiles[mt][:, nch * 512:(nch + 1) * 512],
                        sq2[:], mk.bcol(ncol), None, op0=ALU.mult)
                o_out = p.add("dve", outf, [o_rs], tag="l1qk_out")
                evs_all.append(o_out)

            for j, job in enumerate(jobs):
                mt, nch = s0 + j // NCH, j % NCH
                b = [mk.ps[0], mk.ps[1], mk.ps[2],
                     mk.ps[3]][(state.get("j", 0)) % 4]
                o_mm = p.add("pe", (lambda q, f=job["mms"], bb=b: f(q, bb)),
                             list(job["deps"]) +
                             [mk.bank_last.get(id(b))], tag="l1qk_mm")
                flushq(state.get("pend"))
                state["pend"] = None
                o_ev = p.add("act", (lambda q, f=job["evac"], bb=b:
                                     f(q, bb)),
                             [o_mm, evs_all[-1] if evs_all else None],
                             tag="l1qk_ev")

                o_sq = p.add("act", lambda q: q.activation(
                    sq2[:], tmpq[:], AF.Square), [o_ev], tag="l1qk_sq")
                state["pend"] = (mt, nch, o_sq)
                mk.bank_last[id(b)] = o_sq
                state["j"] = state.get("j", 0) + 1
            ws.release(slot, evs_all[-1] if evs_all else None)
        flushq(state.get("pend"))
        state["pend"] = None
        return evs_all

    qf = reg_tiles(mk, R2_C)
    kf = reg_tiles(mk, R0_C)
    evs_q = qk_path("l1f_wq", "l1f_qn", qf, h1_ops[-8:])
    evs_k = qk_path("l1f_wk", "l1f_kn", kf, em_ops + evs_q[-1:])

    vext = [mk.bf(XR_C + tt * 1040, 1040) for tt in range(16)]
    ones_ops = []
    for tt in range(16):
        dst = vext[tt].rearrange("pp (h c) -> pp h c", h=16)[:, :, 64:65]
        ones_ops.append(p.add("dve", lambda q, dst=dst: q.memset(dst, 1.0),
                              sp_ops, tag="l1fvones"))
    v_evs = []
    vstate = {}
    for half in range(2):
        o_ld, slab, slot = ws.load(W["l1f_wv"], half * 512, 512, D,
                                   deps=())
        jobs = []
        for tt in range(16):
            def mms(q, bank, tt=tt, slab=slab):
                ii = None
                for kk in range(DT):
                    ii = q.matmul(bank[:, 0:512],
                                  h1[kk][:, tt * 128:(tt + 1) * 128],
                                  slab[:, kk * 512:(kk + 1) * 512],
                                  start=(kk == 0), stop=(kk == DT - 1))
                return ii
            h0 = half * 8
            dst = vext[tt].rearrange("pp (h c) -> pp h c", h=16)[
                :, h0:h0 + 8, 0:64]

            def evac(q, bank, tt=tt, dst=dst):
                return q.tensor_copy(
                    dst, bank[:, 0:512].rearrange("pp (a c) -> pp a c", a=8))
            jobs.append(dict(mms=mms,
                             deps=[o_ld] + list(evs_k[-2:]) + ones_ops,
                             evac=evac, evac_q="dve"))
        jevs = mm_jobs(mk, jobs, [mk.ps[0], mk.ps[1], mk.ps[2], mk.ps[3]],
                       tag="l1fv",
                       state=vstate)
        ws.release(slot, jevs[-1])
        v_evs += jevs

    esc = [mk.bf(XR_C + 16640 + kt_ * 512, 512) for kt_ in range(16)]
    trilb = mk.bf(XR_C + 28928, 2048)
    o_tril = p.dma("pool", trilb, W["tril4"], deps=sp_ops, tag="l1ftril")
    ctx = h1
    diagb = mk.f32(SM_C + 4096, 512)
    invreps = [mk.f32(SM_C, 512, rows=64), mk.f32(SM_C + 1024, 512, rows=64)]
    rowbufs = [mk.rows[0:1, 0:512], mk.rows2[0:1, 0:512]]
    repbanks2 = [mk.ps[6], mk.ps[7]]
    ctx_evs = []
    sbanks = [mk.ps[0], mk.ps[1], mk.ps[2], mk.ps[3]]
    lastexp = {}
    lastctx = [None, None]
    last_cmm = None
    prev_cev = None
    lastrep = [None, None]
    prev_cevs2 = [None, None]
    last_dgexp = [None]
    pend = [None]

    def flush1f(pe_):
        if pe_ is None:
            return
        h, qb, cb, r0, par, o_cmm, o_inv = pe_
        invrep, rowb, repbank = invreps[par], rowbufs[par], repbanks2[par]
        o_rep = p.add("pe", lambda q, rowb=rowb, repbank=repbank:
                      q.matmul(repbank[0:64, 0:512],
                               mk.ones_row_f[0:1, 0:64],
                               rowb, start=True, stop=True),
                      [o_inv, prev_cevs2[par]], tag="l1frep")
        o_repc = p.add("act", lambda q, invrep=invrep, repbank=repbank:
                       q.activation(invrep[:], repbank[0:64, 0:512],
                                    AF.Copy), [o_rep], tag="l1frepc")
        lastrep[par] = o_repc

        def cev(q, h=h, qb=qb, cb=cb, r0=r0, invrep=invrep):
            return q.tensor_tensor(
                ctx[h // 2][r0:r0 + 64, qb * 512:(qb + 1) * 512],
                cb[0:64, 0:512], invrep[:], op=ALU.mult)
        o_cev = p.add("dve", cev, [o_repc, o_cmm], tag="l1fcev")
        lastctx[qb % 2] = o_cev
        prev_cevs2[par] = o_cev
        ctx_evs.append(o_cev)

    for h in range(H):
        r0 = (h % 2) * 64
        for qb in range(NCH):
            par = (h * NCH + qb) % 2
            rowb = rowbufs[par]
            nkt = qb * 4 + 4
            exp_ops = []
            for kt_ in range(nkt):
                b = sbanks[kt_ % 4]

                def smm(q, bank, h=h, qb=qb, kt_=kt_, r0=r0):
                    return q.matmul(
                        bank[:, 0:512],
                        kf[h // 2][r0:r0 + 64, kt_ * 128:(kt_ + 1) * 128],
                        qf[h // 2][r0:r0 + 64, qb * 512:(qb + 1) * 512],
                        start=True, stop=True, tile_position=(r0, 0))
                dd = [v_evs[-1], lastexp.get(kt_ % 4), last_cmm, evs_q[-1],
                      evs_k[-1]]
                o_smm = p.add("pe", (lambda q, f=smm, bb=b: f(q, bb)), dd,
                              tag="l1fsmm")
                if kt_ >= qb * 4:
                    o_dg = p.add(
                        "dve", lambda q, bb=b, o=kt_ - qb * 4:
                        q.tensor_tensor(diagb[:], bb[:, 0:512],
                                        trilb[:, o * 512:(o + 1) * 512],
                                        op=ALU.add),
                        [o_smm, o_tril, last_dgexp[0]], tag="l1fdiag")
                    o_exp = p.add("act", lambda q, kt_=kt_: q.activation(
                        esc[kt_][:], diagb[:], AF.Exp, scale=0.125),
                        [o_dg], tag="l1fexp")
                    last_dgexp[0] = o_exp
                else:
                    o_exp = p.add("act", (lambda q, bb=b, kt_=kt_:
                                          q.activation(esc[kt_][:],
                                                       bb[:, 0:512], AF.Exp,
                                                       scale=0.125)),
                                  [o_smm], tag="l1fexp")
                lastexp[kt_ % 4] = o_exp
                exp_ops.append(o_exp)
            flush1f(pend[0])
            pend[0] = None
            cb = mk.ps[4 + (qb % 2)]
            o_cmm = None
            for kt_ in range(nkt):
                def cmm1(q, h=h, cb=cb, kt_=kt_, nkt=nkt):
                    return q.matmul(cb[0:65, 0:512],
                                    vext[kt_][:, h * 65:h * 65 + 65],
                                    esc[kt_][:], start=(kt_ == 0),
                                    stop=(kt_ == nkt - 1),
                                    skip_group_check=True)
                dd = [exp_ops[kt_]] + ([lastctx[qb % 2], v_evs[-1]]
                                       if kt_ == 0 else [])
                o_cmm = p.add("pe", cmm1, dd, tag="l1fcmm")
            last_cmm = o_cmm
            o_inv = p.add("dve", lambda q, cb=cb, rowb=rowb: q.reciprocal(
                rowb, cb[64:65, 0:512]),
                [o_cmm, lastrep[par]], tag="l1finv")
            pend[0] = (h, qb, cb, r0, par, o_cmm, o_inv)
    flush1f(pend[0])
    pend[0] = None

    em_ld = [p.dma("sp", kf[dt][:], scr_emb[dt * 128:(dt + 1) * 128, :],
                   deps=ctx_evs, tag="l1fembld") for dt in range(DT)]
    embr = kf
    xmid = qf

    def evwo(q, bank, mt, nch):
        return q.tensor_tensor(xmid[mt][:, nch * 512:(nch + 1) * 512],
                               bank[:, 0:512],
                               embr[mt][:, nch * 512:(nch + 1) * 512],
                               op=ALU.add)
    evs_wo = e_linear_fm2(mk, ws, W["l1f_wo"], D, D, ctx, evwo,
                          em_ld + ctx_evs,
                          [mk.ps[0], mk.ps[1], mk.ps[2], mk.ps[3]],
                          evac_q="dve", tag="l1fwo", mt_range=range(2, 8))
    cp_ops = []
    for mt in range(2):
        cp_ops.append(p.add("dve", lambda q, m=mt: q.tensor_copy(
            xmid[m][:], embr[m][:]), em_ld + ctx_evs + [evs_q[-1]],
            tag="l1fxmcp"))

    rs_ops = [p.dma("sp", XR[dt], scr_xr[dt * 128:(dt + 1) * 128, :],
                    deps=ctx_evs + evs_wo + cp_ops, tag="l1frest")
              for dt in range(DT)]
    xm_add = []
    for mt in range(DT):
        def xaf(q, m=mt):
            ii = None
            for nch in range(NCH):
                ii = q.tensor_tensor(XR[m][:, nch * 512:(nch + 1) * 512],
                                     XR[m][:, nch * 512:(nch + 1) * 512],
                                     xmid[m][:, nch * 512:(nch + 1) * 512],
                                     op=ALU.add)
            return ii
        xm_add.append(p.add("dve", xaf, rs_ops + evs_wo + cp_ops,
                            tag="l1fxma"))

    xf = kf
    xf_ops = []
    prevc = None
    for nch in range(NCH):
        st_mm = []
        for mt in range(DT):
            o_sq = p.add("dve", lambda q, m=mt, n=nch: q.tensor_tensor(
                sqtmp[:], xmid[m][:, n * 512:(n + 1) * 512],
                xmid[m][:, n * 512:(n + 1) * 512], op=ALU.mult),
                evs_wo + cp_ops + [st_mm[-1] if st_mm else prevc] + xm_add,
                tag="l1xfsq")
            o_mm = p.add("pe", lambda q, first=(mt == 0),
                         lastf=(mt == DT - 1): q.matmul(
                mk.ps[6][:, 0:512], mk.ones_f[:], sqtmp[:],
                start=first, stop=lastf), [o_sq], tag="l1xfsm")
            st_mm.append(o_mm)

        def stf(q):
            q.tensor_scalar(srep[:], mk.ps[6][:, 0:512], 1.0 / D,
                            1.1920929e-07, op0=ALU.mult, op1=ALU.add)
            return q.reciprocal(srep[:], srep[:])
        o_st = p.add("dve", stf, [st_mm[-1]], tag="l1xfst")
        o_rs = p.add("act", lambda q: q.activation(
            srep[:], srep[:], AF.Sqrt), [o_st], tag="l1xfrs")
        for mt in range(DT):
            def nf(q, m=mt, n=nch):
                q.tensor_tensor(sqtmp[:], xmid[m][:, n * 512:(n + 1) * 512],
                                srep[:], op=ALU.mult)
                return q.tensor_scalar(xf[m][:, n * 512:(n + 1) * 512],
                                       sqtmp[:], mk.bcol("l1f_n2", m), None,
                                       op0=ALU.mult)
            xf_ops.append(p.add("dve", nf, [o_rs], tag="l1xfn"))
        prevc = xf_ops[-1]

    gsl = mk.bf(SM_C, 32)
    o_gld = p.dma("pool", gsl, W["l1f_gate_kt"], deps=xf_ops[-8:],
                  tag="l1gld")
    scro = mk.bias_off["l1f_scr"]
    probs = mk.bias[:, scro:scro + 64]
    sums = mk.bias[:, scro + 64:scro + 80]
    cnt = mk.bias[:, scro + 80:scro + 96]
    g_mm = []
    for tt in range(16):
        def gmm(q, tt=tt):
            ii = None
            for kk in range(DT):
                ii = q.matmul(mk.ps[7][:, tt * 4:tt * 4 + 4],
                              xf[kk][:, tt * 128:(tt + 1) * 128],
                              gsl[:, kk * 4:(kk + 1) * 4],
                              start=(kk == 0), stop=(kk == DT - 1))
            return ii
        g_mm.append(p.add("pe", gmm,
                          [o_gld, xf_ops[-1]] + lastrep, tag="l1gmm"))
    o_gexp = p.add("act", lambda q: q.activation(
        probs[:], mk.ps[7][:, 0:64], AF.Exp), g_mm, tag="l1gexp")

    def topf(q):
        pv = probs[:].rearrange("p (tt e) -> p tt e", e=4)
        for e in range(4):
            first = True
            for e2 in range(4):
                if e2 == e:
                    continue
                op = ALU.is_ge if e2 < e else ALU.is_gt
                if first:
                    q.tensor_tensor(cnt[:], pv[:, :, e2:e2 + 1],
                                    pv[:, :, e:e + 1], op=op)
                    first = False
                else:
                    q.tensor_tensor(sums[:], pv[:, :, e2:e2 + 1],
                                    pv[:, :, e:e + 1], op=op)
                    mk.dve_gap(q)
                    q.tensor_tensor(cnt[:], cnt[:], sums[:], op=ALU.add)
                mk.dve_gap(q)
            q.tensor_scalar(cnt[:], cnt[:], 2.0, None, op0=ALU.is_lt)
            mk.dve_gap(q)
            q.tensor_tensor(pv[:, :, e:e + 1], pv[:, :, e:e + 1], cnt[:],
                            op=ALU.mult)
            mk.dve_gap(q)
        q.tensor_tensor(sums[:], pv[:, :, 0:1], pv[:, :, 1:2], op=ALU.add)
        mk.dve_gap(q)
        q.tensor_tensor(cnt[:], pv[:, :, 2:3], pv[:, :, 3:4], op=ALU.add)
        mk.dve_gap(q)
        q.tensor_tensor(sums[:], sums[:], cnt[:], op=ALU.add)
        mk.dve_gap(q)
        q.tensor_scalar(sums[:], sums[:], 1e-8, None, op0=ALU.add)
        mk.dve_gap(q)
        q.reciprocal(sums[:], sums[:])
        mk.dve_gap(q)
        for e in range(4):
            q.tensor_tensor(pv[:, :, e:e + 1], pv[:, :, e:e + 1], sums[:],
                            op=ALU.mult)
            mk.dve_gap(q)
        return q.tensor_copy(mk.wtsbf[:], probs[:])
    o_top = p.add("dve", topf, [o_gexp], tag="l1top")
    o_wtp = p.add("pe", lambda q: q.transpose(
        mk.ps_bf[6][0:64, 0:128], mk.wtsbf[:], mk.ident[:]), [o_top],
        tag="l1wtp")
    o_wtc = p.add("act", lambda q: q.activation(
        mk.wtt[:], mk.ps_bf[6][0:64, 0:128], AF.Copy), [o_wtp], tag="l1wtc")
    o_wst = p.dma("sp", scr_gate, mk.wtt[:], deps=[o_wtc], tag="l1wst")

    hA = [mk.bf(R2_C + ft * 1024, 1024) for ft in range(16)]
    hB = [mk.bf(R1_C + ft * 1024, 1024) for ft in range(16)]
    wrep = mk.bf(SM_C, 2048)
    tmpo = mk.f32(SM_C + 4096, 512)
    moe_tails = list([o_wst]) + xm_add
    for e in range(4):
        src_ap = scr_gate.rearrange("(tt e) p -> e tt p", e=4)[e]
        o_wr = p.dma("sp", wrep[:].rearrange("pp (tt c) -> pp tt c", tt=16),
                     src_ap.partition_broadcast(128),
                     deps=moe_tails, tag="l1wrep")
        wr_dep = [o_wr]

        def down_ev(q, bank, mt, nch):
            q.tensor_tensor(tmpo[:], bank[:, 0:512],
                            wrep[:, nch * 512:(nch + 1) * 512], op=ALU.mult)
            return q.tensor_tensor(XR[mt][:, nch * 512:(nch + 1) * 512],
                                   XR[mt][:, nch * 512:(nch + 1) * 512],
                                   tmpo[:], op=ALU.add)
        for tb in range(2):
            nr = [2 * tb, 2 * tb + 1]
            co = tb * 1024
            if e == 0:
                def evh(q, bank, mt, nch, co=co):
                    return q.activation(
                        hA[mt][:, nch * 512 - co:(nch + 1) * 512 - co],
                        bank[:, 0:512], AF.Gelu)
                eh = e_linear_fm2(mk, ws, W["l1f_e0_up"], D, 2048, xf, evh,
                                  moe_tails, [mk.ps[0], mk.ps[1], mk.ps[4]],
                                  tag="e0u", nch_range=nr)
                ed = e_linear_fm2(mk, ws, W["l1f_e0_down"], 2048, D, hA,
                                  down_ev, eh[-2:] + wr_dep,
                                  [mk.ps[2], mk.ps[3], mk.ps[5]],
                                  evac_q="dve",
                                  tag="e0d", nch_range=nr,
                                  mt_range=range(2, 8), mcols=256,
                                  rhs_coloff=co)
            elif e == 1:
                def evh1(q, bank, mt, nch, co=co):
                    return q.activation(
                        hA[mt][:, nch * 512 - co:(nch + 1) * 512 - co],
                        bank[:, 0:512], AF.Silu)
                eh = e_linear_fm2(mk, ws, W["l1f_e1_w1"], D, 2048, xf, evh1,
                                  moe_tails, [mk.ps[0], mk.ps[1], mk.ps[4]],
                                  tag="e1a", nch_range=nr)

                def evh2(q, bank, mt, nch, co=co):
                    return q.tensor_tensor(
                        hA[mt][:, nch * 512 - co:(nch + 1) * 512 - co],
                        bank[:, 0:512],
                        hA[mt][:, nch * 512 - co:(nch + 1) * 512 - co],
                        op=ALU.mult)
                eh2 = e_linear_fm2(mk, ws, W["l1f_e1_w2"], D, 2048, xf,
                                   evh2, eh[-2:], [mk.ps[0], mk.ps[1],
                                                   mk.ps[4]],
                                   evac_q="dve", tag="e1b", nch_range=nr)
                ed = e_linear_fm2(mk, ws, W["l1f_e1_down"], 2048, D, hA,
                                  down_ev, eh2[-2:] + wr_dep,
                                  [mk.ps[2], mk.ps[3], mk.ps[5]],
                                  evac_q="dve",
                                  tag="e1d", nch_range=nr,
                                  mt_range=range(2, 8), mcols=256,
                                  rhs_coloff=co)
            elif e == 2:
                def evg(dst, co=co):
                    def f(q, bank, mt, nch):
                        return q.activation(
                            dst[mt][:, nch * 512 - co:(nch + 1) * 512 - co],
                            bank[:, 0:512], AF.Gelu)
                    return f
                eh = e_linear_fm2(mk, ws, W["l1f_e2_l1"], D, 2048, xf,
                                  evg(hA), moe_tails, [mk.ps[0], mk.ps[1],
                                                       mk.ps[4]],
                                  tag="e2a", nch_range=nr)
                eh2 = e_linear_fm2(mk, ws, W["l1f_e2_l2"], 2048, 2048, hA,
                                   evg(hB), eh[-2:], [mk.ps[0], mk.ps[1],
                                                      mk.ps[4]],
                                   tag="e2b", nch_range=nr, mcols=256,
                                   rhs_coloff=co)
                eh3 = e_linear_fm2(mk, ws, W["l1f_e2_l3"], 2048, 2048, hB,
                                   evg(hA), eh2[-2:], [mk.ps[0], mk.ps[1],
                                                       mk.ps[4]],
                                   tag="e2c", nch_range=nr, mcols=256,
                                   rhs_coloff=co)
                ed = e_linear_fm2(mk, ws, W["l1f_e2_l4"], 2048, D, hA,
                                  down_ev, eh3[-2:] + wr_dep,
                                  [mk.ps[2], mk.ps[3], mk.ps[5]],
                                  evac_q="dve",
                                  tag="e2d", nch_range=nr,
                                  mt_range=range(2, 8), mcols=256,
                                  rhs_coloff=co)
            else:
                def evg3(dst, co=co):
                    def f(q, bank, mt, nch):
                        return q.activation(
                            dst[mt][:, nch * 512 - co:(nch + 1) * 512 - co],
                            bank[:, 0:512], AF.Gelu)
                    return f
                hD = hA[:8]
                eh = e_linear_fm2(mk, ws, W["l1f_e3_down"], D, D, xf,
                                  evg3(hD), moe_tails, [mk.ps[0], mk.ps[1],
                                                        mk.ps[4]],
                                  tag="e3a", nch_range=nr)
                eh2 = e_linear_fm2(mk, ws, W["l1f_e3_up"], D, 2048, hD,
                                   evg3(hB), eh[-2:], [mk.ps[0], mk.ps[1],
                                                       mk.ps[4]],
                                   tag="e3b", nch_range=nr, rhs_coloff=co)
                ed = e_linear_fm2(mk, ws, W["l1f_e3_out"], 2048, D, hB,
                                  down_ev, eh2[-2:] + wr_dep,
                                  [mk.ps[2], mk.ps[3], mk.ps[5]],
                                  evac_q="dve",
                                  tag="e3d", nch_range=nr,
                                  mt_range=range(2, 8), mcols=256,
                                  rhs_coloff=co)
            moe_tails = ed
    return moe_tails


def e_layer1(mk, W, deps, stage="all"):
    p = mk.p
    scr_xr = mk.scr["xr"]
    ln_ops = e_layernorm(mk, deps)
    at = e_l1_attn(mk, W, ln_ops[-8:], scr_xr)
    if stage == "attn":
        return at
    return e_l1_ffn(mk, W, at, scr_xr, mk.scr["emb"], mk.scr["gate"])




LAYER_EMIT = {0: e_layer0, 1: e_layer1, 2: e_layer2, 3: e_layer3}


def build_program(shapes, bias_off, layers, nbatch=2, stage="all",
                  dump=False):
    nc = bass.Bass("TRN2", target_bir_lowering=False)
    xin = nc.dram_tensor("xin", [nbatch, D, L], F32, kind="ExternalInput").ap()
    xout = nc.dram_tensor("xout", [nbatch, D, L], F32,
                          kind="ExternalOutput").ap()
    dbg = {}
    if dump:
        for nm in ["r0", "r1", "r2"]:
            dbg[nm] = nc.dram_tensor("dbg_" + nm, [128, 16384], BF16,
                                     kind="ExternalOutput").ap()
        dbg["bias"] = nc.dram_tensor("dbg_bias", [128, 260], F32,
                                     kind="ExternalOutput").ap()
        dbg["rows"] = nc.dram_tensor("dbg_rows", [1, 512], F32,
                                     kind="ExternalOutput").ap()
    W = {}
    for name, shp in shapes.items():
        W[name] = nc.dram_tensor(name, list(shp), F32,
                                 kind="ExternalInput").ap()
    ctx = contextlib.ExitStack()
    with ctx:
        p = Prog(nc, ctx)
        mk = make_mk(nc, ctx, p, bias_off,
                     bias_w=shapes["biasbuf"][1] + 40)
        if 3 in layers:
            mk.scr["w2"] = nc.dram_tensor("scr_w2", [D, 4096], BF16,
                                          kind="Internal").ap()
            mk.scr["w3"] = nc.dram_tensor("scr_w3", [4096, D], BF16,
                                          kind="Internal").ap()
        if 3 in layers or 0 in layers or 1 in layers:
            mk.scr["xr"] = nc.dram_tensor("scr_xr", [D, L], F32,
                                          kind="Internal").ap()
        if 1 in layers:
            mk.scr["emb"] = nc.dram_tensor("scr_emb", [D, L], BF16,
                                           kind="Internal").ap()
            mk.scr["gate"] = nc.dram_tensor("scr_gate", [64, 128], BF16,
                                            kind="Internal").ap()
        if 0 in layers:
            mk.scr["row"] = nc.dram_tensor("scr_row", [1, 64], F32,
                                           kind="Internal").ap()
            mk.scr["l0up"] = nc.dram_tensor("scr_l0up", [D, 4096], BF16,
                                            kind="Internal").ap()
            mk.scr["l0dn"] = nc.dram_tensor("scr_l0dn", [8, 128, 4096], BF16,
                                            kind="Internal").ap()
        ini = []
        ini.append(p.dma("pool", mk.ident[:], W["ident"], tag="ini"))
        ini.append(p.dma("sp", mk.ident_f[:], W["ident"], tag="ini"))
        ini.append(p.dma("sp", mk.ones_f[:], W["ones"], tag="ini"))
        ini.append(p.dma("pool", mk.ones_bf[:], W["ones"], tag="ini"))
        ini.append(p.dma("sp", mk.ones_row_f[:], W["ones"][0:1, :], tag="ini"))
        if "ones64" in W:
            ini.append(p.dma("sp", mk.ones64_f[:], W["ones64"], tag="ini"))
        if "l0sel" in W:
            ini.append(p.dma("sp", mk.sel[:], W["l0sel"], tag="ini"))
            ini.append(p.dma("pool", mk.selA[:],
                             W["l0sel"][:, 0:128], tag="ini"))
        if "bmasks" in W:
            ini.append(p.dma(
                "pool", mk.bmasks[:].rearrange("p (s m) -> p s m", s=8),
                W["bmasks"].rearrange("s p m -> p s m"), tag="ini"))
        ini.append(p.dma("sp", mk.bias[:, 0:W["biasbuf"].shape[1]],
                         W["biasbuf"], tag="ini"))
        for i, nm in enumerate(ROWPACK):
            ini.append(p.dma("pool", mk.brow[:, i * D:(i + 1) * D],
                             W["rowpack"][i:i + 1, :].partition_broadcast(128),
                             tag="ini"))
        store_ops = []
        for b in range(nbatch):
            XR = xr_tiles(mk)
            xl = []
            ld_deps = ini + (p.tails() if b else [])
            for dt in range(DT):
                xl.append(p.dma("sp", XR[dt],
                                xin[b, dt * 128:(dt + 1) * 128, :],
                                deps=ld_deps, tag="xload"))
            deps = xl
            for li in layers:
                deps = LAYER_EMIT[li](mk, W, deps, stage=stage)
                deps = deps[-2:] + p.tails("pe", "act", "dve", "pool", "sp")
            for dt in range(DT):
                store_ops.append(p.dma("sp",
                                       xout[b, dt * 128:(dt + 1) * 128, :],
                                       XR[dt], deps=p.tails("pe", "act", "dve"),
                                       tag="xstore"))
        if dump:
            if "w2" in mk.scr:
                dbg["w2"] = nc.dram_tensor("dbg_w2", [256, 4096], BF16,
                                           kind="ExternalOutput").ap()
                dbg["w3"] = nc.dram_tensor("dbg_w3", [256, D], BF16,
                                           kind="ExternalOutput").ap()
                store_ops.append(p.dma("sp", dbg["w2"],
                                       mk.scr["w2"][0:256, :],
                                       deps=p.tails("pe", "act", "dve"),
                                       tag="dbgstore"))
                store_ops.append(p.dma("sp", dbg["w3"],
                                       mk.scr["w3"][0:256, :],
                                       deps=p.tails("pe", "act", "dve"),
                                       tag="dbgstore"))
            for nm, c0 in (("r0", R0_C), ("r1", R1_C), ("r2", R2_C)):
                store_ops.append(p.dma("sp", dbg[nm], mk.bf(c0, 16384),
                                       deps=p.tails("pe", "act", "dve"),
                                       tag="dbgstore"))
            store_ops.append(p.dma("sp", dbg["bias"],
                                   mk.bias[:, 0:260],
                                   deps=p.tails("pe", "act", "dve"),
                                   tag="dbgstore"))
            store_ops.append(p.dma("sp", dbg["rows"], mk.rows[0:1, :],
                                   deps=p.tails("pe", "act", "dve"),
                                   tag="dbgstore"))
        p.add("sp", lambda q: q.nop(), store_ops, tag="fin")
        block = ctx.enter_context(nc.Block())
        p.finalize(block)
    return nc


def kernel(**inputs):
    W, bias_off = prep_weights(inputs)
    shapes = {k: v.shape for k, v in W.items()}
    x = np.ascontiguousarray(
        np.asarray(inputs["L0_attn_x"], np.float32).transpose(0, 2, 1))
    nc = build_program(shapes, bias_off, [0, 1, 2, 3], nbatch=2)
    in_maps = []
    for c in range(8):
        m = dict(W)
        m["xin"] = np.ascontiguousarray(x[2 * c:2 * c + 2])
        in_maps.append(m)
    res = run_bass_kernel_spmd(nc, in_maps, core_ids=list(range(8)))
    out = np.concatenate([r["xout"] for r in res.results], axis=0)
    return np.ascontiguousarray(out.transpose(0, 2, 1))
